# version 01
# baseline (speedup 1.0000x reference)
import numpy as np
import ml_dtypes
from contextlib import ExitStack

import concourse.bass as bass
import concourse.mybir as mybir
from concourse.bass_utils import run_bass_kernel_spmd

F32 = mybir.dt.float32
BF16 = mybir.dt.bfloat16
AF = mybir.ActivationFunctionType
ALU = mybir.AluOpType
AX = mybir.AxisListType

T = 4096
D = 1024
NT = T // 128
DB = D // 128
EPS = 1e-5

ENGS = ("tensor", "vector", "scalar", "sync", "gpsimd")


class _Sem:
    __slots__ = ("h", "n", "name")

    def __init__(self, h, name):
        self.h, self.n, self.name = h, 0, name


class Plan:
    def __init__(self, nc, stack):
        self.nc = nc
        self.ops = {e: [] for e in ENGS}
        self.pending = {e: [] for e in ENGS}
        self._stack = stack
        self._semi = 0

    def sem(self, name=None):
        name = name or f"s{self._semi}"
        self._semi += 1
        h = self._stack.enter_context(self.nc.semaphore(name))
        return _Sem(h, name)

    def emit(self, eng, fn, waits=(), incs=(), post_drain=False):
        w = list(self.pending[eng]) + list(waits)
        self.pending[eng] = []
        for s, amt in incs:
            s.n += amt
        self.ops[eng].append((fn, w, list(incs), post_drain))

    def gate(self, conds, engs=ENGS):
        for e in engs:
            self.pending[e].extend(conds)

    def replay(self, eng_obj, name):
        hi = {}
        for fn, waits, incs, post_drain in self.ops[name]:
            for s, v in waits:
                if hi.get(s.name, -1) < v:
                    eng_obj.wait_ge(s.h, v)
                    hi[s.name] = v
            instr = fn(eng_obj)
            if post_drain:
                instr = eng_obj.drain()
            for s, amt in incs:
                assert instr is not None, f"inc with no instr on {name}"
                instr.then_inc(s.h, amt)


def _ap(t):
    return t.ap() if hasattr(t, "ap") else t


PC_STAGE = 9
DBGN = 0

def build_nc(debug=False, phases=("p0", "pa", "pb", "pc")):
    nc = bass.Bass("TRN2", target_bir_lowering=False)
    st = ExitStack()

    di = {}
    def din(name, shape):
        di[name] = nc.dram_tensor(name, list(shape), F32, kind="ExternalInput").ap()
    din("x", (T, D))
    din("rW", (D, 15)); din("rb", (15,))
    din("sW1", (D, D)); din("sb1", (D,)); din("sW2", (D, D)); din("sb2", (D,))
    din("eW1", (15, D, D)); din("eb1", (15, D)); din("eW2", (15, D, D)); din("eb2", (15, D))
    din("aWq", (D, D)); din("aWk", (D, D)); din("aWv", (D, D)); din("aWo", (D, D))
    din("aWg", (16, D)); din("aWb", (16, D))
    din("cqw", (D, 1, 4)); din("cqb", (D,)); din("ckw", (D, 1, 4)); din("ckb", (D,))
    din("cvw", (D, 1, 4)); din("cvb", (D,))
    din("fWq", (D, D)); din("frouters", (255, D)); din("fleafv", (256, D))
    din("fWout", (D, D)); din("fWoutb", (D,))

    out = nc.dram_tensor("out", [T, D], F32, kind="ExternalOutput").ap()

    def dscr(name, shape, dtype=F32, dbg=False):
        kind = "ExternalOutput" if (debug and dbg) else "Internal"
        return nc.dram_tensor(name, list(shape), dtype, kind=kind).ap()

    xnT_hbm = dscr("xnT_hbm", (DB, 128, T), BF16, dbg=True)
    attn0_hbm = dscr("attn0_hbm", (T, D), dbg=True)
    x1_hbm = dscr("x1_hbm", (T, D), dbg=True)

    dbg_outs = {}
    if debug:
        dbg_outs["xn_nat_d"] = nc.dram_tensor("xn_nat_d", [128, NT, D], BF16, kind="ExternalOutput").ap()
        dbg_outs["logpT_d"] = nc.dram_tensor("logpT_d", [128, DB, T], BF16, kind="ExternalOutput").ap()

    ident_h = nc.inline_tensor(np.eye(128, dtype=ml_dtypes.bfloat16), name="ident128")

    plan = Plan(nc, st)

    id_sb = st.enter_context(nc.sbuf_tensor("id_sb", [128, 128], BF16))
    onecol = st.enter_context(nc.sbuf_tensor("onecol", [128, 1], BF16))
    ones1_f = st.enter_context(nc.sbuf_tensor("ones1_f", [1, 128], F32))

    consts = st.enter_context(nc.sbuf_tensor("consts", [128, 16], F32))

    s_const = plan.sem("const")
    s_ms = plan.sem("ms")
    plan.emit("sync", lambda e: e.dma_start(out=id_sb[:], in_=ident_h.ap()), incs=[(s_const, 16)])
    plan.emit("vector", lambda e: e.memset(onecol[:], 1.0), incs=[(s_ms, 1)])
    plan.emit("vector", lambda e: e.memset(ones1_f[:], 1.0), incs=[(s_ms, 1)])

    _cc = {}
    def ccol(v):
        if v not in _cc:
            idx = len(_cc)
            assert idx < 16
            _cc[v] = consts[:, idx:idx + 1]
            plan.emit("vector", (lambda ap=_cc[v], v=v: lambda e: e.memset(ap, float(v)))(),
                      incs=[(s_ms, 1)])
        return _cc[v]

    _cci = {}
    def ccol16(v, b):
        ccol(v)
        if v not in _cci:
            for i, (kv, ap) in enumerate(_cc.items()):
                _cci[kv] = i
        idx = _cci[v]
        return consts[b:b + 16, idx:idx + 1]

    l0res = ExitStack()
    xn_nat = l0res.enter_context(nc.sbuf_tensor("xn_nat", [128, NT, D], BF16))
    logpT = l0res.enter_context(nc.sbuf_tensor("logpT", [128, DB, T], BF16))
    p0 = ExitStack()
    x_tile = p0.enter_context(nc.sbuf_tensor("x_tile", [128, 2, D], F32))
    xc = p0.enter_context(nc.sbuf_tensor("xc", [128, D], F32))
    sq = p0.enter_context(nc.sbuf_tensor("sq", [128, D], F32))
    logp_nat = p0.enter_context(nc.sbuf_tensor("logp_nat", [128, D], BF16))
    stt = p0.enter_context(nc.sbuf_tensor("stt", [128, 2, 8], F32))
    xnT_stage = p0.enter_context(nc.sbuf_tensor("xnT_stage", [128, DB, 512], BF16))
    psT = p0.enter_context(nc.psum_tensor("psT", [128, 1024], BF16))
    psT2 = p0.enter_context(nc.psum_tensor("psT2", [128, 1024], BF16))

    s_ldx = plan.sem()
    s_xc = plan.sem()
    s_mu = plan.sem()
    s_ln = plan.sem()
    s_lh = plan.sem()
    s_xn = plan.sem()
    s_m = plan.sem()
    s_l1 = plan.sem()
    s_nl = plan.sem()
    s_lp = plan.sem()
    s_tr = plan.sem()
    s_tr2 = plan.sem()
    s_tev = plan.sem()
    s_pt = plan.sem()
    s_lsd = plan.sem()
    s_xnt = plan.sem()
    s_xntd = plan.sem()

    for t in range(NT):
        par = t % 2
        xt = x_tile[:, par, :]
        sa = stt[:, par, :]
        plan.emit("sync", (lambda xt=xt, t=t: lambda e: e.dma_start(out=xt, in_=di["x"][t * 128:(t + 1) * 128, :]))(),
                  waits=([(s_xc, t - 1)] if t >= 2 else []), incs=[(s_ldx, 16)])
        plan.emit("vector", (lambda xt=xt, sa=sa: lambda e: e.reduce_sum(sa[:, 0:1], xt, axis=AX.X))(),
                  waits=[(s_ldx, 16 * (t + 1))] + ([(s_lp, t - 1)] if t >= 2 else []), post_drain=True)
        plan.emit("vector", (lambda sa=sa: lambda e: e.tensor_scalar(sa[:, 1:2], sa[:, 0:1], -1.0 / D, None, ALU.mult))(),
                  incs=[(s_mu, 1)])
        plan.emit("scalar", (lambda xt=xt, sa=sa: lambda e: e.activation(xc[:], xt, AF.Identity, bias=sa[:, 1:2]))(),
                  waits=[(s_mu, t + 1)], incs=[(s_xc, 1)])
        plan.emit("scalar", (lambda sa=sa: lambda e: e.activation(sq[:], xc[:], AF.Square, accum_out=sa[:, 2:3]))(), post_drain=True)
        plan.emit("scalar", (lambda sa=sa, sc=ccol(1.0 / D), bi=ccol(EPS): lambda e: e.activation(sa[:, 3:4], sa[:, 2:3], AF.Ln, scale=sc, bias=bi))(),
                  waits=[(s_ms, s_ms.n)], incs=[(s_ln, 1)])
        plan.emit("vector", (lambda sa=sa: lambda e: e.tensor_scalar(sa[:, 4:5], sa[:, 3:4], -0.5, None, ALU.mult))(),
                  waits=[(s_ln, t + 1)], incs=[(s_lh, 1)])
        plan.emit("scalar", (lambda sa=sa: lambda e: e.activation(sa[:, 5:6], sa[:, 4:5], AF.Exp))(),
                  waits=[(s_lh, t + 1)])
        plan.emit("scalar", (lambda t=t, sa=sa: lambda e: e.activation(xn_nat[:, t, :], xc[:], AF.Identity, scale=sa[:, 5:6]))(),
                  incs=[(s_xn, 1)])
        plan.emit("vector", (lambda t=t, sa=sa: lambda e: e.reduce_max(sa[:, 6:7], xn_nat[:, t, :], axis=AX.X))(),
                  waits=[(s_xn, t + 1)], post_drain=True)
        plan.emit("vector", (lambda sa=sa: lambda e: e.tensor_scalar(sa[:, 7:8], sa[:, 6:7], -1.0, None, ALU.mult))(),
                  incs=[(s_m, 1)])
        plan.emit("scalar", (lambda t=t, sa=sa: lambda e: e.activation(sq[:], xn_nat[:, t, :], AF.Exp, bias=sa[:, 7:8], accum_out=sa[:, 0:1]))(),
                  waits=[(s_m, t + 1)], post_drain=True)
        plan.emit("scalar", (lambda sa=sa: lambda e: e.activation(sa[:, 1:2], sa[:, 0:1], AF.Ln))(),
                  incs=[(s_l1, 1)])
        plan.emit("vector", (lambda sa=sa: lambda e: e.scalar_tensor_tensor(sa[:, 2:3], sa[:, 6:7], -1.0, sa[:, 1:2], ALU.mult, ALU.subtract))(),
                  waits=[(s_l1, t + 1)], incs=[(s_nl, 1)])
        plan.emit("scalar", (lambda t=t, sa=sa: lambda e: e.activation(logp_nat[:], xn_nat[:, t, :], AF.Identity, bias=sa[:, 2:3]))(),
                  waits=[(s_nl, t + 1)], incs=[(s_lp, 1)])
        def mk_tr(t=t):
            def f(e):
                ins = None
                for d in range(DB):
                    ins = e.transpose(psT[:, d * 128:(d + 1) * 128], logp_nat[:, d * 128:(d + 1) * 128], id_sb[:])
                return ins
            return f
        plan.emit("tensor", mk_tr(),
                  waits=[(s_lp, t + 1), (s_tev, t), (s_const, 16)], incs=[(s_tr, 1)])
        plan.emit("vector", (lambda t=t: lambda e: e.tensor_copy(
            logpT[:, :, t * 128:(t + 1) * 128],
            psT[:].rearrange("p (a b) -> p a b", a=DB)))(),
            waits=[(s_tr, t + 1)], incs=[(s_tev, 1)])
        def mk_tr2(t=t):
            def f(e):
                ins = None
                for d in range(DB):
                    ins = e.transpose(psT2[:, d * 128:(d + 1) * 128], xn_nat[:, t, d * 128:(d + 1) * 128], id_sb[:])
                return ins
            return f
        plan.emit("tensor", mk_tr2(), waits=[(s_xn, t + 1), (s_xnt, t)], incs=[(s_tr2, 1)])
        plan.emit("vector", (lambda t=t: lambda e: e.tensor_copy(
            xnT_stage[:, :, (t % 4) * 128:(t % 4 + 1) * 128],
            psT2[:].rearrange("p (a b) -> p a b", a=DB)))(),
            waits=[(s_tr2, t + 1)] + ([(s_xntd, 16 * (t // 4))] if t >= 4 else []),
            incs=[(s_xnt, 1)])
        if t % 4 == 3:
            plan.emit("sync", (lambda t=t: lambda e: e.dma_start(
                out=xnT_hbm[:, :, (t - 3) * 128:(t + 1) * 128].rearrange("a p c -> p a c"),
                in_=xnT_stage[:]))(),
                waits=[(s_xnt, t + 1)], incs=[(s_xntd, 16)])

    if debug:
        s_dbg = plan.sem()
        plan.emit("gpsimd", lambda e: e.dma_start(out=dbg_outs["xn_nat_d"], in_=xn_nat[:]),
                  waits=[(s_xn, NT)], incs=[(s_dbg, 16)])
        plan.emit("gpsimd", lambda e: e.dma_start(out=dbg_outs["logpT_d"], in_=logpT[:]),
                  waits=[(s_tev, NT)], incs=[(s_dbg, 16)])

    if "pa" in phases:
        p0.close()
        pa = ExitStack()
        pT = pa.enter_context(nc.sbuf_tensor("pT", [128, DB, T], BF16))
        expst = pa.enter_context(nc.sbuf_tensor("expst", [128, 2, 128], BF16))
        attn_sb = pa.enter_context(nc.sbuf_tensor("attn_sb", [128, 2, D], F32))
        rs_sb = pa.enter_context(nc.sbuf_tensor("rs_sb", [128, 2], F32))
        psS = [pa.enter_context(nc.psum_tensor(f"psS{k}", [128, 512], F32)) for k in range(2)]
        psO = [pa.enter_context(nc.psum_tensor(f"psO{k}", [128, 1024], F32)) for k in range(2)]
        psR = [pa.enter_context(nc.psum_tensor(f"psR{k}", [128, 512], F32)) for k in range(2)]

        for d in range(DB):
            plan.emit("scalar", (lambda d=d: lambda e: e.activation(pT[:, d, :], logpT[:, d, :], AF.Exp))(),
                      waits=[(s_tev, NT), (s_xnt, NT), (s_xntd, 16 * (NT // 4))], incs=[(s_pt, 1)])

        s_sc = plan.sem()
        s_ex = plan.sem()
        s_og = plan.sem()
        s_ev = plan.sem()
        s_st = plan.sem()

        for i in range(NT):
            par = i % 2
            for j in range(NT):
                q = i * NT + j
                sp = q % 2
                def mk_sc(i=i, j=j, sp=sp):
                    def f(e):
                        ins = None
                        for d in range(DB):
                            ins = e.matmul(psS[sp][:, 0:128],
                                           logpT[:, d, j * 128:(j + 1) * 128],
                                           pT[:, d, i * 128:(i + 1) * 128],
                                           start=(d == 0), stop=(d == DB - 1))
                        return ins
                    return f
                plan.emit("tensor", mk_sc(),
                          waits=[(s_pt, DB)] + ([(s_ex, q - 1)] if q >= 2 else []),
                          incs=[(s_sc, 1)])
                plan.emit("scalar", (lambda q=q, sp=sp: lambda e: e.activation(
                    expst[:, sp, :], psS[sp][:, 0:128], AF.Exp))(),
                    waits=[(s_sc, q + 1)] + ([(s_og, q - 1)] if q >= 2 else []),
                    incs=[(s_ex, 1)], post_drain=True)
                def mk_o(i=i, j=j, sp=sp, par=par):
                    def f(e):
                        e.matmul(psO[par][:, 0:512], expst[:, sp, :], xn_nat[:, j, 0:512],
                                 start=(j == 0), stop=(j == NT - 1))
                        e.matmul(psO[par][:, 512:1024], expst[:, sp, :], xn_nat[:, j, 512:1024],
                                 start=(j == 0), stop=(j == NT - 1))
                        return e.matmul(psR[par][:, 0:1], expst[:, sp, :], onecol[:],
                                        start=(j == 0), stop=(j == NT - 1))
                    return f
                plan.emit("tensor", mk_o(),
                          waits=[(s_ex, q + 1), (s_ms, 2)] + ([(s_ev, i - 1)] if (j == 0 and i >= 2) else []),
                          incs=[(s_og, 1)])
            plan.emit("vector", (lambda par=par: lambda e: e.reciprocal(rs_sb[:, par:par + 1], psR[par][:, 0:1]))(),
                      waits=[(s_og, NT * (i + 1))] + ([(s_st, 16 * (i - 1))] if i >= 2 else []), post_drain=True)
            plan.emit("vector", (lambda par=par: lambda e: e.tensor_scalar(
                attn_sb[:, par, :], psO[par][:], rs_sb[:, par:par + 1], None, ALU.mult))(),
                incs=[(s_ev, 1)])
            plan.emit("sync", (lambda i=i, par=par: lambda e: e.dma_start(
                out=attn0_hbm[i * 128:(i + 1) * 128, :], in_=attn_sb[:, par, :]))(),
                waits=[(s_ev, i + 1)], incs=[(s_st, 16)])

        pa_done = [(s_st, 16 * NT)]
        pa.close()
    else:
        p0.close()
    l0res.close()

    SLAB = 2048
    NTB = SLAB // 128
    NCHK = SLAB // 512
    NU = 16
    if "pb" in phases:
        pb = ExitStack()
        xnT_sl = pb.enter_context(nc.sbuf_tensor("xnT_sl", [128, DB, SLAB], BF16))
        acc = pb.enter_context(nc.sbuf_tensor("acc", [128, NTB, D], F32))
        wbuf = pb.enter_context(nc.sbuf_tensor("wbuf", [128, 2, 2, DB, D], BF16))
        hbuf = pb.enter_context(nc.sbuf_tensor("hbuf", [128, DB, 3, 512], BF16))
        eb1col = pb.enter_context(nc.sbuf_tensor("eb1col", [128, 2, DB], F32))
        eb2row = pb.enter_context(nc.sbuf_tensor("eb2row", [1, 2, D], F32))
        cw_sb = pb.enter_context(nc.sbuf_tensor("cw_sb", [128, NTB, 16], F32))
        rWf = pb.enter_context(nc.sbuf_tensor("rWf", [128, DB, 15], F32))
        rWsb = pb.enter_context(nc.sbuf_tensor("rWsb", [128, DB, 15], BF16))
        rbrow = pb.enter_context(nc.sbuf_tensor("rbrow", [1, 15], F32))
        pe_sb = pb.enter_context(nc.sbuf_tensor("pe_sb", [128, 15], F32))
        pe2_sb = pb.enter_context(nc.sbuf_tensor("pe2_sb", [128, 15], F32))
        z_dbg = pb.enter_context(nc.sbuf_tensor("z_dbg", [128, 15], F32))
        z2_dbg = pb.enter_context(nc.sbuf_tensor("z2_dbg", [128, 53], F32))
        rst = pb.enter_context(nc.sbuf_tensor("rst", [128, 8], F32))
        ai_t = pb.enter_context(nc.sbuf_tensor("ai_t", [128, 1, 2, D], F32))
        psH = [pb.enter_context(nc.psum_tensor(f"psH{k}", [128, 512], F32)) for k in range(2)]
        psY = [pb.enter_context(nc.psum_tensor(f"psY{k}", [128, 1024], F32)) for k in range(2)]
        psZ = pb.enter_context(nc.psum_tensor("psZ", [128, 512], F32))

        s_xsl = plan.sem()
        s_ai = plan.sem()
        s_aid = plan.sem()
        s_cw = plan.sem()
        s_rt = plan.sem()
        s_zmm = plan.sem()
        s_zrd = plan.sem()
        s_wW = plan.sem()
        s_wB = plan.sem()
        s_wR = plan.sem()
        s_m1 = plan.sem()
        s_hev = plan.sem()
        s_m2 = plan.sem()
        s_yev = plan.sem()
        s_x1 = plan.sem()
        s_u = plan.sem()
        s_hdump = plan.sem()
        s_accd = plan.sem()

        plan.gate([(s_st, 16 * NT)] if "pa" in phases else [])

        plan.emit("scalar", lambda e: e.dma_start(out=rWf[:], in_=di["rW"].rearrange("(d p) e -> p d e", p=128),
                                                  allow_slow_non_contiguous=True), incs=[(s_wR, 16)])
        plan.emit("scalar", lambda e: e.dma_start(out=rbrow[:], in_=di["rb"][None, :]), incs=[(s_wR, 16)])
        plan.emit("vector", lambda e: e.tensor_copy(rWsb[:], rWf[:]), waits=[(s_wR, 32)], incs=[(s_wR, 16)])

        BIGN = -1e30
        uglob = 0
        for s in range(2):
            t0 = s * NTB
            plan.emit("sync", (lambda s=s: lambda e: e.dma_start(
                out=xnT_sl[:], in_=xnT_hbm[:, :, s * SLAB:(s + 1) * SLAB].rearrange("a p c -> p a c")))(),
                waits=[(s_xntd, 16 * (NT // 4))] + ([(s_u16, 16)] if s == 1 else []),
                incs=[(s_xsl, 16)])
            for tb in range(NTB):
                gt = t0 + tb
                par = 0
                w_ai = [(s_ai, (s * NTB + tb))] if (s * NTB + tb) >= 1 else []
                plan.emit("sync", (lambda gt=gt, par=par: lambda e: e.dma_start(
                    out=ai_t[:, par, 0, :], in_=di["x"][gt * 128:(gt + 1) * 128, :]))(),
                    waits=w_ai, incs=[(s_aid, 16)])
                plan.emit("sync", (lambda gt=gt, par=par: lambda e: e.dma_start(
                    out=ai_t[:, par, 1, :], in_=attn0_hbm[gt * 128:(gt + 1) * 128, :]))(),
                    waits=w_ai, incs=[(s_aid, 16)])
                plan.emit("vector", (lambda tb=tb, par=par: lambda e: e.tensor_tensor(
                    acc[:, tb, :], ai_t[:, par, 0, :], ai_t[:, par, 1, :], ALU.add))(),
                    waits=[(s_aid, 32 * (s * NTB + tb + 1)), (s_x1, 16 * NTB * s)], incs=[(s_ai, 1)])
            for tb in range(NTB):
                gtb = s * NTB + tb
                def mk_z(tb=tb, s=s):
                    def f(e):
                        for d in range(DB):
                            e.matmul(psZ[:, 0:15], xnT_sl[:, d, tb * 128:(tb + 1) * 128],
                                     rWsb[:, d, :], start=(d == 0), stop=False)
                        return e.matmul(psZ[:, 0:15], ones1_f[:], rbrow[:], start=False, stop=True)
                    return f
                plan.emit("tensor", mk_z(),
                          waits=[(s_wR, 48), (s_xsl, 16 * (s + 1)), (s_ms, 2)] + ([(s_zrd, gtb)] if gtb >= 1 else []),
                          incs=[(s_zmm, 1)])
                cw = cw_sb[:, tb, 0:15]
                if debug == "deep" and gtb == 0:
                    plan.emit("vector", (lambda: lambda e: e.tensor_copy(z_dbg[:], psZ[:, 0:15]))(),
                              waits=[(s_zmm, 1)], post_drain=True)
                plan.emit("vector", (lambda: lambda e: e.reduce_max(rst[:, 0:1], psZ[:, 0:15], axis=AX.X))(),
                          waits=[(s_zmm, gtb + 1)], post_drain=True)
                plan.emit("vector", (lambda: lambda e: e.tensor_scalar(rst[:, 1:2], rst[:, 0:1], -1.0, None, ALU.mult))(),
                          incs=[(s_rt, 1)], post_drain=True)
                plan.emit("scalar", (lambda: lambda e: e.activation(pe_sb[:], psZ[:, 0:15], AF.Exp, bias=rst[:, 1:2], accum_out=rst[:, 2:3]))(),
                          waits=[(s_rt, gtb + 1), (s_cw, gtb)], post_drain=True, incs=[(s_zrd, 1)])
                plan.emit("vector", (lambda: lambda e: e.reciprocal(rst[:, 3:4], rst[:, 2:3]))(),
                          waits=[(s_zrd, gtb + 1)], post_drain=True)
                plan.emit("vector", (lambda: lambda e: e.tensor_scalar(pe_sb[:], pe_sb[:], rst[:, 3:4], None, ALU.mult))(), post_drain=True)
                if debug == "deep" and gtb == 0:
                    plan.emit("vector", (lambda: lambda e: e.tensor_copy(z2_dbg[:, 0:15], pe_sb[:]))(), post_drain=True)
                    plan.emit("vector", (lambda: lambda e: e.tensor_copy(z2_dbg[:, 15:23], rst[:]))(), post_drain=True)
                plan.emit("vector", (lambda: lambda e: e.reduce_max(rst[:, 4:5], pe_sb[:], axis=AX.X))(), post_drain=True)
                plan.emit("vector", (lambda cw=cw: lambda e: e.tensor_scalar(cw, pe_sb[:], rst[:, 4:5], None, ALU.is_ge))(), post_drain=True)
                plan.emit("vector", (lambda cw=cw: lambda e: e.scalar_tensor_tensor(cw, cw, BIGN, pe_sb[:], ALU.mult, ALU.add))(), post_drain=True)
                plan.emit("vector", (lambda cw=cw: lambda e: e.reduce_max(rst[:, 5:6], cw, axis=AX.X))(), post_drain=True)
                plan.emit("vector", (lambda cw=cw: lambda e: e.tensor_scalar(pe2_sb[:], cw, rst[:, 5:6], None, ALU.is_ge))(), post_drain=True)
                plan.emit("vector", (lambda cw=cw: lambda e: e.scalar_tensor_tensor(cw, pe2_sb[:], BIGN, cw, ALU.mult, ALU.add))(), post_drain=True)
                plan.emit("vector", (lambda cw=cw: lambda e: e.reduce_max(rst[:, 6:7], cw, axis=AX.X))(), post_drain=True)
                plan.emit("vector", (lambda cw=cw: lambda e: e.tensor_scalar(cw, pe_sb[:], rst[:, 6:7], None, ALU.is_ge))(), post_drain=True)
                plan.emit("vector", (lambda cw=cw: lambda e: e.tensor_tensor(cw, cw, pe_sb[:], ALU.mult))(),
                          incs=[(s_cw, 1)], post_drain=True)
                if debug == "deep" and gtb == 0:
                    plan.emit("vector", (lambda: lambda e: e.tensor_copy(z2_dbg[:, 23:38], pe_sb[:]))(), post_drain=True)
                    plan.emit("vector", (lambda cw=cw: lambda e: e.tensor_copy(z2_dbg[:, 38:53], cw))(), post_drain=True)
            if debug == "deep" and s == 0:
                plan.emit("sync", lambda e: e.dma_start(out=dbg_outs["acc_d"], in_=acc[:]),
                          waits=[(s_yev, 16)], incs=[(s_x1, 16), (s_accd, 16)])
            for u in range(NU):
                ug = uglob; uglob += 1
                par = ug % 2
                if u == 0:
                    w1src, w2src = di["sW1"], di["sW2"]
                    b1src, b2src = di["sb1"], di["sb2"]
                else:
                    w1src, w2src = di["eW1"][u - 1], di["eW2"][u - 1]
                    b1src, b2src = di["eb1"][u - 1], di["eb2"][u - 1]
                wwait = [(s_u, ug - 1)] if ug >= 2 else []
                plan.emit("gpsimd", (lambda w=w1src, par=par: lambda e: e.dma_start(
                    out=wbuf[:, par, 0, :, :], in_=w.rearrange("(d p) h -> p d h", p=128)))(),
                    waits=wwait, incs=[(s_wW, 16)])
                plan.emit("gpsimd", (lambda w=w2src, par=par: lambda e: e.dma_start(
                    out=wbuf[:, par, 1, :, :], in_=w.rearrange("(d p) h -> p d h", p=128)))(),
                    incs=[(s_wW, 16)])
                plan.emit("scalar", (lambda b=b1src, par=par: lambda e: e.dma_start(
                    out=eb1col[:, par, :], in_=b.rearrange("(d p) -> p d", p=128),
                    allow_slow_non_contiguous=True))(),
                    waits=wwait, incs=[(s_wB, 16)])
                plan.emit("scalar", (lambda b=b2src, par=par: lambda e: e.dma_start(
                    out=eb2row[:, par, :], in_=b[None, :]))(),
                    incs=[(s_wB, 16)])
                wready = 32 * (ug + 1)
                for ch in range(NCHK + 1):
                    cidx = ug * NCHK + min(ch, NCHK - 1)
                    if False:
                        pass
                    elif ch < NCHK:
                        cidx = ug * NCHK + ch
                        slot = cidx % 3
                        for hb in range(DB):
                            hidx = cidx * DB + hb
                            def mk_m1b(ch=ch, hb=hb, par=par, bank=hidx % 2, s=s):
                                def f(e):
                                    ins = None
                                    for d in range(DB):
                                        ins = e.matmul(psH[bank][:], wbuf[:, par, 0, d, hb * 128:(hb + 1) * 128],
                                                       xnT_sl[:, d, ch * 512:(ch + 1) * 512],
                                                       start=(d == 0), stop=(d == DB - 1))
                                    return ins
                                return f
                            plan.emit("tensor", mk_m1b(),
                                      waits=[(s_wW, wready), (s_xsl, 16 * (s + 1))] + ([(s_hev, hidx - 1)] if hidx >= 2 else []),
                                      incs=[(s_m1, 1)])
                            hdw = []
                            plan.emit("scalar", (lambda hb=hb, par=par, slot=slot, bank=hidx % 2: lambda e: e.activation(
                                hbuf[:, hb, slot, :], psH[bank][:], AF.Relu, bias=eb1col[:, par, hb:hb + 1]))(),
                                waits=[(s_m1, hidx + 1), (s_wB, wready)] + ([(s_m2, 4 * (cidx - 2))] if cidx >= 3 else []) + hdw,
                                incs=[(s_hev, 1)], post_drain=True)
                    if ch < NCHK:
                        chm = ch
                        cidx2 = ug * NCHK + chm
                        slot2 = cidx2 % 3
                        for tb4 in range(4):
                            tb = chm * 4 + tb4
                            yidx = cidx2 * 4 + tb4
                            bank = yidx % 2
                            def mk_m2(tb=tb, tb4=tb4, par=par, slot2=slot2, bank=bank):
                                def f(e):
                                    e.matmul(psY[bank][:, 0:512], ones1_f[:], eb2row[0:1, par, 0:512], start=True, stop=False)
                                    e.matmul(psY[bank][:, 512:1024], ones1_f[:], eb2row[0:1, par, 512:1024], start=True, stop=False)
                                    ins = None
                                    for hb in range(DB):
                                        lt = hbuf[:, hb, slot2, tb4 * 128:(tb4 + 1) * 128]
                                        e.matmul(psY[bank][:, 0:512], lt, wbuf[:, par, 1, hb, 0:512],
                                                 start=False, stop=(hb == DB - 1))
                                        ins = e.matmul(psY[bank][:, 512:1024], lt, wbuf[:, par, 1, hb, 512:1024],
                                                       start=False, stop=(hb == DB - 1))
                                    return ins
                                return f
                            plan.emit("tensor", mk_m2(),
                                      waits=[(s_hev, DB * (cidx2 + 1)), (s_ms, 2), (s_wB, wready)] + ([(s_yev, yidx - 1)] if yidx >= 2 else []),
                                      incs=[(s_m2, 1)])
                            accdw = [(s_accd, 16)] if (debug == "deep" and s == 0 and u == 1) else []
                            if u == 0:
                                plan.emit("vector", (lambda tb=tb, bank=bank: lambda e: e.tensor_tensor(
                                    acc[:, tb, :], acc[:, tb, :], psY[bank][:], ALU.add))(),
                                    waits=[(s_m2, yidx + 1), (s_ai, NTB * (s + 1))] + accdw,
                                    incs=[(s_yev, 1)])
                            else:
                                plan.emit("vector", (lambda tb=tb, bank=bank, u=u: lambda e: e.scalar_tensor_tensor(
                                    acc[:, tb, :], psY[bank][:], cw_sb[:, tb, u - 1:u], acc[:, tb, :], ALU.mult, ALU.add))(),
                                    waits=[(s_m2, yidx + 1), (s_ai, NTB * (s + 1)), (s_cw, NTB * (s + 1))] + accdw,
                                    incs=[(s_yev, 1)])
                plan.emit("vector", (lambda: lambda e: e.tensor_copy(rst[:, 0:1], rst[:, 0:1]))(),
                          waits=[(s_yev, 16 * (ug + 1))], incs=[(s_u, 1)])
            if s == 0:
                s_u16 = plan.sem()
                plan.emit("vector", (lambda: lambda e: e.tensor_copy(rst[:, 0:1], rst[:, 0:1]))(),
                          waits=[(s_yev, 16 * NU)], incs=[(s_u16, 16)])
            for tb in range(NTB):
                gt = t0 + tb
                plan.emit("sync", (lambda gt=gt, tb=tb: lambda e: e.dma_start(
                    out=x1_hbm[gt * 128:(gt + 1) * 128, :], in_=acc[:, tb, :]))(),
                    waits=[(s_yev, 16 * NU * (s + 1))], incs=[(s_x1, 16)])
        if debug == "deep":
            dbg_outs["z_d"] = nc.dram_tensor("z_d", [128, 15], F32, kind="ExternalOutput").ap()
            plan.emit("gpsimd", lambda e: e.dma_start(out=dbg_outs["z_d"], in_=z_dbg[:]),
                      waits=[(s_cw, NTB)], incs=[(s_x1, 16)])
            dbg_outs["z2_d"] = nc.dram_tensor("z2_d", [128, 53], F32, kind="ExternalOutput").ap()
            plan.emit("gpsimd", lambda e: e.dma_start(out=dbg_outs["z2_d"], in_=z2_dbg[:]),
                      waits=[(s_cw, NTB)], incs=[(s_x1, 16)])
            dbg_outs["cw_d"] = nc.dram_tensor("cw_d", [128, NTB, 16], F32, kind="ExternalOutput").ap()
            dbg_outs["acc_d"] = nc.dram_tensor("acc_d", [128, NTB, D], F32, kind="ExternalOutput").ap()
            plan.emit("gpsimd", lambda e: e.dma_start(out=dbg_outs["cw_d"], in_=cw_sb[:]),
                      waits=[(s_cw, NTB)], incs=[(s_x1, 16)])
        pb_done = [(s_x1, 16 * NT + (64 if debug == "deep" else 0))]
        pb.close()



    if "pc" in phases:
        KD = 64
        NH = 16
        C = 128
        CPS = SLAB // C

        wT_hbm = dscr("wT_hbm", (6, 128, DB, D), BF16)
        retT_hbm = dscr("retT_hbm", (128, DB, T), BF16, dbg=True)
        routersT_hbm = dscr("routersT_hbm", (128, DB, 256), BF16)
        oT_hbm = dscr("oT_hbm", (128, DB, T), BF16, dbg=True)
        rowscrF = dscr("rowscrF", (3, 16, SLAB), F32, dbg=True)

        maskS_h = nc.inline_tensor(np.tril(np.ones((128, 128), np.float32), -1), name="maskS")
        bd = np.zeros((128, 128), np.float32)
        for p_ in range(4):
            bd[p_*32:(p_+1)*32, p_*32:(p_+1)*32] = np.tril(np.ones((32, 32)), -1)
        maskBD_h = nc.inline_tensor(bd, name="maskBD")
        maskBDU_h = nc.inline_tensor(bd.T.copy(), name="maskBDU")
        maskI_h = nc.inline_tensor(np.tril(np.ones((128, 128), np.float32), 0), name="maskI")
        if32_h = nc.inline_tensor(np.eye(128, dtype=np.float32), name="if32")
        ones2_np = np.zeros((8, 128, 16), np.float32)
        for pt in range(8):
            for r in range(2):
                ones2_np[pt, r * 64:(r + 1) * 64, 2 * pt + r] = 1.0
        onesB_h = nc.inline_tensor(ones2_np.astype(ml_dtypes.bfloat16), name="onesB")
        mselnp = np.zeros((4, 128, 256), np.float32)
        for m in range(256):
            for level in range(8):
                node = 2 ** level - 1 + (m >> (8 - level))
                bit = (m >> (7 - level)) & 1
                row = node if bit else 256 + node
                mselnp[row // 128, row % 128, m] = 1.0
        msel_h = nc.inline_tensor(mselnp, name="msel")

        pcc = ExitStack()
        S_sb = pcc.enter_context(nc.sbuf_tensor("S_sb", [128, DB, KD], BF16))
        maskS = pcc.enter_context(nc.sbuf_tensor("maskS_sb", [128, 128], F32))
        maskI = pcc.enter_context(nc.sbuf_tensor("maskI_sb", [128, 128], F32))
        maskBD = pcc.enter_context(nc.sbuf_tensor("maskBD_sb", [128, 128], F32))
        maskBDU = pcc.enter_context(nc.sbuf_tensor("maskBDU_sb", [128, 128], F32))
        if32 = pcc.enter_context(nc.sbuf_tensor("if32_sb", [128, 128], F32))
        onesB = pcc.enter_context(nc.sbuf_tensor("onesB_sb", [128, 8, 16], BF16))
        WgT = pcc.enter_context(nc.sbuf_tensor("WgT", [128, DB, 16], BF16))
        WbT = pcc.enter_context(nc.sbuf_tensor("WbT", [128, DB, 16], BF16))
        ctap = pcc.enter_context(nc.sbuf_tensor("ctap", [128, 3, DB, 4], F32))
        cbia = pcc.enter_context(nc.sbuf_tensor("cbia", [128, 3, DB], F32))
        fwoutb = pcc.enter_context(nc.sbuf_tensor("fwoutb", [1, D], F32))
        halo = pcc.enter_context(nc.sbuf_tensor("halo", [128, 3, DB, 3], BF16))
        ones1b = pcc.enter_context(nc.sbuf_tensor("ones1b", [1, 128], BF16))

        s_pcld = plan.sem()
        s_rowE = plan.sem()
        s_xc = plan.sem()
        s_wn = plan.sem()
        s_wst = plan.sem()
        s_wstd = plan.sem()
        s_out = plan.sem()

        plan.gate([(s_x1, 16 * NT)])

        plan.emit("sync", lambda e: e.dma_start(out=maskS[:], in_=maskS_h.ap()), incs=[(s_pcld, 16)])
        plan.emit("sync", lambda e: e.dma_start(out=maskI[:], in_=maskI_h.ap()), incs=[(s_pcld, 16)])
        plan.emit("sync", lambda e: e.dma_start(out=maskBD[:], in_=maskBD_h.ap()), incs=[(s_pcld, 16)])
        plan.emit("sync", lambda e: e.dma_start(out=maskBDU[:], in_=maskBDU_h.ap()), incs=[(s_pcld, 16)])
        plan.emit("sync", lambda e: e.dma_start(out=if32[:], in_=if32_h.ap()), incs=[(s_pcld, 16)])
        plan.emit("sync", lambda e: e.dma_start(out=onesB[:], in_=onesB_h.ap().rearrange("a p c -> p a c")), incs=[(s_pcld, 16)])
        for i, (wk, bk) in enumerate([("cqw", "cqb"), ("ckw", "ckb"), ("cvw", "cvb")]):
            plan.emit("sync", (lambda i=i, wk=wk: lambda e: e.dma_start(
                out=ctap[:, i, :, :], in_=di[wk].rearrange("(a p) one k -> p a (one k)", p=128),
                allow_slow_non_contiguous=True))(), incs=[(s_pcld, 16)])
            plan.emit("sync", (lambda i=i, bk=bk: lambda e: e.dma_start(
                out=cbia[:, i, :], in_=di[bk].rearrange("(a p) -> p a", p=128),
                allow_slow_non_contiguous=True))(), incs=[(s_pcld, 16)])
        plan.emit("sync", lambda e: e.dma_start(out=fwoutb[:], in_=di["fWoutb"][None, :]), incs=[(s_pcld, 16)])
        plan.emit("vector", lambda e: e.memset(S_sb[:], 0.0), incs=[(s_pcld, 1)])
        plan.emit("vector", lambda e: e.memset(halo[:], 0.0), incs=[(s_pcld, 1)])
        plan.emit("vector", lambda e: e.memset(ones1b[:], 1.0), incs=[(s_pcld, 1)])
        PCLD = s_pcld.n

        wstg = ExitStack()
        wnat = wstg.enter_context(nc.sbuf_tensor("wnat", [128, DB, D], BF16))
        wtv = wstg.enter_context(nc.sbuf_tensor("wtv", [128, DB, D], BF16))
        wgnat = wstg.enter_context(nc.sbuf_tensor("wgnat", [16, D], BF16))
        psW = wstg.enter_context(nc.psum_tensor("psW", [128, 1024], BF16))
        wsrcs = [di["aWq"], di["aWk"], di["aWv"], di["aWo"], di["fWout"], di["fWq"]]
        for wi, wsrc in enumerate(wsrcs):
            plan.emit("gpsimd", (lambda w=wsrc: lambda e: e.dma_start(
                out=wnat[:], in_=w.rearrange("(a p) d -> p a d", p=128)))(),
                waits=([(s_wst, wi * DB)] if wi >= 1 else []), incs=[(s_wn, 16)])
            for db in range(DB):
                def mk_wt(wi=wi, db=db):
                    def f(e):
                        ins = None
                        for a in range(DB):
                            ins = e.transpose(psW[:, a * 128:(a + 1) * 128], wnat[:, a, db * 128:(db + 1) * 128], id_sb[:])
                        return ins
                    return f
                plan.emit("tensor", mk_wt(),
                          waits=[(s_wn, 16 * (wi + 1)), (s_const, 16)] + ([(s_wst, s_wst.n)] if (wi, db) != (0, 0) else []),
                          incs=[(s_tr, 1)])
                plan.emit("vector", (lambda db=db: lambda e: e.tensor_copy(wtv[:, db, :], psW[:]))(),
                          waits=[(s_tr, s_tr.n)], incs=[(s_wst, 1)])
            plan.emit("sync", (lambda wi=wi: lambda e: e.dma_start(out=wT_hbm[wi], in_=wtv[:]))(),
                      waits=[(s_wst, s_wst.n)], incs=[(s_wstd, 16)])
        for gi, (gsrc, gdst) in enumerate([(di["aWg"], WgT), (di["aWb"], WbT)]):
            plan.emit("gpsimd", (lambda g=gsrc: lambda e: e.dma_start(out=wgnat[:], in_=g))(),
                      waits=[(s_wst, s_wst.n)], incs=[(s_wn, 16)])
            def mk_gt(gi=gi):
                def f(e):
                    ins = None
                    for a in range(DB):
                        ins = e.transpose(psW[:, a * 16:(a + 1) * 16], wgnat[:, a * 128:(a + 1) * 128], id_sb[0:16, 0:16])
                    return ins
                return f
            plan.emit("tensor", mk_gt(), waits=[(s_wn, 16 * (7 + gi)), (s_wst, s_wst.n)], incs=[(s_tr, 1)])
            plan.emit("vector", (lambda gdst=gdst: lambda e: e.tensor_copy(
                gdst[:], psW[:, 0:128].rearrange("p (a b) -> p a b", a=DB)))(),
                waits=[(s_tr, s_tr.n)], incs=[(s_wst, 1)])
        plan.emit("vector", lambda e: e.memset(wnat[:, 1, :], 0.0), waits=[(s_wst, s_wst.n)], incs=[(s_wst, 1)])
        plan.emit("gpsimd", lambda e: e.dma_start(out=wnat[:, 0, :], in_=di["frouters"][0:128, :]),
                  waits=[(s_wst, s_wst.n)], incs=[(s_wn, 16)])
        plan.emit("gpsimd", lambda e: e.dma_start(out=wnat[0:127, 1, :], in_=di["frouters"][128:255, :]),
                  incs=[(s_wn, 16)])
        for db in range(DB):
            def mk_rt(db=db):
                def f(e):
                    ins = None
                    for a in range(2):
                        ins = e.transpose(psW[:, a * 128:(a + 1) * 128], wnat[:, a, db * 128:(db + 1) * 128], id_sb[:])
                    return ins
                return f
            plan.emit("tensor", mk_rt(), waits=[(s_wn, 16 * 10), (s_wst, s_wst.n)], incs=[(s_tr, 1)])
            plan.emit("vector", (lambda db=db: lambda e: e.tensor_copy(wtv[:, db, 0:256], psW[:, 0:256]))(),
                      waits=[(s_tr, s_tr.n)], incs=[(s_wst, 1)])
        plan.emit("sync", lambda e: e.dma_start(out=routersT_hbm, in_=wtv[:, :, 0:256]),
                  waits=[(s_wst, s_wst.n)], incs=[(s_wstd, 16)])
        WSTG_E = s_wst.n
        wstg.close()

        s_ffz = plan.sem()
        s_ffe = plan.sem()
        s_ffr = plan.sem()
        s_ffrd = plan.sem()
        s_q1 = plan.sem()
        s_wld2 = plan.sem()
        s_pj = plan.sem()
        s_pje = plan.sem()
        s_cv = plan.sem()
        s_gz = plan.sem()
        s_n2 = plan.sem()
        s_stt = plan.sem()
        s_row = plan.sem()
        s_dl = plan.sem()
        s_dle = plan.sem()
        s_ot = plan.sem()
        s_cmb = plan.sem()
        s_cl = plan.sem()
        s_dbgx = plan.sem()

        for s in range(2):
            t0g = s * NTB
            pcs = ExitStack()
            statsA = pcs.enter_context(nc.sbuf_tensor(f"statsA{s}", [128, SLAB], F32))
            statsB = pcs.enter_context(nc.sbuf_tensor(f"statsB{s}", [128, SLAB], F32))
            statsC = pcs.enter_context(nc.sbuf_tensor(f"statsC{s}", [128, SLAB], F32))
            SL = {"g": (statsA, 0), "e1": (statsA, 32), "lnrho": (statsA, 64), "zb": (statsA, 96),
                  "u": (statsB, 0), "cB": (statsB, 32), "cprev": (statsB, 64), "rn": (statsB, 96),
                  "zg": (statsC, 0), "eps": (statsC, 32), "c": (statsC, 64), "l": (statsC, 96),
                  "lnb": (statsB, 64), "ones": (statsB, 0), "edr": (statsB, 0), "ecr": (statsC, 0)}
            def sl(name, c0=0, c1=SLAB):
                t, b = SL[name]
                return t[b:b + 16, c0:c1]

            sub1 = ExitStack()
            xn1T = sub1.enter_context(nc.sbuf_tensor(f"xn1T{s}", [128, DB, SLAB], BF16, side="right"))

            if PC_STAGE < 2:
                sub1.close()
                pcs.close()
                continue
            ln1 = ExitStack()
            x1t = ln1.enter_context(nc.sbuf_tensor(f"x1t{s}", [128, 2, D], F32))
            xc1 = ln1.enter_context(nc.sbuf_tensor(f"xc1{s}", [128, D], F32))
            sq1 = ln1.enter_context(nc.sbuf_tensor(f"sq1{s}", [128, D], F32))
            xn1n = ln1.enter_context(nc.sbuf_tensor(f"xn1n{s}", [128, D], BF16))
            st1 = ln1.enter_context(nc.sbuf_tensor(f"st1{s}", [128, 2, 8], F32))
            psT1 = ln1.enter_context(nc.psum_tensor(f"psT1{s}", [128, 1024], BF16))
            s_l1s = plan.sem()
            s_l1x = plan.sem()
            s_l1n = plan.sem()
            s_l1t = plan.sem()
            s_l1e = plan.sem()
            for tb in range(NTB):
                gt = t0g + tb
                par = tb % 2
                xt = x1t[:, par, :]
                sa = st1[:, par, :]
                plan.emit("sync", (lambda xt=xt, gt=gt: lambda e: e.dma_start(out=xt, in_=x1_hbm[gt * 128:(gt + 1) * 128, :]))(),
                          waits=[(s_x1, 16 * NT)] + ([(s_l1n, tb - 1)] if tb >= 2 else []), incs=[(s_l1x, 16)])
                plan.emit("vector", (lambda xt=xt, sa=sa: lambda e: e.reduce_sum(sa[:, 0:1], xt, axis=AX.X))(),
                          waits=[(s_l1x, s_l1x.n)] + ([(s_l1n, tb - 1)] if tb >= 2 else []), post_drain=True)
                plan.emit("vector", (lambda sa=sa: lambda e: e.tensor_scalar(sa[:, 1:2], sa[:, 0:1], -1.0 / D, None, ALU.mult))(),
                          incs=[(s_l1s, 1)])
                plan.emit("scalar", (lambda xt=xt, sa=sa: lambda e: e.activation(xc1[:], xt, AF.Identity, bias=sa[:, 1:2]))(),
                          waits=[(s_l1s, s_l1s.n)])
                plan.emit("scalar", (lambda sa=sa: lambda e: e.activation(sq1[:], xc1[:], AF.Square, accum_out=sa[:, 2:3]))(), post_drain=True)
                plan.emit("scalar", (lambda sa=sa, sc=ccol(1.0 / D), bi=ccol(EPS): lambda e: e.activation(sa[:, 3:4], sa[:, 2:3], AF.Ln, scale=sc, bias=bi))())
                plan.emit("scalar", (lambda sa=sa, sc=ccol(-0.5): lambda e: e.activation(sa[:, 4:5], sa[:, 3:4], AF.Exp, scale=sc))(), post_drain=True)
                plan.emit("scalar", (lambda sa=sa: lambda e: e.activation(xn1n[:], xc1[:], AF.Identity, scale=sa[:, 4:5]))(),
                          waits=([(s_l1t, s_l1t.n)] if s_l1t.n >= 1 else []), incs=[(s_l1n, 1)])
                def mk_t1(tb=tb):
                    def f(e):
                        ins = None
                        for d in range(DB):
                            ins = e.transpose(psT1[:, d * 128:(d + 1) * 128], xn1n[:, d * 128:(d + 1) * 128], id_sb[:])
                        return ins
                    return f
                plan.emit("tensor", mk_t1(), waits=[(s_l1n, s_l1n.n), (s_l1e, s_l1e.n)], incs=[(s_l1t, 1)])
                plan.emit("vector", (lambda tb=tb: lambda e: e.tensor_copy(
                    xn1T[:, :, tb * 128:(tb + 1) * 128], psT1[:].rearrange("p (a b) -> p a b", a=DB)))(),
                    waits=[(s_l1t, s_l1t.n)], incs=[(s_l1e, 1)])
            LN1_E = s_l1e.n
            ln1.close()

            if PC_STAGE < 3:
                sub1.close()
                pcs.close()
                continue
            ffn = ExitStack()
            zT = ffn.enter_context(nc.sbuf_tensor(f"zT{s}", [128, 2, SLAB], F32))
            LG = ffn.enter_context(nc.sbuf_tensor(f"LG{s}", [128, 4, SLAB], F32))
            lpT = ffn.enter_context(nc.sbuf_tensor(f"lpT{s}", [128, 2, SLAB], BF16))
            q1c = ffn.enter_context(nc.sbuf_tensor(f"q1c{s}", [128, DB, 512], BF16))
            rstg = ffn.enter_context(nc.sbuf_tensor(f"rstg{s}", [128, DB, 512], BF16))
            wb1 = ffn.enter_context(nc.sbuf_tensor(f"wb1{s}", [128, DB, D], BF16))
            msel = ffn.enter_context(nc.sbuf_tensor(f"msel{s}", [128, 4, 256], F32))
            leafv = ffn.enter_context(nc.sbuf_tensor(f"leafv{s}", [128, 2, D], BF16))
            routersT = ffn.enter_context(nc.sbuf_tensor(f"routersT{s}", [128, DB, 256], BF16))
            psq = ffn.enter_context(nc.psum_tensor(f"psq{s}", [128, 512], F32))
            psz = [ffn.enter_context(nc.psum_tensor(f"psz{s}{k}", [128, 512], F32)) for k in range(2)]

            plan.emit("sync", lambda e: e.dma_start(out=msel[:], in_=msel_h.ap().rearrange("a p c -> p a c")),
                      incs=[(s_pcld, 16)])
            plan.emit("gpsimd", lambda e: e.dma_start(out=leafv[:], in_=di["fleafv"].rearrange("(a p) d -> p a d", p=128)),
                      incs=[(s_pcld, 16)])
            plan.emit("sync", lambda e: e.dma_start(out=routersT[:], in_=routersT_hbm),
                      waits=[(s_wstd, 16 * 7)], incs=[(s_pcld, 16)])
            PCLD2 = s_pcld.n
            plan.emit("sync", lambda e: e.dma_start(out=wb1[:], in_=wT_hbm[5]),
                      waits=[(s_wstd, 16 * 6)], incs=[(s_wld2, 16)])
            W1L = s_wld2.n
            q1_marks = []
            zev_marks = []
            for ch in range(4):
                for eb in range(DB):
                    def mk_q1e(ch=ch, eb=eb):
                        def f(e):
                            ins = None
                            for d in range(DB):
                                ins = e.matmul(psq[:], wb1[:, d, eb * 128:(eb + 1) * 128],
                                               xn1T[:, d, ch * 512:(ch + 1) * 512],
                                               start=(d == 0), stop=(d == DB - 1))
                            return ins
                        return f
                    plan.emit("tensor", mk_q1e(),
                              waits=[(s_wld2, W1L), (s_l1e, LN1_E)] + ([(s_q1, q1_marks[-1])] if q1_marks else []),
                              incs=[(s_ffz, 1)])
                    plan.emit("vector", (lambda eb=eb: lambda e: e.tensor_copy(q1c[:, eb, :], psq[:]))(),
                              waits=[(s_ffz, s_ffz.n)], incs=[(s_q1, 1)])
                    q1_marks.append(s_q1.n)
                for nb in range(2):
                    def mk_z1(ch=ch, nb=nb):
                        def f(e):
                            ins = None
                            for eb in range(DB):
                                ins = e.matmul(psz[nb][:], routersT[:, eb, nb * 128:(nb + 1) * 128],
                                               q1c[:, eb, :], start=(eb == 0), stop=(eb == DB - 1))
                            return ins
                        return f
                    plan.emit("tensor", mk_z1(),
                              waits=[(s_q1, s_q1.n), (s_pcld, PCLD2)] + ([(s_ffe, zev_marks[-2])] if len(zev_marks) >= 2 else []),
                              incs=[(s_ffz, 1)])
                    plan.emit("vector", (lambda ch=ch, nb=nb: lambda e: e.tensor_copy(
                        zT[:, nb, ch * 512:(ch + 1) * 512], psz[nb][:]))(),
                        waits=[(s_ffz, s_ffz.n)], incs=[(s_ffe, 1)])
                    zev_marks.append(s_ffe.n)
            ZT_E = s_ffe.n
            for nb in range(2):
                plan.emit("scalar", (lambda nb=nb: lambda e: e.activation(LG[:, 2 + nb, :], zT[:, nb, :], AF.Abs))(),
                          waits=[(s_ffe, ZT_E)])
                plan.emit("scalar", (lambda nb=nb, sc=ccol(-1.0): lambda e: e.activation(LG[:, 2 + nb, :], LG[:, 2 + nb, :], AF.Exp, scale=sc))())
                plan.emit("scalar", (lambda nb=nb, bi=ccol(1.0): lambda e: e.activation(LG[:, 2 + nb, :], LG[:, 2 + nb, :], AF.Ln, bias=bi))())
                plan.emit("scalar", (lambda nb=nb, sc=ccol(-1.0): lambda e: e.activation(LG[:, nb, :], zT[:, nb, :], AF.Relu, scale=sc))(),
                          incs=[(s_ffe, 1)])
                plan.emit("vector", (lambda nb=nb: lambda e: e.scalar_tensor_tensor(
                    LG[:, nb, :], LG[:, nb, :], -1.0, LG[:, 2 + nb, :], ALU.mult, ALU.subtract))(),
                    waits=[(s_ffe, s_ffe.n)], incs=[(s_ffe, 1)])
                plan.emit("scalar", (lambda nb=nb: lambda e: e.activation(zT[:, nb, :], zT[:, nb, :], AF.Relu))(),
                          waits=[(s_ffe, s_ffe.n)], incs=[(s_ffe, 1)])
                plan.emit("vector", (lambda nb=nb: lambda e: e.scalar_tensor_tensor(
                    LG[:, 2 + nb, :], zT[:, nb, :], -1.0, LG[:, 2 + nb, :], ALU.mult, ALU.subtract))(),
                    waits=[(s_ffe, s_ffe.n)], incs=[(s_ffe, 1)])
            LG_E = s_ffe.n
            lp_marks = []
            for mb in range(2):
                for ch in range(4):
                    def mk_lp(mb=mb, ch=ch):
                        def f(e):
                            ins = None
                            for k in range(4):
                                ins = e.matmul(psz[mb][:], msel[:, k, mb * 128:(mb + 1) * 128],
                                               LG[:, k, ch * 512:(ch + 1) * 512],
                                               start=(k == 0), stop=(k == 3))
                            return ins
                        return f
                    plan.emit("tensor", mk_lp(),
                              waits=[(s_ffe, LG_E), (s_pcld, PCLD2)] + ([(s_ffe, lp_marks[-1])] if lp_marks else []),
                              incs=[(s_ffz, 1)])
                    plan.emit("scalar", (lambda mb=mb, ch=ch: lambda e: e.activation(
                        lpT[:, mb, ch * 512:(ch + 1) * 512], psz[mb][:], AF.Exp))(),
                        waits=[(s_ffz, s_ffz.n)], incs=[(s_ffe, 1)], post_drain=True)
                    lp_marks.append(s_ffe.n)
            LP_E = s_ffe.n
            rt_marks = []
            rtd_marks = []
            for ch in range(4):
                for db in range(DB):
                    def mk_rT(ch=ch, db=db):
                        def f(e):
                            ins = None
                            for m in range(2):
                                ins = e.matmul(psq[:], leafv[:, m, db * 128:(db + 1) * 128],
                                               lpT[:, m, ch * 512:(ch + 1) * 512],
                                               start=(m == 0), stop=(m == 1))
                            return ins
                        return f
                    plan.emit("tensor", mk_rT(),
                              waits=[(s_ffe, LP_E), (s_pcld, PCLD2)] + ([(s_ffr, rt_marks[-1])] if rt_marks else []),
                              incs=[(s_ffz, 1)])
                    plan.emit("vector", (lambda db=db: lambda e: e.tensor_copy(rstg[:, db, :], psq[:]))(),
                              waits=[(s_ffz, s_ffz.n)] + ([(s_ffrd, rtd_marks[-1])] if rtd_marks and db == 0 else []),
                              incs=[(s_ffr, 1)])
                    rt_marks.append(s_ffr.n)
                plan.emit("sync", (lambda s2=s, ch=ch: lambda e: e.dma_start(
                    out=retT_hbm[:, :, (s2 * 4 + ch) * 512:(s2 * 4 + ch + 1) * 512], in_=rstg[:]))(),
                    waits=[(s_ffr, s_ffr.n)], incs=[(s_ffrd, 16)])
                rtd_marks.append(s_ffrd.n)
            RETD_E = s_ffrd.n
            ffn.close()

            if PC_STAGE < 4:
                sub1.close()
                pcs.close()
                continue
            qkvs = ExitStack()
            qc = qkvs.enter_context(nc.sbuf_tensor(f"qc{s}", [128, DB, SLAB], BF16))
            kc = qkvs.enter_context(nc.sbuf_tensor(f"kc{s}", [128, DB, SLAB], BF16))
            vc = qkvs.enter_context(nc.sbuf_tensor(f"vc{s}", [128, DB, SLAB], BF16))
            prj = ExitStack()
            rawwin = prj.enter_context(nc.sbuf_tensor(f"rawwin{s}", [128, 3, DB, 515], BF16, side="right"))
            wb2 = prj.enter_context(nc.sbuf_tensor(f"wb2{s}", [128, DB, D], BF16, side="right"))
            ksq = prj.enter_context(nc.sbuf_tensor(f"ksq{s}", [128, DB, 128], BF16, side="right"))
            psp = [prj.enter_context(nc.psum_tensor(f"psp{s}{k}", [128, 512], F32)) for k in range(2)]
            psg = prj.enter_context(nc.psum_tensor(f"psg{s}", [128, 512], F32))
            convd = [qc, kc, vc]

            gz_marks = []
            for gi, (wt, slot) in enumerate([(WgT, "zg"), (WbT, "zb")]):
                for ch in range(4):
                    gb = 0 if slot == "zg" else 96
                    def mk_gz(wt=wt, ch=ch, gb=gb):
                        def f(e):
                            ins = None
                            for d in range(DB):
                                ins = e.matmul(psg[gb:gb + 16, :], wt[:, d, :], xn1T[:, d, ch * 512:(ch + 1) * 512],
                                               start=(d == 0), stop=(d == DB - 1),
                                               tile_position=(0, gb) if gb else None)
                            return ins
                        return f
                    plan.emit("tensor", mk_gz(),
                              waits=[(s_l1e, LN1_E), (s_wst, WSTG_E)] + ([(s_gz, gz_marks[-1])] if gz_marks else []),
                              incs=[(s_gz, 1)])
                    plan.emit("vector", (lambda slot=slot, ch=ch, gb=gb: lambda e: e.tensor_copy(
                        sl(slot, ch * 512, (ch + 1) * 512), psg[gb:gb + 16, :]))(),
                        waits=[(s_gz, s_gz.n)], incs=[(s_gz, 1)])
                    gz_marks.append(s_gz.n)
            GZ_E = s_gz.n

            pj_marks = []
            cva_marks = {}
            for ch in range(4):
                for ti in range(3):
                    plan.emit("sync", (lambda ti=ti: lambda e: e.dma_start(out=wb2[:], in_=wT_hbm[ti]))(),
                              waits=[(s_wstd, 16 * 6), (s_pj, s_pj.n)], incs=[(s_wld2, 16)])
                    WL2 = s_wld2.n
                    for cb in range(DB):
                        def mk_pj(ti=ti, ch=ch, cb=cb):
                            def f(e):
                                ins = None
                                for d in range(DB):
                                    ins = e.matmul(psp[cb % 2][:], wb2[:, d, cb * 128:(cb + 1) * 128],
                                                   xn1T[:, d, ch * 512:(ch + 1) * 512],
                                                   start=(d == 0), stop=(d == DB - 1))
                                return ins
                            return f
                        plan.emit("tensor", mk_pj(),
                                  waits=[(s_wld2, WL2), (s_l1e, LN1_E)] + ([(s_pje, pj_marks[-2])] if len(pj_marks) >= 2 else []),
                                  incs=[(s_pj, 1)])
                        w_extra = []
                        if ch >= 1 and cb == 0:
                            w_extra = [(s_cv, cva_marks[(ch - 1, ti)])]
                        plan.emit("vector", (lambda ti=ti, cb=cb: lambda e: e.tensor_copy(
                            rawwin[:, ti, cb, 3:515], psp[cb % 2][:]))(),
                            waits=[(s_pj, s_pj.n)] + w_extra, incs=[(s_pje, 1)])
                        pj_marks.append(s_pje.n)
                    plan.emit("vector", (lambda ti=ti: lambda e: e.tensor_copy(rawwin[:, ti, :, 0:3], halo[:, ti, :, :]))(),
                              waits=[(s_pje, s_pje.n), (s_pcld, PCLD)], incs=[(s_cv, 1)])
                    plan.emit("vector", (lambda ti=ti: lambda e: e.tensor_copy(halo[:, ti, :, :], rawwin[:, ti, :, 512:515]))(),
                              incs=[(s_cv, 1)])
                    for cb in range(DB):
                        plan.emit("scalar", (lambda ti=ti, cb=cb, ch=ch: lambda e: e.activation(
                            convd[ti][:, cb, ch * 512:(ch + 1) * 512], rawwin[:, ti, cb, 3:515],
                            AF.Identity, scale=ctap[:, ti, cb, 3:4], bias=cbia[:, ti, cb:cb + 1]))(),
                            waits=[(s_cv, s_cv.n), (s_pcld, PCLD)], incs=[(s_cv, 1)])
                        for tap in range(3):
                            plan.emit("vector", (lambda ti=ti, cb=cb, ch=ch, tap=tap: lambda e: e.scalar_tensor_tensor(
                                convd[ti][:, cb, ch * 512:(ch + 1) * 512], rawwin[:, ti, cb, tap:512 + tap],
                                ctap[:, ti, cb, tap:tap + 1], convd[ti][:, cb, ch * 512:(ch + 1) * 512],
                                ALU.mult, ALU.add))(),
                                waits=[(s_cv, s_cv.n)], incs=[(s_cv, 1)])
                    cva_marks[(ch, ti)] = s_cv.n
                for cc4 in range(4):
                    n = ch * 4 + cc4
                    plan.emit("scalar", (lambda ch=ch, cc4=cc4: lambda e: e.activation(
                        ksq[:],
                        kc[:, :, ch * 512 + cc4 * 128: ch * 512 + (cc4 + 1) * 128],
                        AF.Square))(),
                        waits=[(s_cv, cva_marks[(ch, 1)]), (s_n2, s_n2.n)], incs=[(s_n2, 1)], post_drain=True)
                    def mk_n2(n=n):
                        def f(e):
                            ins = None
                            for pt in range(DB):
                                ins = e.matmul(psg[64:80, 0:128], onesB[:, pt, :], ksq[:, pt, :],
                                               start=(pt == 0), stop=(pt == DB - 1),
                                               tile_position=(0, 64))
                            return ins
                        return f
                    plan.emit("tensor", mk_n2(),
                              waits=[(s_n2, s_n2.n), (s_gz, GZ_E), (s_pcld, PCLD)],
                              incs=[(s_n2, 1)])
                    plan.emit("vector", (lambda n=n: lambda e: e.tensor_copy(
                        sl("lnrho", n * 128, (n + 1) * 128), psg[64:80, 0:128]))(),
                        waits=[(s_n2, s_n2.n)], incs=[(s_n2, 1)])
            PRJ_CV = s_cv.n
            N2_E = s_n2.n

            def sp_compose(zslot, oslot, w0):
                plan.emit("scalar", (lambda zslot=zslot: lambda e: e.activation(sl("u"), sl(zslot), AF.Abs))(), waits=w0)
                plan.emit("scalar", (lambda sc=ccol16(-1.0, SL["l"][1]): lambda e: e.activation(sl("l"), sl("u"), AF.Exp, scale=sc))())
                plan.emit("scalar", (lambda bi=ccol16(1.0, SL["l"][1]): lambda e: e.activation(sl("l"), sl("l"), AF.Ln, bias=bi))())
                plan.emit("scalar", (lambda zslot=zslot, sc=ccol16(-1.0, SL["rn"][1]): lambda e: e.activation(sl("rn"), sl(zslot), AF.Relu, scale=sc))(),
                          incs=[(s_stt, 1)])
                plan.emit("vector", (lambda oslot=oslot: lambda e: e.scalar_tensor_tensor(
                    sl(oslot), sl("rn"), -1.0, sl("l"), ALU.mult, ALU.subtract))(),
                    waits=[(s_stt, s_stt.n)], incs=[(s_stt, 1)])
            sp_compose("zg", "g", [(s_gz, GZ_E)])
            sp_compose("zb", "lnb", [(s_stt, s_stt.n)])
            plan.emit("vector", (lambda: lambda e: e.tensor_scalar(sl("lnrho"), sl("lnrho"), 1e-24, None, ALU.max))(),
                      waits=[(s_n2, N2_E), (s_stt, s_stt.n)], incs=[(s_stt, 1)], post_drain=True)
            plan.emit("scalar", (lambda: lambda e: e.activation(sl("lnrho"), sl("lnrho"), AF.Ln))(),
                      waits=[(s_stt, s_stt.n)], incs=[(s_stt, 1)], post_drain=True)
            plan.emit("vector", (lambda: lambda e: e.tensor_scalar(sl("lnrho"), sl("lnrho"), -0.5, None, ALU.mult))(),
                      waits=[(s_stt, s_stt.n)], incs=[(s_stt, 1)], post_drain=True)
            plan.emit("vector", (lambda: lambda e: e.tensor_tensor(sl("e1"), sl("lnb"), sl("lnrho"), ALU.add))(),
                      waits=[(s_stt, s_stt.n)], incs=[(s_stt, 1)], post_drain=True)
            plan.emit("vector", (lambda: lambda e: e.memset(sl("ones"), 1.0))(), waits=[(s_stt, s_stt.n)], incs=[(s_stt, 1)])
            for n in range(CPS):
                c0, c1 = n * 128, (n + 1) * 128
                plan.emit("vector", (lambda c0=c0, c1=c1: lambda e: e.tensor_tensor_scan(
                    sl("c", c0, c1), sl("ones", c0, c1), sl("g", c0, c1), 0.0, ALU.mult, ALU.add))(),
                    waits=[(s_stt, s_stt.n)], incs=[(s_stt, 1)], post_drain=True)
                plan.emit("vector", (lambda c0=c0, c1=c1: lambda e: e.tensor_tensor_scan(
                    sl("cB", c0, c1), sl("ones", c0, c1), sl("g", c0, c1), 0.0, ALU.mult, ALU.add))(),
                    incs=[(s_stt, 1)], post_drain=True)
                plan.emit("vector", (lambda c0=c0: lambda e: e.memset(sl("cprev", c0, c0 + 1), 0.0))(),
                          incs=[(s_stt, 1)])
                plan.emit("vector", (lambda c0=c0, c1=c1: lambda e: e.tensor_copy(
                    sl("cprev", c0 + 1, c1), sl("c", c0, c1 - 1)))(),
                    incs=[(s_stt, 1)], post_drain=True)
                plan.emit("vector", (lambda c0=c0, c1=c1: lambda e: e.tensor_tensor(
                    sl("e1", c0, c1), sl("e1", c0, c1), sl("cB", c0, c1), ALU.subtract))(),
                    incs=[(s_stt, 1)], post_drain=True)
                plan.emit("vector", (lambda c0=c0, c1=c1: lambda e: e.tensor_scalar(
                    sl("eps", c0, c1), sl("e1", c0, c1), sl("cB", c1 - 1, c1), None, ALU.add))(),
                    incs=[(s_stt, 1)], post_drain=True)
            plan.emit("vector", (lambda: lambda e: e.tensor_tensor(sl("edr"), sl("cprev"), sl("lnrho"), ALU.add))(),
                      waits=[(s_stt, s_stt.n)], incs=[(s_stt, 1)], post_drain=True)
            plan.emit("sync", (lambda: lambda e: e.dma_start(out=rowscrF[0], in_=sl("edr")))(),
                      waits=[(s_stt, s_stt.n)], incs=[(s_row, 16)])
            plan.emit("sync", (lambda: lambda e: e.dma_start(out=rowscrF[1], in_=sl("e1")))(),
                      incs=[(s_row, 16)])
            plan.emit("sync", (lambda: lambda e: e.dma_start(out=rowscrF[2], in_=sl("c")))(),
                      incs=[(s_row, 16)])
            STT_E = s_stt.n
            ROW_E = s_row.n
            prj.close()
            sub1.close()

            if PC_STAGE < 5:
                prj.close() if False else None
                qkvs.close()
                pcs.close()
                continue
            dl = ExitStack()
            knat = dl.enter_context(nc.sbuf_tensor(f"knat{s}", [128, DB, 128], BF16))
            rhsb = dl.enter_context(nc.sbuf_tensor(f"rhsb{s}", [128, NH, 128], BF16))
            kepb = dl.enter_context(nc.sbuf_tensor(f"kepb{s}", [128, NH, 64], BF16))
            rowb = dl.enter_context(nc.sbuf_tensor(f"rowb{s}", [1, 3, SLAB], F32))
            rowbE = dl.enter_context(nc.sbuf_tensor(f"rowbE{s}", [1, SLAB], BF16))
            Earg = dl.enter_context(nc.sbuf_tensor(f"Earg{s}", [128, 2, 128], F32))
            statT = dl.enter_context(nc.sbuf_tensor(f"statT{s}", [128, 64], F32))
            Edm = dl.enter_context(nc.sbuf_tensor(f"Edm{s}", [128, 2, 128], F32))
            DREP = dl.enter_context(nc.sbuf_tensor(f"DREP{s}", [128, 128], F32))
            Abuf = dl.enter_context(nc.sbuf_tensor(f"Abuf{s}", [128, 128], BF16))
            Mbuf = dl.enter_context(nc.sbuf_tensor(f"Mbuf{s}", [128, 128], BF16))
            Tbuf = dl.enter_context(nc.sbuf_tensor(f"Tbuf{s}", [128, 2, 128], BF16))
            Dbuf = dl.enter_context(nc.sbuf_tensor(f"Dbuf{s}", [128, 128], BF16))
            DTbuf = dl.enter_context(nc.sbuf_tensor(f"DTbuf{s}", [128, 128], BF16))
            negAT = dl.enter_context(nc.sbuf_tensor(f"negAT{s}", [128, 128], BF16))
            Bsb = dl.enter_context(nc.sbuf_tensor(f"Bsb{s}", [128, 128], BF16))
            Pbuf = dl.enter_context(nc.sbuf_tensor(f"Pbuf{s}", [128, 2, 256], BF16))
            MTb = dl.enter_context(nc.sbuf_tensor(f"MTb{s}", [128, 128], BF16))
            TRH = dl.enter_context(nc.sbuf_tensor(f"TRH{s}", [128, 128], BF16))
            nMWT = dl.enter_context(nc.sbuf_tensor(f"nMWT{s}", [128, 128], BF16))
            PPT = dl.enter_context(nc.sbuf_tensor(f"PPT{s}", [128, 64], BF16))
            SQb = dl.enter_context(nc.sbuf_tensor(f"SQb{s}", [128, 64], BF16))
            otmp = dl.enter_context(nc.sbuf_tensor(f"otmp{s}", [128, 128], F32))
            ostg = dl.enter_context(nc.sbuf_tensor(f"ostg{s}", [128, DB, 128], BF16))
            psA = dl.enter_context(nc.psum_tensor(f"psA{s}", [128, 512], F32))
            psB = dl.enter_context(nc.psum_tensor(f"psB{s}", [128, 1024], BF16))
            psC = dl.enter_context(nc.psum_tensor(f"psC{s}", [128, 512], F32))
            psD = dl.enter_context(nc.psum_tensor(f"psD{s}", [128, 512], F32))
            psE = dl.enter_context(nc.psum_tensor(f"psE{s}", [128, 512], F32))

            if debug and s == 0:
                dbg_outs.setdefault("qc_d", nc.dram_tensor("qc_d", [128, DB, SLAB], BF16, kind="ExternalOutput").ap())
                dbg_outs.setdefault("kc_d", nc.dram_tensor("kc_d", [128, DB, SLAB], BF16, kind="ExternalOutput").ap())
                dbg_outs.setdefault("vc_d", nc.dram_tensor("vc_d", [128, DB, SLAB], BF16, kind="ExternalOutput").ap())
                plan.emit("gpsimd", lambda e: e.dma_start(out=dbg_outs["qc_d"], in_=qc[:]), waits=[(s_cv, PRJ_CV)], incs=[(s_dbgx, 16)])
                plan.emit("gpsimd", lambda e: e.dma_start(out=dbg_outs["kc_d"], in_=kc[:]), waits=[(s_cv, PRJ_CV)], incs=[(s_dbgx, 16)])
                plan.emit("gpsimd", lambda e: e.dma_start(out=dbg_outs["vc_d"], in_=vc[:]), waits=[(s_cv, PRJ_CV)], incs=[(s_dbgx, 16)])
            if debug and s == 0:
                dbgA_sb = dl.enter_context(nc.sbuf_tensor("dbgA_sb", [128, NH, 128], BF16))
                dbgT_sb = dl.enter_context(nc.sbuf_tensor("dbgT_sb", [128, NH, 128], BF16))
                dbg_outs.setdefault("A_d", nc.dram_tensor("A_d", [128, NH, 128], BF16, kind="ExternalOutput").ap())
                dbg_outs.setdefault("T_d", nc.dram_tensor("T_d", [128, NH, 128], BF16, kind="ExternalOutput").ap())
            ot_mark = 0
            for n in range(CPS):
                gn = s * CPS + n
                c0 = n * 128
                plan.emit("scalar", (lambda n=n: lambda e: e.dma_start(
                    out=rowb[0:1, :, :].rearrange("o q (h t) -> o q h t", h=16),
                    in_=rowscrF[None, :, :, n * 128:(n + 1) * 128]))(),
                    waits=[(s_row, ROW_E), (s_dle, s_dle.n)], incs=[(s_row, 16)])
                plan.emit("scalar", (lambda: lambda e: e.activation(rowbE[0:1, :], rowb[0:1, 2, :], AF.Exp))(),
                          waits=[(s_row, s_row.n)], incs=[(s_row, 1)], post_drain=True)
                ROWN = s_row.n
                if debug and s == 0 and n == DBGN:
                    dbg_outs.setdefault("rowb_d", nc.dram_tensor("rowb_d", [1, 3, SLAB], F32, kind="ExternalOutput").ap())
                    dbg_outs.setdefault("rowbE_d", nc.dram_tensor("rowbE_d", [1, SLAB], BF16, kind="ExternalOutput").ap())
                    plan.emit("gpsimd", lambda e: e.dma_start(out=dbg_outs["rowb_d"], in_=rowb[:]),
                              waits=[(s_row, ROWN)], incs=[(s_dbgx, 16)])
                    plan.emit("gpsimd", lambda e: e.dma_start(out=dbg_outs["rowbE_d"], in_=rowbE[:]),
                              incs=[(s_dbgx, 16)])
                def mk_st2(c0=c0):
                    def f(e):
                        ins = None
                        for qi, qn in enumerate(["cprev", "lnrho", "c", "eps"]):
                            b = SL[qn][1]
                            ins = e.transpose(psA[:, qi * 16:(qi + 1) * 16], sl(qn, c0, c0 + 128),
                                              if32[b:b + 16, b:b + 16])
                        return ins
                    return f
                plan.emit("tensor", mk_st2(),
                          waits=[(s_stt, STT_E), (s_pcld, PCLD), (s_dle, s_dle.n)], incs=[(s_dl, 1)])
                plan.emit("vector", (lambda: lambda e: e.tensor_copy(statT[:, 0:64], psA[:, 0:64]))(),
                          waits=[(s_dl, s_dl.n)], incs=[(s_dl, 1)], post_drain=True)
                plan.emit("vector", (lambda: lambda e: e.tensor_tensor(statT[:, 0:16], statT[:, 0:16], statT[:, 16:32], ALU.add))(),
                          incs=[(s_dl, 1)], post_drain=True)
                plan.emit("scalar", (lambda: lambda e: e.activation(statT[:, 0:64], statT[:, 0:64], AF.Exp))(),
                          waits=[(s_dl, s_dl.n)], incs=[(s_dl, 1)], post_drain=True)
                plan.emit("vector", (lambda: lambda e: e.tensor_scalar(statT[:, 16:32], statT[:, 32:48], 0.125, None, ALU.mult))(),
                          waits=[(s_dl, s_dl.n)], incs=[(s_dl, 1)], post_drain=True)
                STCH = s_dl.n
                def mk_kn1(c0=c0):
                    def f(e):
                        ins = None
                        for pt in range(DB):
                            ins = e.transpose(psB[:, pt * 128:(pt + 1) * 128], kc[:, pt, c0:c0 + 128], id_sb[:])
                        return ins
                    return f
                plan.emit("tensor", mk_kn1(), waits=[(s_cv, PRJ_CV), (s_dle, s_dle.n)], incs=[(s_dl, 1)])
                plan.emit("vector", (lambda: lambda e: e.tensor_copy(knat[:], psB[:].rearrange("p (a b) -> p a b", a=DB)))(),
                          waits=[(s_dl, s_dl.n)], incs=[(s_dle, 1)])
                def mk_kn2(c0=c0):
                    def f(e):
                        ins = None
                        for pt in range(DB):
                            ins = e.transpose(psB[:, pt * 128:(pt + 1) * 128], vc[:, pt, c0:c0 + 128], id_sb[:])
                    
                        return ins
                    return f
                plan.emit("tensor", mk_kn2(), waits=[(s_dle, s_dle.n)], incs=[(s_dl, 1)])
                plan.emit("vector", (lambda: lambda e: e.tensor_copy(
                    rhsb[:, :, 64:128], psB[:].rearrange("p (h x) -> p h x", h=NH)))(),
                    waits=[(s_dl, s_dl.n)], incs=[(s_dle, 1)])
                for h in range(NH):
                    pt, b64 = h // 2, (h % 2) * 64
                    plan.emit("vector", (lambda h=h, pt=pt, b64=b64: lambda e: e.tensor_scalar(
                        rhsb[:, h, 0:64], knat[:, pt, b64:b64 + 64], statT[:, h:h + 1], None, ALU.mult))(),
                        waits=[(s_dle, s_dle.n), (s_dl, STCH)], incs=[(s_dle, 1)])
                    plan.emit("vector", (lambda h=h, pt=pt, b64=b64: lambda e: e.tensor_scalar(
                        kepb[:, h, :], knat[:, pt, b64:b64 + 64], statT[:, 48 + h:49 + h], None, ALU.mult))(),
                        incs=[(s_dle, 1)])
                RH_E = s_dle.n

                for h in range(NH):
                    pt, q64 = h // 2, (h % 2) * 64
                    tpK = (q64, q64)
                    tpM = (0, q64)
                    kTc = kc[q64:q64 + 64, pt, c0:c0 + 128]
                    qTc = qc[q64:q64 + 64, pt, c0:c0 + 128]
                    def mk_s1(h=h, kTc=kTc, qTc=qTc, q64=q64):
                        def f(e):
                            e.matmul(psA[:, 0:128], kTc, kTc, start=True, stop=True)
                            e.matmul(psA[:, 128:256], qTc, kTc, start=True, stop=True)
                            e.matmul(psA[:, 256:384], ones1_f[:], rowb[0:1, 1, h * 128:(h + 1) * 128], start=True, stop=False)
                            e.matmul(psA[:, 256:384], rowb[0:1, 0, h * 128:(h + 1) * 128], ones1_f[:], start=False, stop=True)
                            e.matmul(psA[:, 384:512], ones1_f[:], rowb[0:1, 1, h * 128:(h + 1) * 128], start=True, stop=False)
                            e.matmul(psA[:, 384:512], rowb[0:1, 2, h * 128:(h + 1) * 128], ones1_f[:], start=False, stop=True)
                            return e.matmul(psE[q64:q64 + 64, 384:512], ones1b[0:1, 0:64], rowbE[0:1, h * 128:(h + 1) * 128],
                                            start=True, stop=True, tile_position=(0, q64))
                        return f
                    plan.emit("tensor", mk_s1(),
                              waits=[(s_dle, RH_E), (s_row, ROWN), (s_dle, s_dle.n)], incs=[(s_dl, 1)])
                    S1 = s_dl.n
                    plan.emit("vector", (lambda: lambda e: e.tensor_tensor(Edm[:, 0, :], psA[:, 0:128], maskS[:], ALU.mult))(),
                              waits=[(s_dl, S1)], incs=[(s_dle, 1)])
                    plan.emit("vector", (lambda: lambda e: e.tensor_tensor(Edm[:, 1, :], psA[:, 128:256], maskI[:], ALU.mult))(),
                              incs=[(s_dle, 1)])
                    plan.emit("vector", (lambda q64=q64: lambda e: e.tensor_copy(DREP[q64:q64 + 64, :], psE[q64:q64 + 64, 384:512]))(),
                              incs=[(s_dle, 1)])
                    plan.emit("vector", (lambda: lambda e: e.tensor_scalar(Earg[:, 0, :], psA[:, 256:384], 60.0, None, ALU.min))(),
                              incs=[(s_dle, 1)])
                    plan.emit("vector", (lambda: lambda e: e.tensor_scalar(Earg[:, 1, :], psA[:, 384:512], 60.0, None, ALU.min))(),
                              incs=[(s_dle, 1)], post_drain=True)
                    plan.emit("scalar", (lambda: lambda e: e.activation(Earg[:, 0, :], Earg[:, 0, :], AF.Exp))(),
                              waits=[(s_dle, s_dle.n)], incs=[(s_dle, 1)], post_drain=True)
                    plan.emit("scalar", (lambda bi=ccol(float(np.log(0.125))): lambda e: e.activation(
                        Earg[:, 1, :], Earg[:, 1, :], AF.Exp, bias=bi))(),
                        incs=[(s_dle, 1)], post_drain=True)
                    plan.emit("vector", (lambda: lambda e: e.tensor_tensor(Abuf[:], Edm[:, 0, :], Earg[:, 0, :], ALU.mult))(),
                              waits=[(s_dle, s_dle.n)], incs=[(s_dle, 1)])
                    plan.emit("vector", (lambda: lambda e: e.tensor_tensor(Mbuf[:], Edm[:, 1, :], Earg[:, 1, :], ALU.mult))(),
                              incs=[(s_dle, 1)])
                    EV1 = s_dle.n
                    def mk_tr(h=h):
                        def f(e):
                            e.transpose(psB[:, 0:128], Abuf[:], id_sb[:])
                            return e.transpose(psB[:, 128:256], Mbuf[:], id_sb[:])
                        return f
                    if debug and s == 0 and n == DBGN:
                        plan.emit("vector", (lambda h=h: lambda e: e.tensor_copy(dbgA_sb[:, h, :], Abuf[:]))(),
                                  waits=[(s_dle, EV1)], incs=[(s_dle, 1)])
                        EV1 = s_dle.n
                    plan.emit("tensor", mk_tr(), waits=[(s_dle, EV1)], incs=[(s_dl, 1)])
                    plan.emit("vector", (lambda: lambda e: e.tensor_copy(MTb[:], psB[:, 128:256]))(),
                              waits=[(s_dl, s_dl.n)], incs=[(s_dle, 1)])
                    plan.emit("vector", (lambda: lambda e: e.tensor_scalar(negAT[:], psB[:, 0:128], -1.0, None, ALU.mult))(),
                              incs=[(s_dle, 1)])
                    plan.emit("vector", (lambda: lambda e: e.tensor_tensor(DTbuf[:], psB[:, 0:128], maskBDU[:], ALU.mult))(),
                              incs=[(s_dle, 1)])
                    plan.emit("vector", (lambda: lambda e: e.tensor_tensor(Dbuf[:], Abuf[:], maskBD[:], ALU.mult))(),
                              incs=[(s_dle, 1)], post_drain=True)
                    for lv in range(4):
                        cur, nxt = lv % 2, (lv + 1) % 2
                        def mk_pw(cur=cur, lv=lv):
                            def f(e):
                                pa = Dbuf[:] if lv == 0 else Pbuf[:, cur, 0:128]
                                na = DTbuf[:] if lv == 0 else Pbuf[:, cur, 128:256]
                                e.matmul(psC[:, 0:128], na, pa, start=True, stop=True)
                                return e.matmul(psC[:, 128:256], pa, na, start=True, stop=True)
                            return f
                        plan.emit("tensor", mk_pw(), waits=[(s_dle, s_dle.n)], incs=[(s_dl, 1)])
                        plan.emit("vector", (lambda nxt=nxt: lambda e: e.tensor_copy(Pbuf[:, nxt, :], psC[:, 0:256]))(),
                                  waits=[(s_dl, s_dl.n)], incs=[(s_dle, 1)])
                        if lv == 0:
                            plan.emit("vector", (lambda: lambda e: e.scalar_tensor_tensor(
                                Tbuf[:, 0, :], DTbuf[:], -1.0, id_sb[:], ALU.mult, ALU.add))(),
                                incs=[(s_dle, 1)], post_drain=True)
                        def mk_ch(cur=cur, nxt=nxt):
                            def f(e):
                                e.matmul(psD[:, 0:128], id_sb[:], Tbuf[:, cur, :], start=True, stop=False)
                                return e.matmul(psD[:, 0:128], Pbuf[:, nxt, 0:128], Tbuf[:, cur, :], start=False, stop=True)
                            return f
                        plan.emit("tensor", mk_ch(), waits=[(s_dle, s_dle.n)], incs=[(s_dl, 1)])
                        plan.emit("vector", (lambda nxt=nxt: lambda e: e.tensor_copy(Tbuf[:, nxt, :], psD[:, 0:128]))(),
                                  waits=[(s_dl, s_dl.n)], incs=[(s_dle, 1)])
                    fin = 4 % 2
                    def mk_binit(h=h):
                        def f(e):
                            return e.matmul(psE[:, 0:128], id_sb[:], rhsb[:, h, :], start=True, stop=False)
                        return f
                    plan.emit("tensor", mk_binit(), waits=[(s_dle, s_dle.n)], incs=[(s_dl, 1)])
                    for p in range(4):
                        r0 = p * 32
                        if p == 0:
                            plan.emit("vector", (lambda h=h: lambda e: e.tensor_copy(Bsb[0:32, :], rhsb[0:32, h, :]))(),
                                      waits=[(s_dle, s_dle.n), (s_dl, s_dl.n)], incs=[(s_dle, 1)], post_drain=True)
                        else:
                            plan.emit("vector", (lambda r0=r0: lambda e: e.tensor_copy(Bsb[r0:r0 + 32, :], psE[r0:r0 + 32, 0:128]))(),
                                      waits=[(s_dl, s_dl.n)], incs=[(s_dle, 1)], post_drain=True)
                        def mk_u(p=p, r0=r0, fin=fin):
                            def f(e):
                                return e.matmul(psD[r0:r0 + 32, 128:256], Tbuf[r0:r0 + 32, fin, r0:r0 + 32],
                                                Bsb[r0:r0 + 32, :], start=True, stop=True,
                                                tile_position=(r0, r0))
                            return f
                        plan.emit("tensor", mk_u(), waits=[(s_dle, s_dle.n)], incs=[(s_dl, 1)])
                        plan.emit("vector", (lambda r0=r0: lambda e: e.tensor_copy(TRH[r0:r0 + 32, :], psD[r0:r0 + 32, 128:256]))(),
                                  waits=[(s_dl, s_dl.n)], incs=[(s_dle, 1)], post_drain=True)
                        if p < 3:
                            def mk_corr(p=p, r0=r0):
                                def f(e):
                                    r = None
                                    for pd in range(p + 1, 4):
                                        c0 = pd * 32
                                        r = e.matmul(psE[c0:c0 + 32, 0:128], negAT[r0:r0 + 32, c0:c0 + 32],
                                                     TRH[r0:r0 + 32, :], start=False, stop=(pd == p + 1),
                                                     tile_position=(r0, c0))
                                    return r
                                return f
                            plan.emit("tensor", mk_corr(), waits=[(s_dle, s_dle.n)], incs=[(s_dl, 1)])
                    if debug and s == 0 and n == DBGN:
                        plan.emit("vector", (lambda h=h: lambda e: e.tensor_copy(dbgT_sb[:, h, :], TRH[:]))(),
                                  incs=[(s_dle, 1)])
                    def mk_sp(h=h, q64=q64):
                        def f(e):
                            e.matmul(psE[q64:q64 + 64, 128:192], TRH[:, 0:64], kepb[:, h, :], start=True, stop=True, tile_position=(0, q64))
                            e.matmul(psE[q64:q64 + 64, 192:256], kepb[:, h, :], TRH[:, 64:128], start=True, stop=True, tile_position=(0, q64))
                            return e.matmul(psE[q64:q64 + 64, 256:384], TRH[:, 0:64], MTb[:], start=True, stop=True, tile_position=(0, q64))
                        return f
                    plan.emit("tensor", mk_sp(), waits=[(s_dle, s_dle.n)], incs=[(s_dl, 1)])
                    SP1 = s_dl.n
                    plan.emit("vector", (lambda q64=q64: lambda e: e.tensor_scalar(
                        nMWT[q64:q64 + 64, :], psE[q64:q64 + 64, 256:384], -1.0, None, ALU.mult))(),
                        waits=[(s_dl, SP1)], incs=[(s_dle, 1)])
                    plan.emit("vector", (lambda q64=q64: lambda e: e.scalar_tensor_tensor(
                        PPT[q64:q64 + 64, :], if32[q64:q64 + 64, q64:q64 + 64], DREP[q64:q64 + 64, 127:128],
                        psE[q64:q64 + 64, 128:192], ALU.mult, ALU.subtract))(),
                        incs=[(s_dle, 1)])
                    plan.emit("vector", (lambda q64=q64: lambda e: e.tensor_copy(SQb[q64:q64 + 64, :], psE[q64:q64 + 64, 192:256]))(),
                              incs=[(s_dle, 1)])
                    EV2 = s_dle.n
                    def mk_o(h=h, pt=pt, q64=q64, qTc=qTc):
                        def f(e):
                            e.matmul(psD[q64:q64 + 64, 256:384], S_sb[q64:q64 + 64, pt, :], qTc, start=True, stop=True, tile_position=(q64, q64))
                            e.matmul(psD[q64:q64 + 64, 384:512], TRH[:, 64:128], MTb[:], start=True, stop=False, tile_position=(0, q64))
                            return e.matmul(psD[q64:q64 + 64, 384:512], S_sb[q64:q64 + 64, pt, :], nMWT[q64:q64 + 64, :],
                                            start=False, stop=True, tile_position=(q64, q64))
                        return f
                    plan.emit("tensor", mk_o(), waits=[(s_dle, EV2)], incs=[(s_dl, 1)])
                    O1 = s_dl.n
                    def mk_su(h=h, pt=pt, q64=q64):
                        def f(e):
                            e.matmul(psC[q64:q64 + 64, 256:320], PPT[q64:q64 + 64, :], S_sb[q64:q64 + 64, pt, :],
                                     start=True, stop=False, tile_position=(q64, q64))
                            return e.matmul(psC[q64:q64 + 64, 256:320], id_sb[q64:q64 + 64, q64:q64 + 64], SQb[q64:q64 + 64, :],
                                            start=False, stop=True, tile_position=(q64, q64))
                        return f
                    plan.emit("tensor", mk_su(), waits=[(s_dl, O1)], incs=[(s_dl, 1)])
                    SU1 = s_dl.n
                    plan.emit("vector", (lambda q64=q64: lambda e: e.tensor_tensor(
                        otmp[q64:q64 + 64, :], psD[q64:q64 + 64, 256:384], DREP[q64:q64 + 64, :], ALU.mult))(),
                        waits=[(s_dl, SU1)] + ([(s_ot, ot_mark)] if h == 0 else []), incs=[(s_dle, 1)])
                    plan.emit("vector", (lambda pt=pt, q64=q64: lambda e: e.scalar_tensor_tensor(
                        ostg[q64:q64 + 64, pt, :], otmp[q64:q64 + 64, :], 0.125, psD[q64:q64 + 64, 384:512],
                        ALU.mult, ALU.add))(),
                        incs=[(s_dle, 1)])
                    plan.emit("vector", (lambda pt=pt, q64=q64: lambda e: e.tensor_copy(
                        S_sb[q64:q64 + 64, pt, :], psC[q64:q64 + 64, 256:320]))(),
                        incs=[(s_dle, 1)])
                plan.emit("sync", (lambda gn=gn: lambda e: e.dma_start(
                    out=oT_hbm[:, :, gn * 128:(gn + 1) * 128], in_=ostg[:]))(),
                    waits=[(s_dle, s_dle.n)], incs=[(s_ot, 16)])

                ot_mark = s_ot.n
                if debug and s == 0 and n == DBGN:
                    plan.emit("gpsimd", lambda e: e.dma_start(out=dbg_outs["A_d"], in_=dbgA_sb[:]),
                              waits=[(s_dle, s_dle.n)], incs=[(s_dbgx, 16)])
                    plan.emit("gpsimd", lambda e: e.dma_start(out=dbg_outs["T_d"], in_=dbgT_sb[:]),
                              incs=[(s_dbgx, 16)])

            pin_scr = dscr(f"pin{s}_scr", (1, 16), F32)
            plan.emit("gpsimd", (lambda pin_scr=pin_scr: lambda e: e.dma_start(out=pin_scr, in_=rowb[0:1, 0, 0:16]))(),
                      waits=[(s_dle, s_dle.n)], incs=[(s_ot, 16)])
            dl.close()
            qkvs.close()

            if PC_STAGE < 6:
                pcs.close()
                continue
            cmb = ExitStack()
            WoT = cmb.enter_context(nc.sbuf_tensor(f"WoT{s}", [128, DB, D], BF16))
            WoutT = cmb.enter_context(nc.sbuf_tensor(f"WoutT{s}", [128, DB, D], BF16))
            otile = cmb.enter_context(nc.sbuf_tensor(f"otile{s}", [128, 2, DB, 512], BF16))
            rtile = cmb.enter_context(nc.sbuf_tensor(f"rtile{s}", [128, 2, DB, 512], BF16))
            x2t = cmb.enter_context(nc.sbuf_tensor(f"x2t{s}", [128, 2, D], F32))
            x1c = cmb.enter_context(nc.sbuf_tensor(f"x1c{s}", [128, 2, D], F32))
            psX = [cmb.enter_context(nc.psum_tensor(f"psX{s}{k}", [128, 1024], F32)) for k in range(2)]
            plan.emit("scalar", lambda e: e.dma_start(out=WoT[:], in_=wT_hbm[3]), incs=[(s_cl, 16)])
            plan.emit("scalar", lambda e: e.dma_start(out=WoutT[:], in_=wT_hbm[4]), incs=[(s_cl, 16)])
            cm_marks = []
            for tb in range(NTB):
                gt = t0g + tb
                par = tb % 2
                qpar = (tb // 4) % 2
                if tb % 4 == 0:
                    tile_gate = [(s_cmb, cm_marks[tb - 5])] if tb >= 8 else []
                    plan.emit("scalar", (lambda s2=s, tb=tb, qpar=qpar: lambda e: e.dma_start(
                        out=otile[:, qpar, :, :], in_=oT_hbm[:, :, (s2 * NTB + tb) * 128:(s2 * NTB + tb + 4) * 128]))(),
                        waits=[(s_ot, ot_mark)] + tile_gate, incs=[(s_cl, 16)])
                    plan.emit("scalar", (lambda s2=s, tb=tb, qpar=qpar: lambda e: e.dma_start(
                        out=rtile[:, qpar, :, :], in_=retT_hbm[:, :, (s2 * NTB + tb) * 128:(s2 * NTB + tb + 4) * 128]))(),
                        waits=[(s_ffrd, RETD_E)], incs=[(s_cl, 16)])
                CL_T = s_cl.n
                plan.emit("sync", (lambda gt=gt, par=par: lambda e: e.dma_start(
                    out=x1c[:, par, :], in_=x1_hbm[gt * 128:(gt + 1) * 128, :]))(),
                    waits=([(s_cmb, cm_marks[-2])] if len(cm_marks) >= 2 else []), incs=[(s_xc, 16)])
                XL_T = s_xc.n
                def mk_cmb(tb=tb, par=par, qpar=qpar):
                    def f(e):
                        t4 = (tb % 4) * 128
                        e.matmul(psX[par][:, 0:512], ones1_f[:], fwoutb[:, 0:512], start=True, stop=False)
                        e.matmul(psX[par][:, 512:1024], ones1_f[:], fwoutb[:, 512:1024], start=True, stop=False)
                        ins = None
                        for cb in range(DB):
                            e.matmul(psX[par][:, 0:512], otile[:, qpar, cb, t4:t4 + 128], WoT[:, cb, 0:512], start=False, stop=False)
                            e.matmul(psX[par][:, 512:1024], otile[:, qpar, cb, t4:t4 + 128], WoT[:, cb, 512:1024], start=False, stop=False)
                            e.matmul(psX[par][:, 0:512], rtile[:, qpar, cb, t4:t4 + 128], WoutT[:, cb, 0:512], start=False, stop=(cb == DB - 1))
                            ins = e.matmul(psX[par][:, 512:1024], rtile[:, qpar, cb, t4:t4 + 128], WoutT[:, cb, 512:1024], start=False, stop=(cb == DB - 1))
                        return ins
                    return f
                ev_gate = [(s_cmb, cm_marks[-2])] if len(cm_marks) >= 2 else []
                plan.emit("tensor", mk_cmb(),
                          waits=[(s_cl, CL_T), (s_pcld, PCLD), (s_ms, 2)] + ev_gate,
                          incs=[(s_cmb, 1)])
                plan.emit("vector", (lambda tb=tb, par=par: lambda e: e.tensor_tensor(
                    x2t[:, par, :], psX[par][:], x1c[:, par, :], ALU.add))(),
                    waits=[(s_cmb, s_cmb.n), (s_xc, XL_T)] + ([(s_out, s_out.n - 16)] if s_out.n >= 16 else []),
                    incs=[(s_cmb, 1)])
                plan.emit("sync", (lambda gt=gt, par=par: lambda e: e.dma_start(
                    out=out[gt * 128:(gt + 1) * 128, :], in_=x2t[:, par, :]))(),
                    waits=[(s_cmb, s_cmb.n)], incs=[(s_out, 16)])
                cm_marks.append(s_cmb.n)
            cmb.close()
            pcs.close()
        pcc.close()
        fin_waits = [(s_wstd, s_wstd.n), (s_l1e, s_l1e.n) if PC_STAGE >= 2 else (s_wstd, s_wstd.n)]
        for sm in [s_ffrd, s_row, s_ot, s_out, s_pje, s_cv, s_n2, s_stt, s_dle, s_gz, s_cl, s_xc, s_cmb]:
            fin_waits.append((sm, sm.n))
        plan.emit("sync", lambda e: e.nop(), waits=fin_waits)
        pin_sb = st.enter_context(nc.sbuf_tensor("pin_sb", [1, 128], F32))
        plan.emit("gpsimd", lambda e: e.memset(pin_sb[:], 0.0), waits=list(fin_waits))

    if "pc" not in phases:
        zeros = st.enter_context(nc.sbuf_tensor("zeros_ph", [128, D], F32))
        s_z = plan.sem()
        s_fin = plan.sem()
        plan.emit("vector", lambda e: e.memset(zeros[:], 0.0), incs=[(s_z, 1)])
        for t in range(NT):
            plan.emit("sync", (lambda t=t: lambda e: e.dma_start(out=out[t * 128:(t + 1) * 128, :], in_=zeros[:]))(),
                      waits=[(s_z, 1)], incs=[(s_fin, 16)])
        plan.emit("sync", lambda e: e.nop(),
                  waits=[(s_fin, 16 * NT), (s_xntd, 16 * (NT // 4))]
                        + (pa_done if "pa" in phases else [])
                        + (pb_done if "pb" in phases else []))

    with nc.Block() as block:
        block.tensor(lambda e: plan.replay(e, "tensor"))
        block.vector(lambda e: plan.replay(e, "vector"))
        block.scalar(lambda e: plan.replay(e, "scalar"))
        block.sync(lambda e: plan.replay(e, "sync"))
        block.gpsimd(lambda e: plan.replay(e, "gpsimd"))

    st.close()
    return nc, di, dbg_outs


_IN_KEYMAP = {
    "rW": "L0_ffn_router_W", "rb": "L0_ffn_router_b",
    "sW1": "L0_ffn_sW1", "sb1": "L0_ffn_sb1", "sW2": "L0_ffn_sW2", "sb2": "L0_ffn_sb2",
    "eW1": "L0_ffn_eW1", "eb1": "L0_ffn_eb1", "eW2": "L0_ffn_eW2", "eb2": "L0_ffn_eb2",
    "aWq": "L1_attn_Wq", "aWk": "L1_attn_Wk", "aWv": "L1_attn_Wv", "aWo": "L1_attn_Wo",
    "aWg": "L1_attn_Wg", "aWb": "L1_attn_Wb",
    "cqw": "L1_attn_cq_w", "cqb": "L1_attn_cq_b", "ckw": "L1_attn_ck_w", "ckb": "L1_attn_ck_b",
    "cvw": "L1_attn_cv_w", "cvb": "L1_attn_cv_b",
    "fWq": "L1_ffn_Wq", "frouters": "L1_ffn_routers", "fleafv": "L1_ffn_leaf_values",
    "fWout": "L1_ffn_Wout_w", "fWoutb": "L1_ffn_Wout_b",
}


def kernel(**inputs):
    x_full = np.ascontiguousarray(np.asarray(inputs["L0_attn_x"], dtype=np.float32))
    B = x_full.shape[0]
    nc, di, dbg = build_nc()
    base = {k: np.ascontiguousarray(np.asarray(inputs[v], dtype=np.float32))
            for k, v in _IN_KEYMAP.items()}
    in_maps = []
    for b in range(B):
        m = dict(base)
        m["x"] = np.ascontiguousarray(x_full[b])
        in_maps.append(m)
    res = run_bass_kernel_spmd(nc, in_maps, core_ids=list(range(B)))
    return np.stack([r["out"] for r in res.results], axis=0)


# version 02 (final)
# speedup vs baseline: 1.0091x; 1.0091x over previous
import numpy as np
import ml_dtypes
from contextlib import ExitStack

import concourse.bass as bass
import concourse.mybir as mybir
from concourse.bass_utils import run_bass_kernel_spmd

F32 = mybir.dt.float32
BF16 = mybir.dt.bfloat16
AF = mybir.ActivationFunctionType
ALU = mybir.AluOpType
AX = mybir.AxisListType

T = 4096
D = 1024
NT = T // 128
DB = D // 128
EPS = 1e-5

ENGS = ("tensor", "vector", "scalar", "sync", "gpsimd")


class _Sem:
    __slots__ = ("h", "n", "name")

    def __init__(self, h, name):
        self.h, self.n, self.name = h, 0, name


class Plan:
    def __init__(self, nc, stack):
        self.nc = nc
        self.ops = {e: [] for e in ENGS}
        self.pending = {e: [] for e in ENGS}
        self._stack = stack
        self._semi = 0

    def sem(self, name=None):
        name = name or f"s{self._semi}"
        self._semi += 1
        h = self._stack.enter_context(self.nc.semaphore(name))
        return _Sem(h, name)

    def emit(self, eng, fn, waits=(), incs=(), post_drain=False):
        w = list(self.pending[eng]) + list(waits)
        self.pending[eng] = []
        for s, amt in incs:
            s.n += amt
        self.ops[eng].append((fn, w, list(incs), post_drain))

    def gate(self, conds, engs=ENGS):
        for e in engs:
            self.pending[e].extend(conds)

    def replay(self, eng_obj, name):
        hi = {}
        for fn, waits, incs, post_drain in self.ops[name]:
            for s, v in waits:
                if hi.get(s.name, -1) < v:
                    eng_obj.wait_ge(s.h, v)
                    hi[s.name] = v
            instr = fn(eng_obj)
            if post_drain:
                instr = eng_obj.drain()
            for s, amt in incs:
                assert instr is not None, f"inc with no instr on {name}"
                instr.then_inc(s.h, amt)


def _ap(t):
    return t.ap() if hasattr(t, "ap") else t


PC_STAGE = 9
DBGN = 0

def build_nc(debug=False, phases=("p0", "pa", "pb", "pc")):
    nc = bass.Bass("TRN2", target_bir_lowering=False)
    st = ExitStack()

    di = {}
    def din(name, shape):
        di[name] = nc.dram_tensor(name, list(shape), F32, kind="ExternalInput").ap()
    din("x", (T, D))
    din("rW", (D, 15)); din("rb", (15,))
    din("sW1", (D, D)); din("sb1", (D,)); din("sW2", (D, D)); din("sb2", (D,))
    din("eW1", (15, D, D)); din("eb1", (15, D)); din("eW2", (15, D, D)); din("eb2", (15, D))
    din("aWq", (D, D)); din("aWk", (D, D)); din("aWv", (D, D)); din("aWo", (D, D))
    din("aWg", (16, D)); din("aWb", (16, D))
    din("cqw", (D, 1, 4)); din("cqb", (D,)); din("ckw", (D, 1, 4)); din("ckb", (D,))
    din("cvw", (D, 1, 4)); din("cvb", (D,))
    din("fWq", (D, D)); din("frouters", (255, D)); din("fleafv", (256, D))
    din("fWout", (D, D)); din("fWoutb", (D,))

    out = nc.dram_tensor("out", [T, D], F32, kind="ExternalOutput").ap()

    def dscr(name, shape, dtype=F32, dbg=False):
        kind = "ExternalOutput" if (debug and dbg) else "Internal"
        return nc.dram_tensor(name, list(shape), dtype, kind=kind).ap()

    xnT_hbm = dscr("xnT_hbm", (DB, 128, T), BF16, dbg=True)
    attn0_hbm = dscr("attn0_hbm", (T, D), dbg=True)
    x1_hbm = dscr("x1_hbm", (T, D), dbg=True)

    dbg_outs = {}
    if debug:
        dbg_outs["xn_nat_d"] = nc.dram_tensor("xn_nat_d", [128, NT, D], BF16, kind="ExternalOutput").ap()
        dbg_outs["logpT_d"] = nc.dram_tensor("logpT_d", [128, DB, T], BF16, kind="ExternalOutput").ap()

    ident_h = nc.inline_tensor(np.eye(128, dtype=ml_dtypes.bfloat16), name="ident128")

    plan = Plan(nc, st)

    id_sb = st.enter_context(nc.sbuf_tensor("id_sb", [128, 128], BF16))
    onecol = st.enter_context(nc.sbuf_tensor("onecol", [128, 1], BF16))
    ones1_f = st.enter_context(nc.sbuf_tensor("ones1_f", [1, 128], F32))

    consts = st.enter_context(nc.sbuf_tensor("consts", [128, 16], F32))

    s_const = plan.sem("const")
    s_ms = plan.sem("ms")
    plan.emit("sync", lambda e: e.dma_start(out=id_sb[:], in_=ident_h.ap()), incs=[(s_const, 16)])
    plan.emit("vector", lambda e: e.memset(onecol[:], 1.0), incs=[(s_ms, 1)])
    plan.emit("vector", lambda e: e.memset(ones1_f[:], 1.0), incs=[(s_ms, 1)])

    _cc = {}
    def ccol(v):
        if v not in _cc:
            idx = len(_cc)
            assert idx < 16
            _cc[v] = consts[:, idx:idx + 1]
            plan.emit("vector", (lambda ap=_cc[v], v=v: lambda e: e.memset(ap, float(v)))(),
                      incs=[(s_ms, 1)])
        return _cc[v]

    _cci = {}
    def ccol16(v, b):
        ccol(v)
        if v not in _cci:
            for i, (kv, ap) in enumerate(_cc.items()):
                _cci[kv] = i
        idx = _cci[v]
        return consts[b:b + 16, idx:idx + 1]

    l0res = ExitStack()
    xn_nat = l0res.enter_context(nc.sbuf_tensor("xn_nat", [128, NT, D], BF16))
    logpT = l0res.enter_context(nc.sbuf_tensor("logpT", [128, DB, T], BF16))
    p0 = ExitStack()
    x_tile = p0.enter_context(nc.sbuf_tensor("x_tile", [128, 2, D], F32))
    xc = p0.enter_context(nc.sbuf_tensor("xc", [128, D], F32))
    sq = p0.enter_context(nc.sbuf_tensor("sq", [128, D], F32))
    logp_nat = p0.enter_context(nc.sbuf_tensor("logp_nat", [128, D], BF16))
    stt = p0.enter_context(nc.sbuf_tensor("stt", [128, 2, 8], F32))
    xnT_stage = p0.enter_context(nc.sbuf_tensor("xnT_stage", [128, DB, 512], BF16))
    psT = p0.enter_context(nc.psum_tensor("psT", [128, 1024], BF16))
    psT2 = p0.enter_context(nc.psum_tensor("psT2", [128, 1024], BF16))

    s_ldx = plan.sem()
    s_xc = plan.sem()
    s_mu = plan.sem()
    s_ln = plan.sem()
    s_lh = plan.sem()
    s_xn = plan.sem()
    s_m = plan.sem()
    s_l1 = plan.sem()
    s_nl = plan.sem()
    s_lp = plan.sem()
    s_tr = plan.sem()
    s_tr2 = plan.sem()
    s_tev = plan.sem()
    s_pt = plan.sem()
    s_lsd = plan.sem()
    s_xnt = plan.sem()
    s_xntd = plan.sem()

    for t in range(NT):
        par = t % 2
        xt = x_tile[:, par, :]
        sa = stt[:, par, :]
        plan.emit("sync", (lambda xt=xt, t=t: lambda e: e.dma_start(out=xt, in_=di["x"][t * 128:(t + 1) * 128, :]))(),
                  waits=([(s_xc, t - 1)] if t >= 2 else ([(s_ldx, 16)] if t == 1 else [])), incs=[(s_ldx, 16)])
        plan.emit("vector", (lambda xt=xt, sa=sa: lambda e: e.reduce_sum(sa[:, 0:1], xt, axis=AX.X))(),
                  waits=[(s_ldx, 16 * (t + 1))] + ([(s_lp, t - 1)] if t >= 2 else []), post_drain=True)
        plan.emit("vector", (lambda sa=sa: lambda e: e.tensor_scalar(sa[:, 1:2], sa[:, 0:1], -1.0 / D, None, ALU.mult))(),
                  incs=[(s_mu, 1)])
        plan.emit("scalar", (lambda xt=xt, sa=sa: lambda e: e.activation(xc[:], xt, AF.Identity, bias=sa[:, 1:2]))(),
                  waits=[(s_mu, t + 1)], incs=[(s_xc, 1)])
        plan.emit("scalar", (lambda sa=sa: lambda e: e.activation(sq[:], xc[:], AF.Square, accum_out=sa[:, 2:3]))(), post_drain=True)
        plan.emit("scalar", (lambda sa=sa, sc=ccol(1.0 / D), bi=ccol(EPS): lambda e: e.activation(sa[:, 3:4], sa[:, 2:3], AF.Ln, scale=sc, bias=bi))(),
                  waits=[(s_ms, s_ms.n)], incs=[(s_ln, 1)])
        plan.emit("vector", (lambda sa=sa: lambda e: e.tensor_scalar(sa[:, 4:5], sa[:, 3:4], -0.5, None, ALU.mult))(),
                  waits=[(s_ln, t + 1)], incs=[(s_lh, 1)])
        plan.emit("scalar", (lambda sa=sa: lambda e: e.activation(sa[:, 5:6], sa[:, 4:5], AF.Exp))(),
                  waits=[(s_lh, t + 1)])
        plan.emit("scalar", (lambda t=t, sa=sa: lambda e: e.activation(xn_nat[:, t, :], xc[:], AF.Identity, scale=sa[:, 5:6]))(),
                  incs=[(s_xn, 1)])
        plan.emit("vector", (lambda t=t, sa=sa: lambda e: e.reduce_max(sa[:, 6:7], xn_nat[:, t, :], axis=AX.X))(),
                  waits=[(s_xn, t + 1)], post_drain=True)
        plan.emit("vector", (lambda sa=sa: lambda e: e.tensor_scalar(sa[:, 7:8], sa[:, 6:7], -1.0, None, ALU.mult))(),
                  incs=[(s_m, 1)])
        plan.emit("scalar", (lambda t=t, sa=sa: lambda e: e.activation(sq[:], xn_nat[:, t, :], AF.Exp, bias=sa[:, 7:8], accum_out=sa[:, 0:1]))(),
                  waits=[(s_m, t + 1)], post_drain=True)
        plan.emit("scalar", (lambda sa=sa: lambda e: e.activation(sa[:, 1:2], sa[:, 0:1], AF.Ln))(),
                  incs=[(s_l1, 1)])
        plan.emit("vector", (lambda sa=sa: lambda e: e.scalar_tensor_tensor(sa[:, 2:3], sa[:, 6:7], -1.0, sa[:, 1:2], ALU.mult, ALU.subtract))(),
                  waits=[(s_l1, t + 1)], incs=[(s_nl, 1)])
        plan.emit("scalar", (lambda t=t, sa=sa: lambda e: e.activation(logp_nat[:], xn_nat[:, t, :], AF.Identity, bias=sa[:, 2:3]))(),
                  waits=[(s_nl, t + 1)], incs=[(s_lp, 1)])
        def mk_tr(t=t):
            def f(e):
                ins = None
                for d in range(DB):
                    ins = e.transpose(psT[:, d * 128:(d + 1) * 128], logp_nat[:, d * 128:(d + 1) * 128], id_sb[:])
                return ins
            return f
        plan.emit("tensor", mk_tr(),
                  waits=[(s_lp, t + 1), (s_tev, t), (s_const, 16)], incs=[(s_tr, 1)])
        plan.emit("vector", (lambda t=t: lambda e: e.tensor_copy(
            logpT[:, :, t * 128:(t + 1) * 128],
            psT[:].rearrange("p (a b) -> p a b", a=DB)))(),
            waits=[(s_tr, t + 1)], incs=[(s_tev, 1)])
        def mk_tr2(t=t):
            def f(e):
                ins = None
                for d in range(DB):
                    ins = e.transpose(psT2[:, d * 128:(d + 1) * 128], xn_nat[:, t, d * 128:(d + 1) * 128], id_sb[:])
                return ins
            return f
        plan.emit("tensor", mk_tr2(), waits=[(s_xn, t + 1), (s_xnt, t)], incs=[(s_tr2, 1)])
        plan.emit("vector", (lambda t=t: lambda e: e.tensor_copy(
            xnT_stage[:, :, (t % 4) * 128:(t % 4 + 1) * 128],
            psT2[:].rearrange("p (a b) -> p a b", a=DB)))(),
            waits=[(s_tr2, t + 1)] + ([(s_xntd, 16 * (t // 4))] if t >= 4 else []),
            incs=[(s_xnt, 1)])
        if t % 4 == 3:
            plan.emit("sync", (lambda t=t: lambda e: e.dma_start(
                out=xnT_hbm[:, :, (t - 3) * 128:(t + 1) * 128].rearrange("a p c -> p a c"),
                in_=xnT_stage[:]))(),
                waits=[(s_xnt, t + 1)], incs=[(s_xntd, 16)])

    if debug:
        s_dbg = plan.sem()
        plan.emit("gpsimd", lambda e: e.dma_start(out=dbg_outs["xn_nat_d"], in_=xn_nat[:]),
                  waits=[(s_xn, NT)], incs=[(s_dbg, 16)])
        plan.emit("gpsimd", lambda e: e.dma_start(out=dbg_outs["logpT_d"], in_=logpT[:]),
                  waits=[(s_tev, NT)], incs=[(s_dbg, 16)])

    if "pa" in phases:
        p0.close()
        pa = ExitStack()
        pT = pa.enter_context(nc.sbuf_tensor("pT", [128, DB, T], BF16))
        expst = pa.enter_context(nc.sbuf_tensor("expst", [128, 2, 128], BF16))
        attn_sb = pa.enter_context(nc.sbuf_tensor("attn_sb", [128, 2, D], F32))
        rs_sb = pa.enter_context(nc.sbuf_tensor("rs_sb", [128, 2], F32))
        psS = [pa.enter_context(nc.psum_tensor(f"psS{k}", [128, 512], F32)) for k in range(2)]
        psO = [pa.enter_context(nc.psum_tensor(f"psO{k}", [128, 1024], F32)) for k in range(2)]
        psR = [pa.enter_context(nc.psum_tensor(f"psR{k}", [128, 512], F32)) for k in range(2)]

        for d in range(DB):
            plan.emit("scalar", (lambda d=d: lambda e: e.activation(pT[:, d, :], logpT[:, d, :], AF.Exp))(),
                      waits=[(s_tev, NT), (s_xnt, NT), (s_xntd, 16 * (NT // 4))], incs=[(s_pt, 1)])

        s_sc = plan.sem()
        s_ex = plan.sem()
        s_og = plan.sem()
        s_ev = plan.sem()
        s_st = plan.sem()

        for i in range(NT):
            par = i % 2
            for j in range(NT):
                q = i * NT + j
                sp = q % 2
                def mk_sc(i=i, j=j, sp=sp):
                    def f(e):
                        ins = None
                        for d in range(DB):
                            ins = e.matmul(psS[sp][:, 0:128],
                                           logpT[:, d, j * 128:(j + 1) * 128],
                                           pT[:, d, i * 128:(i + 1) * 128],
                                           start=(d == 0), stop=(d == DB - 1))
                        return ins
                    return f
                plan.emit("tensor", mk_sc(),
                          waits=[(s_pt, DB)] + ([(s_ex, q - 1)] if q >= 2 else []),
                          incs=[(s_sc, 1)])
                plan.emit("scalar", (lambda q=q, sp=sp: lambda e: e.activation(
                    expst[:, sp, :], psS[sp][:, 0:128], AF.Exp))(),
                    waits=[(s_sc, q + 1)] + ([(s_og, q - 1)] if q >= 2 else []),
                    incs=[(s_ex, 1)], post_drain=True)
                def mk_o(i=i, j=j, sp=sp, par=par):
                    def f(e):
                        e.matmul(psO[par][:, 0:512], expst[:, sp, :], xn_nat[:, j, 0:512],
                                 start=(j == 0), stop=(j == NT - 1))
                        e.matmul(psO[par][:, 512:1024], expst[:, sp, :], xn_nat[:, j, 512:1024],
                                 start=(j == 0), stop=(j == NT - 1))
                        return e.matmul(psR[par][:, 0:1], expst[:, sp, :], onecol[:],
                                        start=(j == 0), stop=(j == NT - 1))
                    return f
                plan.emit("tensor", mk_o(),
                          waits=[(s_ex, q + 1), (s_ms, 2)] + ([(s_ev, i - 1)] if (j == 0 and i >= 2) else []),
                          incs=[(s_og, 1)])
            plan.emit("vector", (lambda par=par: lambda e: e.reciprocal(rs_sb[:, par:par + 1], psR[par][:, 0:1]))(),
                      waits=[(s_og, NT * (i + 1))] + ([(s_st, 16 * (i - 1))] if i >= 2 else []), post_drain=True)
            plan.emit("vector", (lambda par=par: lambda e: e.tensor_scalar(
                attn_sb[:, par, :], psO[par][:], rs_sb[:, par:par + 1], None, ALU.mult))(),
                incs=[(s_ev, 1)])
            plan.emit("sync", (lambda i=i, par=par: lambda e: e.dma_start(
                out=attn0_hbm[i * 128:(i + 1) * 128, :], in_=attn_sb[:, par, :]))(),
                waits=[(s_ev, i + 1)], incs=[(s_st, 16)])

        pa_done = [(s_st, 16 * NT)]
        pa.close()
    else:
        p0.close()
    l0res.close()

    SLAB = 2048
    NTB = SLAB // 128
    NCHK = SLAB // 512
    NU = 16
    if "pb" in phases:
        pb = ExitStack()
        xnT_sl = pb.enter_context(nc.sbuf_tensor("xnT_sl", [128, DB, SLAB], BF16))
        acc = pb.enter_context(nc.sbuf_tensor("acc", [128, NTB, D], F32))
        wbuf = pb.enter_context(nc.sbuf_tensor("wbuf", [128, 2, 2, DB, D], BF16))
        hbuf = pb.enter_context(nc.sbuf_tensor("hbuf", [128, DB, 3, 512], BF16))
        eb1col = pb.enter_context(nc.sbuf_tensor("eb1col", [128, 2, DB], F32))
        eb2row = pb.enter_context(nc.sbuf_tensor("eb2row", [1, 2, D], F32))
        cw_sb = pb.enter_context(nc.sbuf_tensor("cw_sb", [128, NTB, 16], F32))
        rWf = pb.enter_context(nc.sbuf_tensor("rWf", [128, DB, 15], F32))
        rWsb = pb.enter_context(nc.sbuf_tensor("rWsb", [128, DB, 15], BF16))
        rbrow = pb.enter_context(nc.sbuf_tensor("rbrow", [1, 15], F32))
        pe_sb = pb.enter_context(nc.sbuf_tensor("pe_sb", [128, 15], F32))
        pe2_sb = pb.enter_context(nc.sbuf_tensor("pe2_sb", [128, 15], F32))
        z_dbg = pb.enter_context(nc.sbuf_tensor("z_dbg", [128, 15], F32))
        z2_dbg = pb.enter_context(nc.sbuf_tensor("z2_dbg", [128, 53], F32))
        rst = pb.enter_context(nc.sbuf_tensor("rst", [128, 8], F32))
        ai_t = pb.enter_context(nc.sbuf_tensor("ai_t", [128, 1, 2, D], F32))
        psH = [pb.enter_context(nc.psum_tensor(f"psH{k}", [128, 512], F32)) for k in range(2)]
        psY = [pb.enter_context(nc.psum_tensor(f"psY{k}", [128, 1024], F32)) for k in range(2)]
        psZ = pb.enter_context(nc.psum_tensor("psZ", [128, 512], F32))

        s_xsl = plan.sem()
        s_ai = plan.sem()
        s_aid = plan.sem()
        s_cw = plan.sem()
        s_rt = plan.sem()
        s_zmm = plan.sem()
        s_zrd = plan.sem()
        s_wW = plan.sem()
        s_wB = plan.sem()
        s_wR = plan.sem()
        s_m1 = plan.sem()
        s_hev = plan.sem()
        s_m2 = plan.sem()
        s_yev = plan.sem()
        s_x1 = plan.sem()
        s_u = plan.sem()
        s_hdump = plan.sem()
        s_accd = plan.sem()

        plan.gate([(s_st, 16 * NT)] if "pa" in phases else [])

        plan.emit("scalar", lambda e: e.dma_start(out=rWf[:], in_=di["rW"].rearrange("(d p) e -> p d e", p=128),
                                                  allow_slow_non_contiguous=True), incs=[(s_wR, 16)])
        plan.emit("scalar", lambda e: e.dma_start(out=rbrow[:], in_=di["rb"][None, :]), incs=[(s_wR, 16)])
        plan.emit("vector", lambda e: e.tensor_copy(rWsb[:], rWf[:]), waits=[(s_wR, 32)], incs=[(s_wR, 16)])

        BIGN = -1e30
        uglob = 0
        for s in range(2):
            t0 = s * NTB
            plan.emit("sync", (lambda s=s: lambda e: e.dma_start(
                out=xnT_sl[:], in_=xnT_hbm[:, :, s * SLAB:(s + 1) * SLAB].rearrange("a p c -> p a c")))(),
                waits=[(s_xntd, 16 * (NT // 4))] + ([(s_u16, 16)] if s == 1 else []),
                incs=[(s_xsl, 16)])
            for tb in range(NTB):
                gt = t0 + tb
                par = 0
                w_ai = [(s_ai, (s * NTB + tb))] if (s * NTB + tb) >= 1 else []
                plan.emit("sync", (lambda gt=gt, par=par: lambda e: e.dma_start(
                    out=ai_t[:, par, 0, :], in_=di["x"][gt * 128:(gt + 1) * 128, :]))(),
                    waits=w_ai, incs=[(s_aid, 16)])
                plan.emit("sync", (lambda gt=gt, par=par: lambda e: e.dma_start(
                    out=ai_t[:, par, 1, :], in_=attn0_hbm[gt * 128:(gt + 1) * 128, :]))(),
                    waits=w_ai, incs=[(s_aid, 16)])
                plan.emit("vector", (lambda tb=tb, par=par: lambda e: e.tensor_tensor(
                    acc[:, tb, :], ai_t[:, par, 0, :], ai_t[:, par, 1, :], ALU.add))(),
                    waits=[(s_aid, 32 * (s * NTB + tb + 1)), (s_x1, 16 * NTB * s)], incs=[(s_ai, 1)])
            for tb in range(NTB):
                gtb = s * NTB + tb
                def mk_z(tb=tb, s=s):
                    def f(e):
                        for d in range(DB):
                            e.matmul(psZ[:, 0:15], xnT_sl[:, d, tb * 128:(tb + 1) * 128],
                                     rWsb[:, d, :], start=(d == 0), stop=False)
                        return e.matmul(psZ[:, 0:15], ones1_f[:], rbrow[:], start=False, stop=True)
                    return f
                plan.emit("tensor", mk_z(),
                          waits=[(s_wR, 48), (s_xsl, 16 * (s + 1)), (s_ms, 2)] + ([(s_zrd, gtb)] if gtb >= 1 else []),
                          incs=[(s_zmm, 1)])
                cw = cw_sb[:, tb, 0:15]
                if debug == "deep" and gtb == 0:
                    plan.emit("vector", (lambda: lambda e: e.tensor_copy(z_dbg[:], psZ[:, 0:15]))(),
                              waits=[(s_zmm, 1)], post_drain=True)
                plan.emit("vector", (lambda: lambda e: e.reduce_max(rst[:, 0:1], psZ[:, 0:15], axis=AX.X))(),
                          waits=[(s_zmm, gtb + 1)], post_drain=True)
                plan.emit("vector", (lambda: lambda e: e.tensor_scalar(rst[:, 1:2], rst[:, 0:1], -1.0, None, ALU.mult))(),
                          incs=[(s_rt, 1)], post_drain=True)
                plan.emit("scalar", (lambda: lambda e: e.activation(pe_sb[:], psZ[:, 0:15], AF.Exp, bias=rst[:, 1:2], accum_out=rst[:, 2:3]))(),
                          waits=[(s_rt, gtb + 1), (s_cw, gtb)], post_drain=True, incs=[(s_zrd, 1)])
                plan.emit("vector", (lambda: lambda e: e.reciprocal(rst[:, 3:4], rst[:, 2:3]))(),
                          waits=[(s_zrd, gtb + 1)], post_drain=True)
                plan.emit("vector", (lambda: lambda e: e.tensor_scalar(pe_sb[:], pe_sb[:], rst[:, 3:4], None, ALU.mult))(), post_drain=True)
                if debug == "deep" and gtb == 0:
                    plan.emit("vector", (lambda: lambda e: e.tensor_copy(z2_dbg[:, 0:15], pe_sb[:]))(), post_drain=True)
                    plan.emit("vector", (lambda: lambda e: e.tensor_copy(z2_dbg[:, 15:23], rst[:]))(), post_drain=True)
                plan.emit("vector", (lambda: lambda e: e.reduce_max(rst[:, 4:5], pe_sb[:], axis=AX.X))(), post_drain=True)
                plan.emit("vector", (lambda cw=cw: lambda e: e.tensor_scalar(cw, pe_sb[:], rst[:, 4:5], None, ALU.is_ge))(), post_drain=True)
                plan.emit("vector", (lambda cw=cw: lambda e: e.scalar_tensor_tensor(cw, cw, BIGN, pe_sb[:], ALU.mult, ALU.add))(), post_drain=True)
                plan.emit("vector", (lambda cw=cw: lambda e: e.reduce_max(rst[:, 5:6], cw, axis=AX.X))(), post_drain=True)
                plan.emit("vector", (lambda cw=cw: lambda e: e.tensor_scalar(pe2_sb[:], cw, rst[:, 5:6], None, ALU.is_ge))(), post_drain=True)
                plan.emit("vector", (lambda cw=cw: lambda e: e.scalar_tensor_tensor(cw, pe2_sb[:], BIGN, cw, ALU.mult, ALU.add))(), post_drain=True)
                plan.emit("vector", (lambda cw=cw: lambda e: e.reduce_max(rst[:, 6:7], cw, axis=AX.X))(), post_drain=True)
                plan.emit("vector", (lambda cw=cw: lambda e: e.tensor_scalar(cw, pe_sb[:], rst[:, 6:7], None, ALU.is_ge))(), post_drain=True)
                plan.emit("vector", (lambda cw=cw: lambda e: e.tensor_tensor(cw, cw, pe_sb[:], ALU.mult))(),
                          incs=[(s_cw, 1)], post_drain=True)
                if debug == "deep" and gtb == 0:
                    plan.emit("vector", (lambda: lambda e: e.tensor_copy(z2_dbg[:, 23:38], pe_sb[:]))(), post_drain=True)
                    plan.emit("vector", (lambda cw=cw: lambda e: e.tensor_copy(z2_dbg[:, 38:53], cw))(), post_drain=True)
            if debug == "deep" and s == 0:
                plan.emit("sync", lambda e: e.dma_start(out=dbg_outs["acc_d"], in_=acc[:]),
                          waits=[(s_yev, 16)], incs=[(s_x1, 16), (s_accd, 16)])
            for u in range(NU):
                ug = uglob; uglob += 1
                par = ug % 2
                if u == 0:
                    w1src, w2src = di["sW1"], di["sW2"]
                    b1src, b2src = di["sb1"], di["sb2"]
                else:
                    w1src, w2src = di["eW1"][u - 1], di["eW2"][u - 1]
                    b1src, b2src = di["eb1"][u - 1], di["eb2"][u - 1]
                wwait = [(s_u, ug - 1)] if ug >= 2 else []
                plan.emit("gpsimd", (lambda w=w1src, par=par: lambda e: e.dma_start(
                    out=wbuf[:, par, 0, :, :], in_=w.rearrange("(d p) h -> p d h", p=128)))(),
                    waits=wwait, incs=[(s_wW, 16)])
                plan.emit("gpsimd", (lambda w=w2src, par=par: lambda e: e.dma_start(
                    out=wbuf[:, par, 1, :, :], in_=w.rearrange("(d p) h -> p d h", p=128)))(),
                    incs=[(s_wW, 16)])
                plan.emit("scalar", (lambda b=b1src, par=par: lambda e: e.dma_start(
                    out=eb1col[:, par, :], in_=b.rearrange("(d p) -> p d", p=128),
                    allow_slow_non_contiguous=True))(),
                    waits=wwait, incs=[(s_wB, 16)])
                plan.emit("scalar", (lambda b=b2src, par=par: lambda e: e.dma_start(
                    out=eb2row[:, par, :], in_=b[None, :]))(),
                    incs=[(s_wB, 16)])
                wready = 32 * (ug + 1)
                for ch in range(NCHK + 1):
                    cidx = ug * NCHK + min(ch, NCHK - 1)
                    if False:
                        pass
                    elif ch < NCHK:
                        cidx = ug * NCHK + ch
                        slot = cidx % 3
                        for hb in range(DB):
                            hidx = cidx * DB + hb
                            def mk_m1b(ch=ch, hb=hb, par=par, bank=hidx % 2, s=s):
                                def f(e):
                                    ins = None
                                    for d in range(DB):
                                        ins = e.matmul(psH[bank][:], wbuf[:, par, 0, d, hb * 128:(hb + 1) * 128],
                                                       xnT_sl[:, d, ch * 512:(ch + 1) * 512],
                                                       start=(d == 0), stop=(d == DB - 1))
                                    return ins
                                return f
                            plan.emit("tensor", mk_m1b(),
                                      waits=[(s_wW, wready), (s_xsl, 16 * (s + 1))] + ([(s_hev, hidx - 1)] if hidx >= 2 else []),
                                      incs=[(s_m1, 1)])
                            hdw = []
                            plan.emit("scalar", (lambda hb=hb, par=par, slot=slot, bank=hidx % 2: lambda e: e.activation(
                                hbuf[:, hb, slot, :], psH[bank][:], AF.Relu, bias=eb1col[:, par, hb:hb + 1]))(),
                                waits=[(s_m1, hidx + 1), (s_wB, wready)] + ([(s_m2, 4 * (cidx - 2))] if cidx >= 3 else []) + hdw,
                                incs=[(s_hev, 1)], post_drain=True)
                    if ch < NCHK:
                        chm = ch
                        cidx2 = ug * NCHK + chm
                        slot2 = cidx2 % 3
                        for tb4 in range(4):
                            tb = chm * 4 + tb4
                            yidx = cidx2 * 4 + tb4
                            bank = yidx % 2
                            def mk_m2(tb=tb, tb4=tb4, par=par, slot2=slot2, bank=bank):
                                def f(e):
                                    e.matmul(psY[bank][:, 0:512], ones1_f[:], eb2row[0:1, par, 0:512], start=True, stop=False)
                                    e.matmul(psY[bank][:, 512:1024], ones1_f[:], eb2row[0:1, par, 512:1024], start=True, stop=False)
                                    ins = None
                                    for hb in range(DB):
                                        lt = hbuf[:, hb, slot2, tb4 * 128:(tb4 + 1) * 128]
                                        e.matmul(psY[bank][:, 0:512], lt, wbuf[:, par, 1, hb, 0:512],
                                                 start=False, stop=(hb == DB - 1))
                                        ins = e.matmul(psY[bank][:, 512:1024], lt, wbuf[:, par, 1, hb, 512:1024],
                                                       start=False, stop=(hb == DB - 1))
                                    return ins
                                return f
                            plan.emit("tensor", mk_m2(),
                                      waits=[(s_hev, DB * (cidx2 + 1)), (s_ms, 2), (s_wB, wready)] + ([(s_yev, yidx - 1)] if yidx >= 2 else []),
                                      incs=[(s_m2, 1)])
                            accdw = [(s_accd, 16)] if (debug == "deep" and s == 0 and u == 1) else []
                            if u == 0:
                                plan.emit("vector", (lambda tb=tb, bank=bank: lambda e: e.tensor_tensor(
                                    acc[:, tb, :], acc[:, tb, :], psY[bank][:], ALU.add))(),
                                    waits=[(s_m2, yidx + 1), (s_ai, NTB * (s + 1))] + accdw,
                                    incs=[(s_yev, 1)])
                            else:
                                plan.emit("vector", (lambda tb=tb, bank=bank, u=u: lambda e: e.scalar_tensor_tensor(
                                    acc[:, tb, :], psY[bank][:], cw_sb[:, tb, u - 1:u], acc[:, tb, :], ALU.mult, ALU.add))(),
                                    waits=[(s_m2, yidx + 1), (s_ai, NTB * (s + 1)), (s_cw, NTB * (s + 1))] + accdw,
                                    incs=[(s_yev, 1)])
                plan.emit("vector", (lambda: lambda e: e.tensor_copy(rst[:, 0:1], rst[:, 0:1]))(),
                          waits=[(s_yev, 16 * (ug + 1))], incs=[(s_u, 1)])
            if s == 0:
                s_u16 = plan.sem()
                plan.emit("vector", (lambda: lambda e: e.tensor_copy(rst[:, 0:1], rst[:, 0:1]))(),
                          waits=[(s_yev, 16 * NU)], incs=[(s_u16, 16)])
            for tb in range(NTB):
                gt = t0 + tb
                plan.emit("sync", (lambda gt=gt, tb=tb: lambda e: e.dma_start(
                    out=x1_hbm[gt * 128:(gt + 1) * 128, :], in_=acc[:, tb, :]))(),
                    waits=[(s_yev, 16 * NU * (s + 1))], incs=[(s_x1, 16)])
        if debug == "deep":
            dbg_outs["z_d"] = nc.dram_tensor("z_d", [128, 15], F32, kind="ExternalOutput").ap()
            plan.emit("gpsimd", lambda e: e.dma_start(out=dbg_outs["z_d"], in_=z_dbg[:]),
                      waits=[(s_cw, NTB)], incs=[(s_x1, 16)])
            dbg_outs["z2_d"] = nc.dram_tensor("z2_d", [128, 53], F32, kind="ExternalOutput").ap()
            plan.emit("gpsimd", lambda e: e.dma_start(out=dbg_outs["z2_d"], in_=z2_dbg[:]),
                      waits=[(s_cw, NTB)], incs=[(s_x1, 16)])
            dbg_outs["cw_d"] = nc.dram_tensor("cw_d", [128, NTB, 16], F32, kind="ExternalOutput").ap()
            dbg_outs["acc_d"] = nc.dram_tensor("acc_d", [128, NTB, D], F32, kind="ExternalOutput").ap()
            plan.emit("gpsimd", lambda e: e.dma_start(out=dbg_outs["cw_d"], in_=cw_sb[:]),
                      waits=[(s_cw, NTB)], incs=[(s_x1, 16)])
        pb_done = [(s_x1, 16 * NT + (64 if debug == "deep" else 0))]
        pb.close()



    if "pc" in phases:
        KD = 64
        NH = 16
        C = 128
        CPS = SLAB // C

        wT_hbm = dscr("wT_hbm", (6, 128, DB, D), BF16)
        retT_hbm = dscr("retT_hbm", (128, DB, T), BF16, dbg=True)
        routersT_hbm = dscr("routersT_hbm", (128, DB, 256), BF16)
        oT_hbm = dscr("oT_hbm", (128, DB, T), BF16, dbg=True)
        rowscrF = dscr("rowscrF", (3, 16, SLAB), F32, dbg=True)

        maskS_h = nc.inline_tensor(np.tril(np.ones((128, 128), np.float32), -1), name="maskS")
        bd = np.zeros((128, 128), np.float32)
        for p_ in range(4):
            bd[p_*32:(p_+1)*32, p_*32:(p_+1)*32] = np.tril(np.ones((32, 32)), -1)
        maskBD_h = nc.inline_tensor(bd, name="maskBD")
        maskBDU_h = nc.inline_tensor(bd.T.copy(), name="maskBDU")
        maskI_h = nc.inline_tensor(np.tril(np.ones((128, 128), np.float32), 0), name="maskI")
        if32_h = nc.inline_tensor(np.eye(128, dtype=np.float32), name="if32")
        ones2_np = np.zeros((8, 128, 16), np.float32)
        for pt in range(8):
            for r in range(2):
                ones2_np[pt, r * 64:(r + 1) * 64, 2 * pt + r] = 1.0
        onesB_h = nc.inline_tensor(ones2_np.astype(ml_dtypes.bfloat16), name="onesB")
        mselnp = np.zeros((4, 128, 256), np.float32)
        for m in range(256):
            for level in range(8):
                node = 2 ** level - 1 + (m >> (8 - level))
                bit = (m >> (7 - level)) & 1
                row = node if bit else 256 + node
                mselnp[row // 128, row % 128, m] = 1.0
        msel_h = nc.inline_tensor(mselnp, name="msel")

        pcc = ExitStack()
        S_sb = pcc.enter_context(nc.sbuf_tensor("S_sb", [128, DB, KD], BF16))
        maskS = pcc.enter_context(nc.sbuf_tensor("maskS_sb", [128, 128], F32))
        maskI = pcc.enter_context(nc.sbuf_tensor("maskI_sb", [128, 128], F32))
        maskBD = pcc.enter_context(nc.sbuf_tensor("maskBD_sb", [128, 128], F32))
        maskBDU = pcc.enter_context(nc.sbuf_tensor("maskBDU_sb", [128, 128], F32))
        if32 = pcc.enter_context(nc.sbuf_tensor("if32_sb", [128, 128], F32))
        onesB = pcc.enter_context(nc.sbuf_tensor("onesB_sb", [128, 8, 16], BF16))
        WgT = pcc.enter_context(nc.sbuf_tensor("WgT", [128, DB, 16], BF16))
        WbT = pcc.enter_context(nc.sbuf_tensor("WbT", [128, DB, 16], BF16))
        ctap = pcc.enter_context(nc.sbuf_tensor("ctap", [128, 3, DB, 4], F32))
        cbia = pcc.enter_context(nc.sbuf_tensor("cbia", [128, 3, DB], F32))
        fwoutb = pcc.enter_context(nc.sbuf_tensor("fwoutb", [1, D], F32))
        halo = pcc.enter_context(nc.sbuf_tensor("halo", [128, 3, DB, 3], BF16))
        ones1b = pcc.enter_context(nc.sbuf_tensor("ones1b", [1, 128], BF16))

        s_pcld = plan.sem()
        s_rowE = plan.sem()
        s_xc = plan.sem()
        s_wn = plan.sem()
        s_wst = plan.sem()
        s_wstd = plan.sem()
        s_out = plan.sem()

        plan.gate([(s_x1, 16 * NT)])

        plan.emit("sync", lambda e: e.dma_start(out=maskS[:], in_=maskS_h.ap()), incs=[(s_pcld, 16)])
        plan.emit("sync", lambda e: e.dma_start(out=maskI[:], in_=maskI_h.ap()), incs=[(s_pcld, 16)])
        plan.emit("sync", lambda e: e.dma_start(out=maskBD[:], in_=maskBD_h.ap()), incs=[(s_pcld, 16)])
        plan.emit("sync", lambda e: e.dma_start(out=maskBDU[:], in_=maskBDU_h.ap()), incs=[(s_pcld, 16)])
        plan.emit("sync", lambda e: e.dma_start(out=if32[:], in_=if32_h.ap()), incs=[(s_pcld, 16)])
        plan.emit("sync", lambda e: e.dma_start(out=onesB[:], in_=onesB_h.ap().rearrange("a p c -> p a c")), incs=[(s_pcld, 16)])
        for i, (wk, bk) in enumerate([("cqw", "cqb"), ("ckw", "ckb"), ("cvw", "cvb")]):
            plan.emit("sync", (lambda i=i, wk=wk: lambda e: e.dma_start(
                out=ctap[:, i, :, :], in_=di[wk].rearrange("(a p) one k -> p a (one k)", p=128),
                allow_slow_non_contiguous=True))(), incs=[(s_pcld, 16)])
            plan.emit("sync", (lambda i=i, bk=bk: lambda e: e.dma_start(
                out=cbia[:, i, :], in_=di[bk].rearrange("(a p) -> p a", p=128),
                allow_slow_non_contiguous=True))(), incs=[(s_pcld, 16)])
        plan.emit("sync", lambda e: e.dma_start(out=fwoutb[:], in_=di["fWoutb"][None, :]), incs=[(s_pcld, 16)])
        plan.emit("vector", lambda e: e.memset(S_sb[:], 0.0), incs=[(s_pcld, 1)])
        plan.emit("vector", lambda e: e.memset(halo[:], 0.0), incs=[(s_pcld, 1)])
        plan.emit("vector", lambda e: e.memset(ones1b[:], 1.0), incs=[(s_pcld, 1)])
        PCLD = s_pcld.n

        wstg = ExitStack()
        wnat = wstg.enter_context(nc.sbuf_tensor("wnat", [128, DB, D], BF16))
        wtv = wstg.enter_context(nc.sbuf_tensor("wtv", [128, DB, D], BF16))
        wgnat = wstg.enter_context(nc.sbuf_tensor("wgnat", [16, D], BF16))
        psW = wstg.enter_context(nc.psum_tensor("psW", [128, 1024], BF16))
        wsrcs = [di["aWq"], di["aWk"], di["aWv"], di["aWo"], di["fWout"], di["fWq"]]
        for wi, wsrc in enumerate(wsrcs):
            plan.emit("gpsimd", (lambda w=wsrc: lambda e: e.dma_start(
                out=wnat[:], in_=w.rearrange("(a p) d -> p a d", p=128)))(),
                waits=([(s_wst, wi * DB)] if wi >= 1 else []), incs=[(s_wn, 16)])
            for db in range(DB):
                def mk_wt(wi=wi, db=db):
                    def f(e):
                        ins = None
                        for a in range(DB):
                            ins = e.transpose(psW[:, a * 128:(a + 1) * 128], wnat[:, a, db * 128:(db + 1) * 128], id_sb[:])
                        return ins
                    return f
                plan.emit("tensor", mk_wt(),
                          waits=[(s_wn, 16 * (wi + 1)), (s_const, 16)] + ([(s_wst, s_wst.n)] if (wi, db) != (0, 0) else []),
                          incs=[(s_tr, 1)])
                plan.emit("vector", (lambda db=db: lambda e: e.tensor_copy(wtv[:, db, :], psW[:]))(),
                          waits=[(s_tr, s_tr.n)], incs=[(s_wst, 1)])
            plan.emit("sync", (lambda wi=wi: lambda e: e.dma_start(out=wT_hbm[wi], in_=wtv[:]))(),
                      waits=[(s_wst, s_wst.n)], incs=[(s_wstd, 16)])
        for gi, (gsrc, gdst) in enumerate([(di["aWg"], WgT), (di["aWb"], WbT)]):
            plan.emit("gpsimd", (lambda g=gsrc: lambda e: e.dma_start(out=wgnat[:], in_=g))(),
                      waits=[(s_wst, s_wst.n)], incs=[(s_wn, 16)])
            def mk_gt(gi=gi):
                def f(e):
                    ins = None
                    for a in range(DB):
                        ins = e.transpose(psW[:, a * 16:(a + 1) * 16], wgnat[:, a * 128:(a + 1) * 128], id_sb[0:16, 0:16])
                    return ins
                return f
            plan.emit("tensor", mk_gt(), waits=[(s_wn, 16 * (7 + gi)), (s_wst, s_wst.n)], incs=[(s_tr, 1)])
            plan.emit("vector", (lambda gdst=gdst: lambda e: e.tensor_copy(
                gdst[:], psW[:, 0:128].rearrange("p (a b) -> p a b", a=DB)))(),
                waits=[(s_tr, s_tr.n)], incs=[(s_wst, 1)])
        plan.emit("vector", lambda e: e.memset(wnat[:, 1, :], 0.0), waits=[(s_wst, s_wst.n)], incs=[(s_wst, 1)])
        plan.emit("gpsimd", lambda e: e.dma_start(out=wnat[:, 0, :], in_=di["frouters"][0:128, :]),
                  waits=[(s_wst, s_wst.n)], incs=[(s_wn, 16)])
        plan.emit("gpsimd", lambda e: e.dma_start(out=wnat[0:127, 1, :], in_=di["frouters"][128:255, :]),
                  incs=[(s_wn, 16)])
        for db in range(DB):
            def mk_rt(db=db):
                def f(e):
                    ins = None
                    for a in range(2):
                        ins = e.transpose(psW[:, a * 128:(a + 1) * 128], wnat[:, a, db * 128:(db + 1) * 128], id_sb[:])
                    return ins
                return f
            plan.emit("tensor", mk_rt(), waits=[(s_wn, 16 * 10), (s_wst, s_wst.n)], incs=[(s_tr, 1)])
            plan.emit("vector", (lambda db=db: lambda e: e.tensor_copy(wtv[:, db, 0:256], psW[:, 0:256]))(),
                      waits=[(s_tr, s_tr.n)], incs=[(s_wst, 1)])
        plan.emit("sync", lambda e: e.dma_start(out=routersT_hbm, in_=wtv[:, :, 0:256]),
                  waits=[(s_wst, s_wst.n)], incs=[(s_wstd, 16)])
        WSTG_E = s_wst.n
        wstg.close()

        s_ffz = plan.sem()
        s_ffe = plan.sem()
        s_ffr = plan.sem()
        s_ffrd = plan.sem()
        s_q1 = plan.sem()
        s_wld2 = plan.sem()
        s_pj = plan.sem()
        s_pje = plan.sem()
        s_cv = plan.sem()
        s_gz = plan.sem()
        s_n2 = plan.sem()
        s_stt = plan.sem()
        s_row = plan.sem()
        s_dl = plan.sem()
        s_dle = plan.sem()
        s_ot = plan.sem()
        s_cmb = plan.sem()
        s_cl = plan.sem()
        s_dbgx = plan.sem()

        for s in range(2):
            t0g = s * NTB
            pcs = ExitStack()
            statsA = pcs.enter_context(nc.sbuf_tensor(f"statsA{s}", [128, SLAB], F32))
            statsB = pcs.enter_context(nc.sbuf_tensor(f"statsB{s}", [128, SLAB], F32))
            statsC = pcs.enter_context(nc.sbuf_tensor(f"statsC{s}", [128, SLAB], F32))
            SL = {"g": (statsA, 0), "e1": (statsA, 32), "lnrho": (statsA, 64), "zb": (statsA, 96),
                  "u": (statsB, 0), "cB": (statsB, 32), "cprev": (statsB, 64), "rn": (statsB, 96),
                  "zg": (statsC, 0), "eps": (statsC, 32), "c": (statsC, 64), "l": (statsC, 96),
                  "lnb": (statsB, 64), "ones": (statsB, 0), "edr": (statsB, 0), "ecr": (statsC, 0)}
            def sl(name, c0=0, c1=SLAB):
                t, b = SL[name]
                return t[b:b + 16, c0:c1]

            sub1 = ExitStack()
            xn1T = sub1.enter_context(nc.sbuf_tensor(f"xn1T{s}", [128, DB, SLAB], BF16, side="right"))

            if PC_STAGE < 2:
                sub1.close()
                pcs.close()
                continue
            ln1 = ExitStack()
            x1t = ln1.enter_context(nc.sbuf_tensor(f"x1t{s}", [128, 2, D], F32))
            xc1 = ln1.enter_context(nc.sbuf_tensor(f"xc1{s}", [128, D], F32))
            sq1 = ln1.enter_context(nc.sbuf_tensor(f"sq1{s}", [128, D], F32))
            xn1n = ln1.enter_context(nc.sbuf_tensor(f"xn1n{s}", [128, D], BF16))
            st1 = ln1.enter_context(nc.sbuf_tensor(f"st1{s}", [128, 2, 8], F32))
            psT1 = ln1.enter_context(nc.psum_tensor(f"psT1{s}", [128, 1024], BF16))
            s_l1s = plan.sem()
            s_l1x = plan.sem()
            s_l1n = plan.sem()
            s_l1t = plan.sem()
            s_l1e = plan.sem()
            for tb in range(NTB):
                gt = t0g + tb
                par = tb % 2
                xt = x1t[:, par, :]
                sa = st1[:, par, :]
                plan.emit("sync", (lambda xt=xt, gt=gt: lambda e: e.dma_start(out=xt, in_=x1_hbm[gt * 128:(gt + 1) * 128, :]))(),
                          waits=[(s_x1, 16 * NT)] + ([(s_l1n, tb - 1)] if tb >= 2 else []), incs=[(s_l1x, 16)])
                plan.emit("vector", (lambda xt=xt, sa=sa: lambda e: e.reduce_sum(sa[:, 0:1], xt, axis=AX.X))(),
                          waits=[(s_l1x, s_l1x.n)] + ([(s_l1n, tb - 1)] if tb >= 2 else []), post_drain=True)
                plan.emit("vector", (lambda sa=sa: lambda e: e.tensor_scalar(sa[:, 1:2], sa[:, 0:1], -1.0 / D, None, ALU.mult))(),
                          incs=[(s_l1s, 1)])
                plan.emit("scalar", (lambda xt=xt, sa=sa: lambda e: e.activation(xc1[:], xt, AF.Identity, bias=sa[:, 1:2]))(),
                          waits=[(s_l1s, s_l1s.n)])
                plan.emit("scalar", (lambda sa=sa: lambda e: e.activation(sq1[:], xc1[:], AF.Square, accum_out=sa[:, 2:3]))(), post_drain=True)
                plan.emit("scalar", (lambda sa=sa, sc=ccol(1.0 / D), bi=ccol(EPS): lambda e: e.activation(sa[:, 3:4], sa[:, 2:3], AF.Ln, scale=sc, bias=bi))())
                plan.emit("scalar", (lambda sa=sa, sc=ccol(-0.5): lambda e: e.activation(sa[:, 4:5], sa[:, 3:4], AF.Exp, scale=sc))(), post_drain=True)
                plan.emit("scalar", (lambda sa=sa: lambda e: e.activation(xn1n[:], xc1[:], AF.Identity, scale=sa[:, 4:5]))(),
                          waits=([(s_l1t, s_l1t.n)] if s_l1t.n >= 1 else []), incs=[(s_l1n, 1)])
                def mk_t1(tb=tb):
                    def f(e):
                        ins = None
                        for d in range(DB):
                            ins = e.transpose(psT1[:, d * 128:(d + 1) * 128], xn1n[:, d * 128:(d + 1) * 128], id_sb[:])
                        return ins
                    return f
                plan.emit("tensor", mk_t1(), waits=[(s_l1n, s_l1n.n), (s_l1e, s_l1e.n)], incs=[(s_l1t, 1)])
                plan.emit("vector", (lambda tb=tb: lambda e: e.tensor_copy(
                    xn1T[:, :, tb * 128:(tb + 1) * 128], psT1[:].rearrange("p (a b) -> p a b", a=DB)))(),
                    waits=[(s_l1t, s_l1t.n)], incs=[(s_l1e, 1)])
            LN1_E = s_l1e.n
            ln1.close()

            if PC_STAGE < 3:
                sub1.close()
                pcs.close()
                continue
            ffn = ExitStack()
            zT = ffn.enter_context(nc.sbuf_tensor(f"zT{s}", [128, 2, SLAB], F32))
            LG = ffn.enter_context(nc.sbuf_tensor(f"LG{s}", [128, 4, SLAB], F32))
            lpT = ffn.enter_context(nc.sbuf_tensor(f"lpT{s}", [128, 2, SLAB], BF16))
            q1c = ffn.enter_context(nc.sbuf_tensor(f"q1c{s}", [128, DB, 512], BF16))
            rstg = ffn.enter_context(nc.sbuf_tensor(f"rstg{s}", [128, DB, 512], BF16))
            wb1 = ffn.enter_context(nc.sbuf_tensor(f"wb1{s}", [128, DB, D], BF16))
            msel = ffn.enter_context(nc.sbuf_tensor(f"msel{s}", [128, 4, 256], F32))
            leafv = ffn.enter_context(nc.sbuf_tensor(f"leafv{s}", [128, 2, D], BF16))
            routersT = ffn.enter_context(nc.sbuf_tensor(f"routersT{s}", [128, DB, 256], BF16))
            psq = ffn.enter_context(nc.psum_tensor(f"psq{s}", [128, 512], F32))
            psz = [ffn.enter_context(nc.psum_tensor(f"psz{s}{k}", [128, 512], F32)) for k in range(2)]

            plan.emit("sync", lambda e: e.dma_start(out=msel[:], in_=msel_h.ap().rearrange("a p c -> p a c")),
                      incs=[(s_pcld, 16)])
            plan.emit("gpsimd", lambda e: e.dma_start(out=leafv[:], in_=di["fleafv"].rearrange("(a p) d -> p a d", p=128)),
                      incs=[(s_pcld, 16)])
            plan.emit("sync", lambda e: e.dma_start(out=routersT[:], in_=routersT_hbm),
                      waits=[(s_wstd, 16 * 7)], incs=[(s_pcld, 16)])
            PCLD2 = s_pcld.n
            plan.emit("sync", lambda e: e.dma_start(out=wb1[:], in_=wT_hbm[5]),
                      waits=[(s_wstd, 16 * 6)], incs=[(s_wld2, 16)])
            W1L = s_wld2.n
            q1_marks = []
            zev_marks = []
            for ch in range(4):
                for eb in range(DB):
                    def mk_q1e(ch=ch, eb=eb):
                        def f(e):
                            ins = None
                            for d in range(DB):
                                ins = e.matmul(psq[:], wb1[:, d, eb * 128:(eb + 1) * 128],
                                               xn1T[:, d, ch * 512:(ch + 1) * 512],
                                               start=(d == 0), stop=(d == DB - 1))
                            return ins
                        return f
                    plan.emit("tensor", mk_q1e(),
                              waits=[(s_wld2, W1L), (s_l1e, LN1_E)] + ([(s_q1, q1_marks[-1])] if q1_marks else []),
                              incs=[(s_ffz, 1)])
                    plan.emit("vector", (lambda eb=eb: lambda e: e.tensor_copy(q1c[:, eb, :], psq[:]))(),
                              waits=[(s_ffz, s_ffz.n)], incs=[(s_q1, 1)])
                    q1_marks.append(s_q1.n)
                for nb in range(2):
                    def mk_z1(ch=ch, nb=nb):
                        def f(e):
                            ins = None
                            for eb in range(DB):
                                ins = e.matmul(psz[nb][:], routersT[:, eb, nb * 128:(nb + 1) * 128],
                                               q1c[:, eb, :], start=(eb == 0), stop=(eb == DB - 1))
                            return ins
                        return f
                    plan.emit("tensor", mk_z1(),
                              waits=[(s_q1, s_q1.n), (s_pcld, PCLD2)] + ([(s_ffe, zev_marks[-2])] if len(zev_marks) >= 2 else []),
                              incs=[(s_ffz, 1)])
                    plan.emit("vector", (lambda ch=ch, nb=nb: lambda e: e.tensor_copy(
                        zT[:, nb, ch * 512:(ch + 1) * 512], psz[nb][:]))(),
                        waits=[(s_ffz, s_ffz.n)], incs=[(s_ffe, 1)])
                    zev_marks.append(s_ffe.n)
            ZT_E = s_ffe.n
            for nb in range(2):
                plan.emit("scalar", (lambda nb=nb: lambda e: e.activation(LG[:, 2 + nb, :], zT[:, nb, :], AF.Abs))(),
                          waits=[(s_ffe, ZT_E)])
                plan.emit("scalar", (lambda nb=nb, sc=ccol(-1.0): lambda e: e.activation(LG[:, 2 + nb, :], LG[:, 2 + nb, :], AF.Exp, scale=sc))())
                plan.emit("scalar", (lambda nb=nb, bi=ccol(1.0): lambda e: e.activation(LG[:, 2 + nb, :], LG[:, 2 + nb, :], AF.Ln, bias=bi))())
                plan.emit("scalar", (lambda nb=nb, sc=ccol(-1.0): lambda e: e.activation(LG[:, nb, :], zT[:, nb, :], AF.Relu, scale=sc))(),
                          incs=[(s_ffe, 1)])
                plan.emit("vector", (lambda nb=nb: lambda e: e.scalar_tensor_tensor(
                    LG[:, nb, :], LG[:, nb, :], -1.0, LG[:, 2 + nb, :], ALU.mult, ALU.subtract))(),
                    waits=[(s_ffe, s_ffe.n)], incs=[(s_ffe, 1)])
                plan.emit("scalar", (lambda nb=nb: lambda e: e.activation(zT[:, nb, :], zT[:, nb, :], AF.Relu))(),
                          waits=[(s_ffe, s_ffe.n)], incs=[(s_ffe, 1)])
                plan.emit("vector", (lambda nb=nb: lambda e: e.scalar_tensor_tensor(
                    LG[:, 2 + nb, :], zT[:, nb, :], -1.0, LG[:, 2 + nb, :], ALU.mult, ALU.subtract))(),
                    waits=[(s_ffe, s_ffe.n)], incs=[(s_ffe, 1)])
            LG_E = s_ffe.n
            lp_marks = []
            for mb in range(2):
                for ch in range(4):
                    def mk_lp(mb=mb, ch=ch):
                        def f(e):
                            ins = None
                            for k in range(4):
                                ins = e.matmul(psz[mb][:], msel[:, k, mb * 128:(mb + 1) * 128],
                                               LG[:, k, ch * 512:(ch + 1) * 512],
                                               start=(k == 0), stop=(k == 3))
                            return ins
                        return f
                    plan.emit("tensor", mk_lp(),
                              waits=[(s_ffe, LG_E), (s_pcld, PCLD2)] + ([(s_ffe, lp_marks[-1])] if lp_marks else []),
                              incs=[(s_ffz, 1)])
                    plan.emit("scalar", (lambda mb=mb, ch=ch: lambda e: e.activation(
                        lpT[:, mb, ch * 512:(ch + 1) * 512], psz[mb][:], AF.Exp))(),
                        waits=[(s_ffz, s_ffz.n)], incs=[(s_ffe, 1)], post_drain=True)
                    lp_marks.append(s_ffe.n)
            LP_E = s_ffe.n
            rt_marks = []
            rtd_marks = []
            for ch in range(4):
                for db in range(DB):
                    def mk_rT(ch=ch, db=db):
                        def f(e):
                            ins = None
                            for m in range(2):
                                ins = e.matmul(psq[:], leafv[:, m, db * 128:(db + 1) * 128],
                                               lpT[:, m, ch * 512:(ch + 1) * 512],
                                               start=(m == 0), stop=(m == 1))
                            return ins
                        return f
                    plan.emit("tensor", mk_rT(),
                              waits=[(s_ffe, LP_E), (s_pcld, PCLD2)] + ([(s_ffr, rt_marks[-1])] if rt_marks else []),
                              incs=[(s_ffz, 1)])
                    plan.emit("vector", (lambda db=db: lambda e: e.tensor_copy(rstg[:, db, :], psq[:]))(),
                              waits=[(s_ffz, s_ffz.n)] + ([(s_ffrd, rtd_marks[-1])] if rtd_marks and db == 0 else []),
                              incs=[(s_ffr, 1)])
                    rt_marks.append(s_ffr.n)
                plan.emit("sync", (lambda s2=s, ch=ch: lambda e: e.dma_start(
                    out=retT_hbm[:, :, (s2 * 4 + ch) * 512:(s2 * 4 + ch + 1) * 512], in_=rstg[:]))(),
                    waits=[(s_ffr, s_ffr.n)], incs=[(s_ffrd, 16)])
                rtd_marks.append(s_ffrd.n)
            RETD_E = s_ffrd.n
            ffn.close()

            if PC_STAGE < 4:
                sub1.close()
                pcs.close()
                continue
            qkvs = ExitStack()
            qc = qkvs.enter_context(nc.sbuf_tensor(f"qc{s}", [128, DB, SLAB], BF16))
            kc = qkvs.enter_context(nc.sbuf_tensor(f"kc{s}", [128, DB, SLAB], BF16))
            vc = qkvs.enter_context(nc.sbuf_tensor(f"vc{s}", [128, DB, SLAB], BF16))
            prj = ExitStack()
            rawwin = prj.enter_context(nc.sbuf_tensor(f"rawwin{s}", [128, 3, DB, 515], BF16, side="right"))
            wb2 = prj.enter_context(nc.sbuf_tensor(f"wb2{s}", [128, DB, D], BF16, side="right"))
            ksq = prj.enter_context(nc.sbuf_tensor(f"ksq{s}", [128, DB, 128], BF16, side="right"))
            psp = [prj.enter_context(nc.psum_tensor(f"psp{s}{k}", [128, 512], F32)) for k in range(2)]
            psg = prj.enter_context(nc.psum_tensor(f"psg{s}", [128, 512], F32))
            convd = [qc, kc, vc]

            gz_marks = []
            for gi, (wt, slot) in enumerate([(WgT, "zg"), (WbT, "zb")]):
                for ch in range(4):
                    gb = 0 if slot == "zg" else 96
                    def mk_gz(wt=wt, ch=ch, gb=gb):
                        def f(e):
                            ins = None
                            for d in range(DB):
                                ins = e.matmul(psg[gb:gb + 16, :], wt[:, d, :], xn1T[:, d, ch * 512:(ch + 1) * 512],
                                               start=(d == 0), stop=(d == DB - 1),
                                               tile_position=(0, gb) if gb else None)
                            return ins
                        return f
                    plan.emit("tensor", mk_gz(),
                              waits=[(s_l1e, LN1_E), (s_wst, WSTG_E)] + ([(s_gz, gz_marks[-1])] if gz_marks else []),
                              incs=[(s_gz, 1)])
                    plan.emit("vector", (lambda slot=slot, ch=ch, gb=gb: lambda e: e.tensor_copy(
                        sl(slot, ch * 512, (ch + 1) * 512), psg[gb:gb + 16, :]))(),
                        waits=[(s_gz, s_gz.n)], incs=[(s_gz, 1)])
                    gz_marks.append(s_gz.n)
            GZ_E = s_gz.n

            pj_marks = []
            cva_marks = {}
            for ch in range(4):
                for ti in range(3):
                    plan.emit("sync", (lambda ti=ti: lambda e: e.dma_start(out=wb2[:], in_=wT_hbm[ti]))(),
                              waits=[(s_wstd, 16 * 6), (s_pj, s_pj.n)], incs=[(s_wld2, 16)])
                    WL2 = s_wld2.n
                    for cb in range(DB):
                        def mk_pj(ti=ti, ch=ch, cb=cb):
                            def f(e):
                                ins = None
                                for d in range(DB):
                                    ins = e.matmul(psp[cb % 2][:], wb2[:, d, cb * 128:(cb + 1) * 128],
                                                   xn1T[:, d, ch * 512:(ch + 1) * 512],
                                                   start=(d == 0), stop=(d == DB - 1))
                                return ins
                            return f
                        plan.emit("tensor", mk_pj(),
                                  waits=[(s_wld2, WL2), (s_l1e, LN1_E)] + ([(s_pje, pj_marks[-2])] if len(pj_marks) >= 2 else []),
                                  incs=[(s_pj, 1)])
                        w_extra = []
                        if ch >= 1 and cb == 0:
                            w_extra = [(s_cv, cva_marks[(ch - 1, ti)])]
                        plan.emit("vector", (lambda ti=ti, cb=cb: lambda e: e.tensor_copy(
                            rawwin[:, ti, cb, 3:515], psp[cb % 2][:]))(),
                            waits=[(s_pj, s_pj.n)] + w_extra, incs=[(s_pje, 1)])
                        pj_marks.append(s_pje.n)
                    plan.emit("vector", (lambda ti=ti: lambda e: e.tensor_copy(rawwin[:, ti, :, 0:3], halo[:, ti, :, :]))(),
                              waits=[(s_pje, s_pje.n), (s_pcld, PCLD)], incs=[(s_cv, 1)])
                    plan.emit("vector", (lambda ti=ti: lambda e: e.tensor_copy(halo[:, ti, :, :], rawwin[:, ti, :, 512:515]))(),
                              incs=[(s_cv, 1)])
                    for cb in range(DB):
                        plan.emit("scalar", (lambda ti=ti, cb=cb, ch=ch: lambda e: e.activation(
                            convd[ti][:, cb, ch * 512:(ch + 1) * 512], rawwin[:, ti, cb, 3:515],
                            AF.Identity, scale=ctap[:, ti, cb, 3:4], bias=cbia[:, ti, cb:cb + 1]))(),
                            waits=[(s_cv, s_cv.n), (s_pcld, PCLD)], incs=[(s_cv, 1)])
                        for tap in range(3):
                            plan.emit("vector", (lambda ti=ti, cb=cb, ch=ch, tap=tap: lambda e: e.scalar_tensor_tensor(
                                convd[ti][:, cb, ch * 512:(ch + 1) * 512], rawwin[:, ti, cb, tap:512 + tap],
                                ctap[:, ti, cb, tap:tap + 1], convd[ti][:, cb, ch * 512:(ch + 1) * 512],
                                ALU.mult, ALU.add))(),
                                waits=[(s_cv, s_cv.n)], incs=[(s_cv, 1)])
                    cva_marks[(ch, ti)] = s_cv.n
                for cc4 in range(4):
                    n = ch * 4 + cc4
                    plan.emit("scalar", (lambda ch=ch, cc4=cc4: lambda e: e.activation(
                        ksq[:],
                        kc[:, :, ch * 512 + cc4 * 128: ch * 512 + (cc4 + 1) * 128],
                        AF.Square))(),
                        waits=[(s_cv, cva_marks[(ch, 1)]), (s_n2, s_n2.n)], incs=[(s_n2, 1)], post_drain=True)
                    def mk_n2(n=n):
                        def f(e):
                            ins = None
                            for pt in range(DB):
                                ins = e.matmul(psg[64:80, 0:128], onesB[:, pt, :], ksq[:, pt, :],
                                               start=(pt == 0), stop=(pt == DB - 1),
                                               tile_position=(0, 64))
                            return ins
                        return f
                    plan.emit("tensor", mk_n2(),
                              waits=[(s_n2, s_n2.n), (s_gz, GZ_E), (s_pcld, PCLD)],
                              incs=[(s_n2, 1)])
                    plan.emit("vector", (lambda n=n: lambda e: e.tensor_copy(
                        sl("lnrho", n * 128, (n + 1) * 128), psg[64:80, 0:128]))(),
                        waits=[(s_n2, s_n2.n)], incs=[(s_n2, 1)])
            PRJ_CV = s_cv.n
            N2_E = s_n2.n

            def sp_compose(zslot, oslot, w0):
                plan.emit("scalar", (lambda zslot=zslot: lambda e: e.activation(sl("u"), sl(zslot), AF.Abs))(), waits=w0)
                plan.emit("scalar", (lambda sc=ccol16(-1.0, SL["l"][1]): lambda e: e.activation(sl("l"), sl("u"), AF.Exp, scale=sc))())
                plan.emit("scalar", (lambda bi=ccol16(1.0, SL["l"][1]): lambda e: e.activation(sl("l"), sl("l"), AF.Ln, bias=bi))())
                plan.emit("scalar", (lambda zslot=zslot, sc=ccol16(-1.0, SL["rn"][1]): lambda e: e.activation(sl("rn"), sl(zslot), AF.Relu, scale=sc))(),
                          incs=[(s_stt, 1)])
                plan.emit("vector", (lambda oslot=oslot: lambda e: e.scalar_tensor_tensor(
                    sl(oslot), sl("rn"), -1.0, sl("l"), ALU.mult, ALU.subtract))(),
                    waits=[(s_stt, s_stt.n)], incs=[(s_stt, 1)])
            sp_compose("zg", "g", [(s_gz, GZ_E)])
            sp_compose("zb", "lnb", [(s_stt, s_stt.n)])
            plan.emit("vector", (lambda: lambda e: e.tensor_scalar(sl("lnrho"), sl("lnrho"), 1e-24, None, ALU.max))(),
                      waits=[(s_n2, N2_E), (s_stt, s_stt.n)], incs=[(s_stt, 1)], post_drain=True)
            plan.emit("scalar", (lambda: lambda e: e.activation(sl("lnrho"), sl("lnrho"), AF.Ln))(),
                      waits=[(s_stt, s_stt.n)], incs=[(s_stt, 1)], post_drain=True)
            plan.emit("vector", (lambda: lambda e: e.tensor_scalar(sl("lnrho"), sl("lnrho"), -0.5, None, ALU.mult))(),
                      waits=[(s_stt, s_stt.n)], incs=[(s_stt, 1)], post_drain=True)
            plan.emit("vector", (lambda: lambda e: e.tensor_tensor(sl("e1"), sl("lnb"), sl("lnrho"), ALU.add))(),
                      waits=[(s_stt, s_stt.n)], incs=[(s_stt, 1)], post_drain=True)
            plan.emit("vector", (lambda: lambda e: e.memset(sl("ones"), 1.0))(), waits=[(s_stt, s_stt.n)], incs=[(s_stt, 1)])
            for n in range(CPS):
                c0, c1 = n * 128, (n + 1) * 128
                plan.emit("vector", (lambda c0=c0, c1=c1: lambda e: e.tensor_tensor_scan(
                    sl("c", c0, c1), sl("ones", c0, c1), sl("g", c0, c1), 0.0, ALU.mult, ALU.add))(),
                    waits=[(s_stt, s_stt.n)], incs=[(s_stt, 1)], post_drain=True)
                plan.emit("vector", (lambda c0=c0, c1=c1: lambda e: e.tensor_tensor_scan(
                    sl("cB", c0, c1), sl("ones", c0, c1), sl("g", c0, c1), 0.0, ALU.mult, ALU.add))(),
                    incs=[(s_stt, 1)], post_drain=True)
                plan.emit("vector", (lambda c0=c0: lambda e: e.memset(sl("cprev", c0, c0 + 1), 0.0))(),
                          incs=[(s_stt, 1)])
                plan.emit("vector", (lambda c0=c0, c1=c1: lambda e: e.tensor_copy(
                    sl("cprev", c0 + 1, c1), sl("c", c0, c1 - 1)))(),
                    incs=[(s_stt, 1)], post_drain=True)
                plan.emit("vector", (lambda c0=c0, c1=c1: lambda e: e.tensor_tensor(
                    sl("e1", c0, c1), sl("e1", c0, c1), sl("cB", c0, c1), ALU.subtract))(),
                    incs=[(s_stt, 1)], post_drain=True)
                plan.emit("vector", (lambda c0=c0, c1=c1: lambda e: e.tensor_scalar(
                    sl("eps", c0, c1), sl("e1", c0, c1), sl("cB", c1 - 1, c1), None, ALU.add))(),
                    incs=[(s_stt, 1)], post_drain=True)
            plan.emit("vector", (lambda: lambda e: e.tensor_tensor(sl("edr"), sl("cprev"), sl("lnrho"), ALU.add))(),
                      waits=[(s_stt, s_stt.n)], incs=[(s_stt, 1)], post_drain=True)
            plan.emit("sync", (lambda: lambda e: e.dma_start(out=rowscrF[0], in_=sl("edr")))(),
                      waits=[(s_stt, s_stt.n)], incs=[(s_row, 16)])
            plan.emit("sync", (lambda: lambda e: e.dma_start(out=rowscrF[1], in_=sl("e1")))(),
                      incs=[(s_row, 16)])
            plan.emit("sync", (lambda: lambda e: e.dma_start(out=rowscrF[2], in_=sl("c")))(),
                      incs=[(s_row, 16)])
            STT_E = s_stt.n
            ROW_E = s_row.n
            prj.close()
            sub1.close()

            if PC_STAGE < 5:
                prj.close() if False else None
                qkvs.close()
                pcs.close()
                continue
            dl = ExitStack()
            knat = dl.enter_context(nc.sbuf_tensor(f"knat{s}", [128, DB, 128], BF16))
            rhsb = dl.enter_context(nc.sbuf_tensor(f"rhsb{s}", [128, NH, 128], BF16))
            kepb = dl.enter_context(nc.sbuf_tensor(f"kepb{s}", [128, NH, 64], BF16))
            rowb = dl.enter_context(nc.sbuf_tensor(f"rowb{s}", [1, 3, SLAB], F32))
            rowbE = dl.enter_context(nc.sbuf_tensor(f"rowbE{s}", [1, SLAB], BF16))
            Earg = dl.enter_context(nc.sbuf_tensor(f"Earg{s}", [128, 2, 128], F32))
            statT = dl.enter_context(nc.sbuf_tensor(f"statT{s}", [128, 64], F32))
            Edm = dl.enter_context(nc.sbuf_tensor(f"Edm{s}", [128, 2, 128], F32))
            DREP = dl.enter_context(nc.sbuf_tensor(f"DREP{s}", [128, 128], F32))
            Abuf = dl.enter_context(nc.sbuf_tensor(f"Abuf{s}", [128, 128], BF16))
            Mbuf = dl.enter_context(nc.sbuf_tensor(f"Mbuf{s}", [128, 128], BF16))
            Tbuf = dl.enter_context(nc.sbuf_tensor(f"Tbuf{s}", [128, 2, 128], BF16))
            Dbuf = dl.enter_context(nc.sbuf_tensor(f"Dbuf{s}", [128, 128], BF16))
            DTbuf = dl.enter_context(nc.sbuf_tensor(f"DTbuf{s}", [128, 128], BF16))
            negAT = dl.enter_context(nc.sbuf_tensor(f"negAT{s}", [128, 128], BF16))
            Bsb = dl.enter_context(nc.sbuf_tensor(f"Bsb{s}", [128, 128], BF16))
            Pbuf = dl.enter_context(nc.sbuf_tensor(f"Pbuf{s}", [128, 2, 256], BF16))
            MTb = dl.enter_context(nc.sbuf_tensor(f"MTb{s}", [128, 128], BF16))
            TRH = dl.enter_context(nc.sbuf_tensor(f"TRH{s}", [128, 128], BF16))
            nMWT = dl.enter_context(nc.sbuf_tensor(f"nMWT{s}", [128, 128], BF16))
            PPT = dl.enter_context(nc.sbuf_tensor(f"PPT{s}", [128, 64], BF16))
            SQb = dl.enter_context(nc.sbuf_tensor(f"SQb{s}", [128, 64], BF16))
            otmp = dl.enter_context(nc.sbuf_tensor(f"otmp{s}", [128, 128], F32))
            ostg = dl.enter_context(nc.sbuf_tensor(f"ostg{s}", [128, DB, 128], BF16))
            psA = dl.enter_context(nc.psum_tensor(f"psA{s}", [128, 512], F32))
            psB = dl.enter_context(nc.psum_tensor(f"psB{s}", [128, 1024], BF16))
            psC = dl.enter_context(nc.psum_tensor(f"psC{s}", [128, 512], F32))
            psD = dl.enter_context(nc.psum_tensor(f"psD{s}", [128, 512], F32))
            psE = dl.enter_context(nc.psum_tensor(f"psE{s}", [128, 512], F32))

            if debug and s == 0:
                dbg_outs.setdefault("qc_d", nc.dram_tensor("qc_d", [128, DB, SLAB], BF16, kind="ExternalOutput").ap())
                dbg_outs.setdefault("kc_d", nc.dram_tensor("kc_d", [128, DB, SLAB], BF16, kind="ExternalOutput").ap())
                dbg_outs.setdefault("vc_d", nc.dram_tensor("vc_d", [128, DB, SLAB], BF16, kind="ExternalOutput").ap())
                plan.emit("gpsimd", lambda e: e.dma_start(out=dbg_outs["qc_d"], in_=qc[:]), waits=[(s_cv, PRJ_CV)], incs=[(s_dbgx, 16)])
                plan.emit("gpsimd", lambda e: e.dma_start(out=dbg_outs["kc_d"], in_=kc[:]), waits=[(s_cv, PRJ_CV)], incs=[(s_dbgx, 16)])
                plan.emit("gpsimd", lambda e: e.dma_start(out=dbg_outs["vc_d"], in_=vc[:]), waits=[(s_cv, PRJ_CV)], incs=[(s_dbgx, 16)])
            if debug and s == 0:
                dbgA_sb = dl.enter_context(nc.sbuf_tensor("dbgA_sb", [128, NH, 128], BF16))
                dbgT_sb = dl.enter_context(nc.sbuf_tensor("dbgT_sb", [128, NH, 128], BF16))
                dbg_outs.setdefault("A_d", nc.dram_tensor("A_d", [128, NH, 128], BF16, kind="ExternalOutput").ap())
                dbg_outs.setdefault("T_d", nc.dram_tensor("T_d", [128, NH, 128], BF16, kind="ExternalOutput").ap())
            ot_mark = 0
            for n in range(CPS):
                gn = s * CPS + n
                c0 = n * 128
                plan.emit("scalar", (lambda n=n: lambda e: e.dma_start(
                    out=rowb[0:1, :, :].rearrange("o q (h t) -> o q h t", h=16),
                    in_=rowscrF[None, :, :, n * 128:(n + 1) * 128]))(),
                    waits=[(s_row, ROW_E), (s_dle, s_dle.n)], incs=[(s_row, 16)])
                plan.emit("scalar", (lambda: lambda e: e.activation(rowbE[0:1, :], rowb[0:1, 2, :], AF.Exp))(),
                          waits=[(s_row, s_row.n)], incs=[(s_row, 1)], post_drain=True)
                ROWN = s_row.n
                if debug and s == 0 and n == DBGN:
                    dbg_outs.setdefault("rowb_d", nc.dram_tensor("rowb_d", [1, 3, SLAB], F32, kind="ExternalOutput").ap())
                    dbg_outs.setdefault("rowbE_d", nc.dram_tensor("rowbE_d", [1, SLAB], BF16, kind="ExternalOutput").ap())
                    plan.emit("gpsimd", lambda e: e.dma_start(out=dbg_outs["rowb_d"], in_=rowb[:]),
                              waits=[(s_row, ROWN)], incs=[(s_dbgx, 16)])
                    plan.emit("gpsimd", lambda e: e.dma_start(out=dbg_outs["rowbE_d"], in_=rowbE[:]),
                              incs=[(s_dbgx, 16)])
                def mk_st2(c0=c0):
                    def f(e):
                        ins = None
                        for qi, qn in enumerate(["cprev", "lnrho", "c", "eps"]):
                            b = SL[qn][1]
                            ins = e.transpose(psA[:, qi * 16:(qi + 1) * 16], sl(qn, c0, c0 + 128),
                                              if32[b:b + 16, b:b + 16])
                        return ins
                    return f
                plan.emit("tensor", mk_st2(),
                          waits=[(s_stt, STT_E), (s_pcld, PCLD), (s_dle, s_dle.n)], incs=[(s_dl, 1)])
                plan.emit("vector", (lambda: lambda e: e.tensor_copy(statT[:, 0:64], psA[:, 0:64]))(),
                          waits=[(s_dl, s_dl.n)], incs=[(s_dl, 1)], post_drain=True)
                plan.emit("vector", (lambda: lambda e: e.tensor_tensor(statT[:, 0:16], statT[:, 0:16], statT[:, 16:32], ALU.add))(),
                          incs=[(s_dl, 1)], post_drain=True)
                plan.emit("scalar", (lambda: lambda e: e.activation(statT[:, 0:64], statT[:, 0:64], AF.Exp))(),
                          waits=[(s_dl, s_dl.n)], incs=[(s_dl, 1)], post_drain=True)
                plan.emit("vector", (lambda: lambda e: e.tensor_scalar(statT[:, 16:32], statT[:, 32:48], 0.125, None, ALU.mult))(),
                          waits=[(s_dl, s_dl.n)], incs=[(s_dl, 1)], post_drain=True)
                STCH = s_dl.n
                def mk_kn1(c0=c0):
                    def f(e):
                        ins = None
                        for pt in range(DB):
                            ins = e.transpose(psB[:, pt * 128:(pt + 1) * 128], kc[:, pt, c0:c0 + 128], id_sb[:])
                        return ins
                    return f
                plan.emit("tensor", mk_kn1(), waits=[(s_cv, PRJ_CV), (s_dle, s_dle.n)], incs=[(s_dl, 1)])
                plan.emit("vector", (lambda: lambda e: e.tensor_copy(knat[:], psB[:].rearrange("p (a b) -> p a b", a=DB)))(),
                          waits=[(s_dl, s_dl.n)], incs=[(s_dle, 1)])
                def mk_kn2(c0=c0):
                    def f(e):
                        ins = None
                        for pt in range(DB):
                            ins = e.transpose(psB[:, pt * 128:(pt + 1) * 128], vc[:, pt, c0:c0 + 128], id_sb[:])
                    
                        return ins
                    return f
                plan.emit("tensor", mk_kn2(), waits=[(s_dle, s_dle.n)], incs=[(s_dl, 1)])
                plan.emit("vector", (lambda: lambda e: e.tensor_copy(
                    rhsb[:, :, 64:128], psB[:].rearrange("p (h x) -> p h x", h=NH)))(),
                    waits=[(s_dl, s_dl.n)], incs=[(s_dle, 1)])
                for h in range(NH):
                    pt, b64 = h // 2, (h % 2) * 64
                    plan.emit("vector", (lambda h=h, pt=pt, b64=b64: lambda e: e.tensor_scalar(
                        rhsb[:, h, 0:64], knat[:, pt, b64:b64 + 64], statT[:, h:h + 1], None, ALU.mult))(),
                        waits=[(s_dle, s_dle.n), (s_dl, STCH)], incs=[(s_dle, 1)])
                    plan.emit("vector", (lambda h=h, pt=pt, b64=b64: lambda e: e.tensor_scalar(
                        kepb[:, h, :], knat[:, pt, b64:b64 + 64], statT[:, 48 + h:49 + h], None, ALU.mult))(),
                        incs=[(s_dle, 1)])
                RH_E = s_dle.n

                for h in range(NH):
                    pt, q64 = h // 2, (h % 2) * 64
                    tpK = (q64, q64)
                    tpM = (0, q64)
                    kTc = kc[q64:q64 + 64, pt, c0:c0 + 128]
                    qTc = qc[q64:q64 + 64, pt, c0:c0 + 128]
                    def mk_s1(h=h, kTc=kTc, qTc=qTc, q64=q64):
                        def f(e):
                            e.matmul(psA[:, 0:128], kTc, kTc, start=True, stop=True)
                            e.matmul(psA[:, 128:256], qTc, kTc, start=True, stop=True)
                            e.matmul(psA[:, 256:384], ones1_f[:], rowb[0:1, 1, h * 128:(h + 1) * 128], start=True, stop=False)
                            e.matmul(psA[:, 256:384], rowb[0:1, 0, h * 128:(h + 1) * 128], ones1_f[:], start=False, stop=True)
                            e.matmul(psA[:, 384:512], ones1_f[:], rowb[0:1, 1, h * 128:(h + 1) * 128], start=True, stop=False)
                            e.matmul(psA[:, 384:512], rowb[0:1, 2, h * 128:(h + 1) * 128], ones1_f[:], start=False, stop=True)
                            return e.matmul(psE[q64:q64 + 64, 384:512], ones1b[0:1, 0:64], rowbE[0:1, h * 128:(h + 1) * 128],
                                            start=True, stop=True, tile_position=(0, q64))
                        return f
                    plan.emit("tensor", mk_s1(),
                              waits=[(s_dle, RH_E), (s_row, ROWN), (s_dle, s_dle.n)], incs=[(s_dl, 1)])
                    S1 = s_dl.n
                    plan.emit("vector", (lambda: lambda e: e.tensor_tensor(Edm[:, 0, :], psA[:, 0:128], maskS[:], ALU.mult))(),
                              waits=[(s_dl, S1)], incs=[(s_dle, 1)])
                    plan.emit("vector", (lambda: lambda e: e.tensor_tensor(Edm[:, 1, :], psA[:, 128:256], maskI[:], ALU.mult))(),
                              incs=[(s_dle, 1)])
                    plan.emit("vector", (lambda q64=q64: lambda e: e.tensor_copy(DREP[q64:q64 + 64, :], psE[q64:q64 + 64, 384:512]))(),
                              incs=[(s_dle, 1)])
                    plan.emit("vector", (lambda: lambda e: e.tensor_scalar(Earg[:, 0, :], psA[:, 256:384], 60.0, None, ALU.min))(),
                              incs=[(s_dle, 1)])
                    plan.emit("vector", (lambda: lambda e: e.tensor_scalar(Earg[:, 1, :], psA[:, 384:512], 60.0, None, ALU.min))(),
                              incs=[(s_dle, 1)], post_drain=True)
                    plan.emit("scalar", (lambda: lambda e: e.activation(Earg[:, 0, :], Earg[:, 0, :], AF.Exp))(),
                              waits=[(s_dle, s_dle.n)], incs=[(s_dle, 1)], post_drain=True)
                    plan.emit("scalar", (lambda bi=ccol(float(np.log(0.125))): lambda e: e.activation(
                        Earg[:, 1, :], Earg[:, 1, :], AF.Exp, bias=bi))(),
                        incs=[(s_dle, 1)], post_drain=True)
                    plan.emit("vector", (lambda: lambda e: e.tensor_tensor(Abuf[:], Edm[:, 0, :], Earg[:, 0, :], ALU.mult))(),
                              waits=[(s_dle, s_dle.n)], incs=[(s_dle, 1)])
                    plan.emit("vector", (lambda: lambda e: e.tensor_tensor(Mbuf[:], Edm[:, 1, :], Earg[:, 1, :], ALU.mult))(),
                              incs=[(s_dle, 1)])
                    EV1 = s_dle.n
                    def mk_tr(h=h):
                        def f(e):
                            e.transpose(psB[:, 0:128], Abuf[:], id_sb[:])
                            return e.transpose(psB[:, 128:256], Mbuf[:], id_sb[:])
                        return f
                    if debug and s == 0 and n == DBGN:
                        plan.emit("vector", (lambda h=h: lambda e: e.tensor_copy(dbgA_sb[:, h, :], Abuf[:]))(),
                                  waits=[(s_dle, EV1)], incs=[(s_dle, 1)])
                        EV1 = s_dle.n
                    plan.emit("tensor", mk_tr(), waits=[(s_dle, EV1)], incs=[(s_dl, 1)])
                    plan.emit("vector", (lambda: lambda e: e.tensor_copy(MTb[:], psB[:, 128:256]))(),
                              waits=[(s_dl, s_dl.n)], incs=[(s_dle, 1)])
                    plan.emit("vector", (lambda: lambda e: e.tensor_scalar(negAT[:], psB[:, 0:128], -1.0, None, ALU.mult))(),
                              incs=[(s_dle, 1)])
                    plan.emit("vector", (lambda: lambda e: e.tensor_tensor(DTbuf[:], psB[:, 0:128], maskBDU[:], ALU.mult))(),
                              incs=[(s_dle, 1)])
                    plan.emit("vector", (lambda: lambda e: e.tensor_tensor(Dbuf[:], Abuf[:], maskBD[:], ALU.mult))(),
                              incs=[(s_dle, 1)], post_drain=True)
                    for lv in range(4):
                        cur, nxt = lv % 2, (lv + 1) % 2
                        def mk_pw(cur=cur, lv=lv):
                            def f(e):
                                pa = Dbuf[:] if lv == 0 else Pbuf[:, cur, 0:128]
                                na = DTbuf[:] if lv == 0 else Pbuf[:, cur, 128:256]
                                e.matmul(psC[:, 0:128], na, pa, start=True, stop=True)
                                return e.matmul(psC[:, 128:256], pa, na, start=True, stop=True)
                            return f
                        plan.emit("tensor", mk_pw(), waits=[(s_dle, s_dle.n)], incs=[(s_dl, 1)])
                        plan.emit("vector", (lambda nxt=nxt: lambda e: e.tensor_copy(Pbuf[:, nxt, :], psC[:, 0:256]))(),
                                  waits=[(s_dl, s_dl.n)], incs=[(s_dle, 1)])
                        if lv == 0:
                            plan.emit("vector", (lambda: lambda e: e.scalar_tensor_tensor(
                                Tbuf[:, 0, :], DTbuf[:], -1.0, id_sb[:], ALU.mult, ALU.add))(),
                                incs=[(s_dle, 1)], post_drain=True)
                        def mk_ch(cur=cur, nxt=nxt):
                            def f(e):
                                e.matmul(psD[:, 0:128], id_sb[:], Tbuf[:, cur, :], start=True, stop=False)
                                return e.matmul(psD[:, 0:128], Pbuf[:, nxt, 0:128], Tbuf[:, cur, :], start=False, stop=True)
                            return f
                        plan.emit("tensor", mk_ch(), waits=[(s_dle, s_dle.n)], incs=[(s_dl, 1)])
                        plan.emit("vector", (lambda nxt=nxt: lambda e: e.tensor_copy(Tbuf[:, nxt, :], psD[:, 0:128]))(),
                                  waits=[(s_dl, s_dl.n)], incs=[(s_dle, 1)])
                    fin = 4 % 2
                    def mk_binit(h=h):
                        def f(e):
                            return e.matmul(psE[:, 0:128], id_sb[:], rhsb[:, h, :], start=True, stop=False)
                        return f
                    plan.emit("tensor", mk_binit(), waits=[(s_dle, s_dle.n)], incs=[(s_dl, 1)])
                    for p in range(4):
                        r0 = p * 32
                        if p == 0:
                            plan.emit("vector", (lambda h=h: lambda e: e.tensor_copy(Bsb[0:32, :], rhsb[0:32, h, :]))(),
                                      waits=[(s_dle, s_dle.n), (s_dl, s_dl.n)], incs=[(s_dle, 1)], post_drain=True)
                        else:
                            plan.emit("vector", (lambda r0=r0: lambda e: e.tensor_copy(Bsb[r0:r0 + 32, :], psE[r0:r0 + 32, 0:128]))(),
                                      waits=[(s_dl, s_dl.n)], incs=[(s_dle, 1)], post_drain=True)
                        def mk_u(p=p, r0=r0, fin=fin):
                            def f(e):
                                return e.matmul(psD[r0:r0 + 32, 128:256], Tbuf[r0:r0 + 32, fin, r0:r0 + 32],
                                                Bsb[r0:r0 + 32, :], start=True, stop=True,
                                                tile_position=(r0, r0))
                            return f
                        plan.emit("tensor", mk_u(), waits=[(s_dle, s_dle.n)], incs=[(s_dl, 1)])
                        plan.emit("vector", (lambda r0=r0: lambda e: e.tensor_copy(TRH[r0:r0 + 32, :], psD[r0:r0 + 32, 128:256]))(),
                                  waits=[(s_dl, s_dl.n)], incs=[(s_dle, 1)], post_drain=True)
                        if p < 3:
                            def mk_corr(p=p, r0=r0):
                                def f(e):
                                    r = None
                                    for pd in range(p + 1, 4):
                                        c0 = pd * 32
                                        r = e.matmul(psE[c0:c0 + 32, 0:128], negAT[r0:r0 + 32, c0:c0 + 32],
                                                     TRH[r0:r0 + 32, :], start=False, stop=(pd == p + 1),
                                                     tile_position=(r0, c0))
                                    return r
                                return f
                            plan.emit("tensor", mk_corr(), waits=[(s_dle, s_dle.n)], incs=[(s_dl, 1)])
                    if debug and s == 0 and n == DBGN:
                        plan.emit("vector", (lambda h=h: lambda e: e.tensor_copy(dbgT_sb[:, h, :], TRH[:]))(),
                                  incs=[(s_dle, 1)])
                    def mk_sp(h=h, q64=q64):
                        def f(e):
                            e.matmul(psE[q64:q64 + 64, 128:192], TRH[:, 0:64], kepb[:, h, :], start=True, stop=True, tile_position=(0, q64))
                            e.matmul(psE[q64:q64 + 64, 192:256], kepb[:, h, :], TRH[:, 64:128], start=True, stop=True, tile_position=(0, q64))
                            return e.matmul(psE[q64:q64 + 64, 256:384], TRH[:, 0:64], MTb[:], start=True, stop=True, tile_position=(0, q64))
                        return f
                    plan.emit("tensor", mk_sp(), waits=[(s_dle, s_dle.n)], incs=[(s_dl, 1)])
                    SP1 = s_dl.n
                    plan.emit("vector", (lambda q64=q64: lambda e: e.tensor_scalar(
                        nMWT[q64:q64 + 64, :], psE[q64:q64 + 64, 256:384], -1.0, None, ALU.mult))(),
                        waits=[(s_dl, SP1)], incs=[(s_dle, 1)])
                    plan.emit("vector", (lambda q64=q64: lambda e: e.scalar_tensor_tensor(
                        PPT[q64:q64 + 64, :], if32[q64:q64 + 64, q64:q64 + 64], DREP[q64:q64 + 64, 127:128],
                        psE[q64:q64 + 64, 128:192], ALU.mult, ALU.subtract))(),
                        incs=[(s_dle, 1)])
                    plan.emit("vector", (lambda q64=q64: lambda e: e.tensor_copy(SQb[q64:q64 + 64, :], psE[q64:q64 + 64, 192:256]))(),
                              incs=[(s_dle, 1)])
                    EV2 = s_dle.n
                    def mk_o(h=h, pt=pt, q64=q64, qTc=qTc):
                        def f(e):
                            e.matmul(psD[q64:q64 + 64, 256:384], S_sb[q64:q64 + 64, pt, :], qTc, start=True, stop=True, tile_position=(q64, q64))
                            e.matmul(psD[q64:q64 + 64, 384:512], TRH[:, 64:128], MTb[:], start=True, stop=False, tile_position=(0, q64))
                            return e.matmul(psD[q64:q64 + 64, 384:512], S_sb[q64:q64 + 64, pt, :], nMWT[q64:q64 + 64, :],
                                            start=False, stop=True, tile_position=(q64, q64))
                        return f
                    plan.emit("tensor", mk_o(), waits=[(s_dle, EV2)], incs=[(s_dl, 1)])
                    O1 = s_dl.n
                    def mk_su(h=h, pt=pt, q64=q64):
                        def f(e):
                            e.matmul(psC[q64:q64 + 64, 256:320], PPT[q64:q64 + 64, :], S_sb[q64:q64 + 64, pt, :],
                                     start=True, stop=False, tile_position=(q64, q64))
                            return e.matmul(psC[q64:q64 + 64, 256:320], id_sb[q64:q64 + 64, q64:q64 + 64], SQb[q64:q64 + 64, :],
                                            start=False, stop=True, tile_position=(q64, q64))
                        return f
                    plan.emit("tensor", mk_su(), waits=[(s_dl, O1)], incs=[(s_dl, 1)])
                    SU1 = s_dl.n
                    plan.emit("vector", (lambda q64=q64: lambda e: e.tensor_tensor(
                        otmp[q64:q64 + 64, :], psD[q64:q64 + 64, 256:384], DREP[q64:q64 + 64, :], ALU.mult))(),
                        waits=[(s_dl, SU1)] + ([(s_ot, ot_mark)] if h == 0 else []), incs=[(s_dle, 1)])
                    plan.emit("vector", (lambda pt=pt, q64=q64: lambda e: e.scalar_tensor_tensor(
                        ostg[q64:q64 + 64, pt, :], otmp[q64:q64 + 64, :], 0.125, psD[q64:q64 + 64, 384:512],
                        ALU.mult, ALU.add))(),
                        incs=[(s_dle, 1)])
                    plan.emit("vector", (lambda pt=pt, q64=q64: lambda e: e.tensor_copy(
                        S_sb[q64:q64 + 64, pt, :], psC[q64:q64 + 64, 256:320]))(),
                        incs=[(s_dle, 1)])
                plan.emit("sync", (lambda gn=gn: lambda e: e.dma_start(
                    out=oT_hbm[:, :, gn * 128:(gn + 1) * 128], in_=ostg[:]))(),
                    waits=[(s_dle, s_dle.n)], incs=[(s_ot, 16)])

                ot_mark = s_ot.n
                if debug and s == 0 and n == DBGN:
                    plan.emit("gpsimd", lambda e: e.dma_start(out=dbg_outs["A_d"], in_=dbgA_sb[:]),
                              waits=[(s_dle, s_dle.n)], incs=[(s_dbgx, 16)])
                    plan.emit("gpsimd", lambda e: e.dma_start(out=dbg_outs["T_d"], in_=dbgT_sb[:]),
                              incs=[(s_dbgx, 16)])

            pin_scr = dscr(f"pin{s}_scr", (1, 16), F32)
            plan.emit("gpsimd", (lambda pin_scr=pin_scr: lambda e: e.dma_start(out=pin_scr, in_=rowb[0:1, 0, 0:16]))(),
                      waits=[(s_dle, s_dle.n)], incs=[(s_ot, 16)])
            dl.close()
            qkvs.close()

            if PC_STAGE < 6:
                pcs.close()
                continue
            cmb = ExitStack()
            WoT = cmb.enter_context(nc.sbuf_tensor(f"WoT{s}", [128, DB, D], BF16))
            WoutT = cmb.enter_context(nc.sbuf_tensor(f"WoutT{s}", [128, DB, D], BF16))
            otile = cmb.enter_context(nc.sbuf_tensor(f"otile{s}", [128, 2, DB, 512], BF16))
            rtile = cmb.enter_context(nc.sbuf_tensor(f"rtile{s}", [128, 2, DB, 512], BF16))
            x2t = cmb.enter_context(nc.sbuf_tensor(f"x2t{s}", [128, 2, D], F32))
            x1c = cmb.enter_context(nc.sbuf_tensor(f"x1c{s}", [128, 2, D], F32))
            psX = [cmb.enter_context(nc.psum_tensor(f"psX{s}{k}", [128, 1024], F32)) for k in range(2)]
            plan.emit("scalar", lambda e: e.dma_start(out=WoT[:], in_=wT_hbm[3]), incs=[(s_cl, 16)])
            plan.emit("scalar", lambda e: e.dma_start(out=WoutT[:], in_=wT_hbm[4]), incs=[(s_cl, 16)])
            cm_marks = []
            for tb in range(NTB):
                gt = t0g + tb
                par = tb % 2
                qpar = (tb // 4) % 2
                if tb % 4 == 0:
                    tile_gate = [(s_cmb, cm_marks[tb - 5])] if tb >= 8 else []
                    plan.emit("scalar", (lambda s2=s, tb=tb, qpar=qpar: lambda e: e.dma_start(
                        out=otile[:, qpar, :, :], in_=oT_hbm[:, :, (s2 * NTB + tb) * 128:(s2 * NTB + tb + 4) * 128]))(),
                        waits=[(s_ot, ot_mark)] + tile_gate, incs=[(s_cl, 16)])
                    plan.emit("scalar", (lambda s2=s, tb=tb, qpar=qpar: lambda e: e.dma_start(
                        out=rtile[:, qpar, :, :], in_=retT_hbm[:, :, (s2 * NTB + tb) * 128:(s2 * NTB + tb + 4) * 128]))(),
                        waits=[(s_ffrd, RETD_E)], incs=[(s_cl, 16)])
                CL_T = s_cl.n
                plan.emit("sync", (lambda gt=gt, par=par: lambda e: e.dma_start(
                    out=x1c[:, par, :], in_=x1_hbm[gt * 128:(gt + 1) * 128, :]))(),
                    waits=([(s_cmb, cm_marks[-2])] if len(cm_marks) >= 2 else []), incs=[(s_xc, 16)])
                XL_T = s_xc.n
                def mk_cmb(tb=tb, par=par, qpar=qpar):
                    def f(e):
                        t4 = (tb % 4) * 128
                        e.matmul(psX[par][:, 0:512], ones1_f[:], fwoutb[:, 0:512], start=True, stop=False)
                        e.matmul(psX[par][:, 512:1024], ones1_f[:], fwoutb[:, 512:1024], start=True, stop=False)
                        ins = None
                        for cb in range(DB):
                            e.matmul(psX[par][:, 0:512], otile[:, qpar, cb, t4:t4 + 128], WoT[:, cb, 0:512], start=False, stop=False)
                            e.matmul(psX[par][:, 512:1024], otile[:, qpar, cb, t4:t4 + 128], WoT[:, cb, 512:1024], start=False, stop=False)
                            e.matmul(psX[par][:, 0:512], rtile[:, qpar, cb, t4:t4 + 128], WoutT[:, cb, 0:512], start=False, stop=(cb == DB - 1))
                            ins = e.matmul(psX[par][:, 512:1024], rtile[:, qpar, cb, t4:t4 + 128], WoutT[:, cb, 512:1024], start=False, stop=(cb == DB - 1))
                        return ins
                    return f
                ev_gate = [(s_cmb, cm_marks[-2])] if len(cm_marks) >= 2 else []
                plan.emit("tensor", mk_cmb(),
                          waits=[(s_cl, CL_T), (s_pcld, PCLD), (s_ms, 2)] + ev_gate,
                          incs=[(s_cmb, 1)])
                plan.emit("vector", (lambda tb=tb, par=par: lambda e: e.tensor_tensor(
                    x2t[:, par, :], psX[par][:], x1c[:, par, :], ALU.add))(),
                    waits=[(s_cmb, s_cmb.n), (s_xc, XL_T)] + ([(s_out, s_out.n - 16)] if s_out.n >= 16 else []),
                    incs=[(s_cmb, 1)])
                plan.emit("sync", (lambda gt=gt, par=par: lambda e: e.dma_start(
                    out=out[gt * 128:(gt + 1) * 128, :], in_=x2t[:, par, :]))(),
                    waits=[(s_cmb, s_cmb.n)], incs=[(s_out, 16)])
                cm_marks.append(s_cmb.n)
            cmb.close()
            pcs.close()
        pcc.close()
        fin_waits = [(s_wstd, s_wstd.n), (s_l1e, s_l1e.n) if PC_STAGE >= 2 else (s_wstd, s_wstd.n)]
        for sm in [s_ffrd, s_row, s_ot, s_out, s_pje, s_cv, s_n2, s_stt, s_dle, s_gz, s_cl, s_xc, s_cmb]:
            fin_waits.append((sm, sm.n))
        plan.emit("sync", lambda e: e.nop(), waits=fin_waits)
        pin_sb = st.enter_context(nc.sbuf_tensor("pin_sb", [1, 128], F32))
        plan.emit("gpsimd", lambda e: e.memset(pin_sb[:], 0.0), waits=list(fin_waits))

    if "pc" not in phases:
        zeros = st.enter_context(nc.sbuf_tensor("zeros_ph", [128, D], F32))
        s_z = plan.sem()
        s_fin = plan.sem()
        plan.emit("vector", lambda e: e.memset(zeros[:], 0.0), incs=[(s_z, 1)])
        for t in range(NT):
            plan.emit("sync", (lambda t=t: lambda e: e.dma_start(out=out[t * 128:(t + 1) * 128, :], in_=zeros[:]))(),
                      waits=[(s_z, 1)], incs=[(s_fin, 16)])
        plan.emit("sync", lambda e: e.nop(),
                  waits=[(s_fin, 16 * NT), (s_xntd, 16 * (NT // 4))]
                        + (pa_done if "pa" in phases else [])
                        + (pb_done if "pb" in phases else []))

    with nc.Block() as block:
        block.tensor(lambda e: plan.replay(e, "tensor"))
        block.vector(lambda e: plan.replay(e, "vector"))
        block.scalar(lambda e: plan.replay(e, "scalar"))
        block.sync(lambda e: plan.replay(e, "sync"))
        block.gpsimd(lambda e: plan.replay(e, "gpsimd"))

    st.close()
    return nc, di, dbg_outs


_IN_KEYMAP = {
    "rW": "L0_ffn_router_W", "rb": "L0_ffn_router_b",
    "sW1": "L0_ffn_sW1", "sb1": "L0_ffn_sb1", "sW2": "L0_ffn_sW2", "sb2": "L0_ffn_sb2",
    "eW1": "L0_ffn_eW1", "eb1": "L0_ffn_eb1", "eW2": "L0_ffn_eW2", "eb2": "L0_ffn_eb2",
    "aWq": "L1_attn_Wq", "aWk": "L1_attn_Wk", "aWv": "L1_attn_Wv", "aWo": "L1_attn_Wo",
    "aWg": "L1_attn_Wg", "aWb": "L1_attn_Wb",
    "cqw": "L1_attn_cq_w", "cqb": "L1_attn_cq_b", "ckw": "L1_attn_ck_w", "ckb": "L1_attn_ck_b",
    "cvw": "L1_attn_cv_w", "cvb": "L1_attn_cv_b",
    "fWq": "L1_ffn_Wq", "frouters": "L1_ffn_routers", "fleafv": "L1_ffn_leaf_values",
    "fWout": "L1_ffn_Wout_w", "fWoutb": "L1_ffn_Wout_b",
}


def kernel(**inputs):
    x_full = np.ascontiguousarray(np.asarray(inputs["L0_attn_x"], dtype=np.float32))
    B = x_full.shape[0]
    nc, di, dbg = build_nc()
    base = {k: np.ascontiguousarray(np.asarray(inputs[v], dtype=np.float32))
            for k, v in _IN_KEYMAP.items()}
    in_maps = []
    for b in range(B):
        m = dict(base)
        m["x"] = np.ascontiguousarray(x_full[b])
        in_maps.append(m)
    res = run_bass_kernel_spmd(nc, in_maps, core_ids=list(range(B)))
    return np.stack([r["out"] for r in res.results], axis=0)


# version 03
# speedup vs baseline: 1.0155x; 1.0063x over previous
import numpy as np
import ml_dtypes
from contextlib import ExitStack

import concourse.bass as bass
import concourse.mybir as mybir
from concourse.bass_utils import run_bass_kernel_spmd

F32 = mybir.dt.float32
BF16 = mybir.dt.bfloat16
AF = mybir.ActivationFunctionType
ALU = mybir.AluOpType
AX = mybir.AxisListType

T = 4096
D = 1024
NT = T // 128
DB = D // 128
EPS = 1e-5

ENGS = ("tensor", "vector", "scalar", "sync", "gpsimd")


class _Sem:
    __slots__ = ("h", "n", "name")

    def __init__(self, h, name):
        self.h, self.n, self.name = h, 0, name


class Plan:
    def __init__(self, nc, stack):
        self.nc = nc
        self.ops = {e: [] for e in ENGS}
        self.pending = {e: [] for e in ENGS}
        self._stack = stack
        self._semi = 0

    def sem(self, name=None):
        name = name or f"s{self._semi}"
        self._semi += 1
        h = self._stack.enter_context(self.nc.semaphore(name))
        return _Sem(h, name)

    def emit(self, eng, fn, waits=(), incs=(), post_drain=False):
        w = list(self.pending[eng]) + list(waits)
        self.pending[eng] = []
        for s, amt in incs:
            s.n += amt
        self.ops[eng].append((fn, w, list(incs), post_drain))

    def gate(self, conds, engs=ENGS):
        for e in engs:
            self.pending[e].extend(conds)

    def replay(self, eng_obj, name):
        hi = {}
        for fn, waits, incs, post_drain in self.ops[name]:
            for s, v in waits:
                if hi.get(s.name, -1) < v:
                    eng_obj.wait_ge(s.h, v)
                    hi[s.name] = v
            instr = fn(eng_obj)
            if post_drain:
                instr = eng_obj.drain()
            for s, amt in incs:
                assert instr is not None, f"inc with no instr on {name}"
                instr.then_inc(s.h, amt)


def _ap(t):
    return t.ap() if hasattr(t, "ap") else t


PC_STAGE = 9
DBGN = 0

def build_nc(debug=False, phases=("p0", "pa", "pb", "pc")):
    nc = bass.Bass("TRN2", target_bir_lowering=False)
    st = ExitStack()

    di = {}
    def din(name, shape):
        di[name] = nc.dram_tensor(name, list(shape), F32, kind="ExternalInput").ap()
    din("x", (T, D))
    din("rW", (D, 15)); din("rb", (15,))
    din("sW1", (D, D)); din("sb1", (D,)); din("sW2", (D, D)); din("sb2", (D,))
    din("eW1", (15, D, D)); din("eb1", (15, D)); din("eW2", (15, D, D)); din("eb2", (15, D))
    din("aWq", (D, D)); din("aWk", (D, D)); din("aWv", (D, D)); din("aWo", (D, D))
    din("aWg", (16, D)); din("aWb", (16, D))
    din("cqw", (D, 1, 4)); din("cqb", (D,)); din("ckw", (D, 1, 4)); din("ckb", (D,))
    din("cvw", (D, 1, 4)); din("cvb", (D,))
    din("fWq", (D, D)); din("frouters", (255, D)); din("fleafv", (256, D))
    din("fWout", (D, D)); din("fWoutb", (D,))

    out = nc.dram_tensor("out", [T, D], F32, kind="ExternalOutput").ap()

    def dscr(name, shape, dtype=F32, dbg=False):
        kind = "ExternalOutput" if (debug and dbg) else "Internal"
        return nc.dram_tensor(name, list(shape), dtype, kind=kind).ap()

    xnT_hbm = dscr("xnT_hbm", (DB, 128, T), BF16, dbg=True)
    attn0_hbm = dscr("attn0_hbm", (T, D), dbg=True)
    x1_hbm = dscr("x1_hbm", (T, D), dbg=True)

    dbg_outs = {}
    if debug:
        dbg_outs["xn_nat_d"] = nc.dram_tensor("xn_nat_d", [128, NT, D], BF16, kind="ExternalOutput").ap()
        dbg_outs["logpT_d"] = nc.dram_tensor("logpT_d", [128, DB, T], BF16, kind="ExternalOutput").ap()

    ident_h = nc.inline_tensor(np.eye(128, dtype=ml_dtypes.bfloat16), name="ident128")

    plan = Plan(nc, st)

    id_sb = st.enter_context(nc.sbuf_tensor("id_sb", [128, 128], BF16))
    onecol = st.enter_context(nc.sbuf_tensor("onecol", [128, 1], BF16))
    ones1_f = st.enter_context(nc.sbuf_tensor("ones1_f", [1, 128], F32))

    consts = st.enter_context(nc.sbuf_tensor("consts", [128, 16], F32))

    s_const = plan.sem("const")
    s_ms = plan.sem("ms")
    plan.emit("sync", lambda e: e.dma_start(out=id_sb[:], in_=ident_h.ap()), incs=[(s_const, 16)])
    plan.emit("vector", lambda e: e.memset(onecol[:], 1.0), incs=[(s_ms, 1)])
    plan.emit("vector", lambda e: e.memset(ones1_f[:], 1.0), incs=[(s_ms, 1)])

    _cc = {}
    def ccol(v):
        if v not in _cc:
            idx = len(_cc)
            assert idx < 16
            _cc[v] = consts[:, idx:idx + 1]
            plan.emit("vector", (lambda ap=_cc[v], v=v: lambda e: e.memset(ap, float(v)))(),
                      incs=[(s_ms, 1)])
        return _cc[v]

    _cci = {}
    def ccol16(v, b):
        ccol(v)
        if v not in _cci:
            for i, (kv, ap) in enumerate(_cc.items()):
                _cci[kv] = i
        idx = _cci[v]
        return consts[b:b + 16, idx:idx + 1]

    l0res = ExitStack()
    xn_nat = l0res.enter_context(nc.sbuf_tensor("xn_nat", [128, NT, D], BF16))
    logpT = l0res.enter_context(nc.sbuf_tensor("logpT", [128, DB, T], BF16))
    p0 = ExitStack()
    x_tile = p0.enter_context(nc.sbuf_tensor("x_tile", [128, 2, D], F32))
    xc = p0.enter_context(nc.sbuf_tensor("xc", [128, D], F32))
    sq = p0.enter_context(nc.sbuf_tensor("sq", [128, D], F32))
    logp_nat = p0.enter_context(nc.sbuf_tensor("logp_nat", [128, D], BF16))
    stt = p0.enter_context(nc.sbuf_tensor("stt", [128, 2, 8], F32))
    xnT_stage = p0.enter_context(nc.sbuf_tensor("xnT_stage", [128, DB, 512], BF16))
    psT = p0.enter_context(nc.psum_tensor("psT", [128, 1024], BF16))
    psT2 = p0.enter_context(nc.psum_tensor("psT2", [128, 1024], BF16))

    s_ldx = plan.sem()
    s_xc = plan.sem()
    s_mu = plan.sem()
    s_ln = plan.sem()
    s_lh = plan.sem()
    s_xn = plan.sem()
    s_m = plan.sem()
    s_l1 = plan.sem()
    s_nl = plan.sem()
    s_lp = plan.sem()
    s_tr = plan.sem()
    s_tr2 = plan.sem()
    s_tev = plan.sem()
    s_pt = plan.sem()
    s_lsd = plan.sem()
    s_xnt = plan.sem()
    s_xntd = plan.sem()

    for t in range(NT):
        par = t % 2
        xt = x_tile[:, par, :]
        sa = stt[:, par, :]
        plan.emit("sync", (lambda xt=xt, t=t: lambda e: e.dma_start(out=xt, in_=di["x"][t * 128:(t + 1) * 128, :]))(),
                  waits=([(s_xc, t - 1)] if t >= 2 else ([(s_ldx, 16)] if t == 1 else [])), incs=[(s_ldx, 16)])
        plan.emit("vector", (lambda xt=xt, sa=sa: lambda e: e.reduce_sum(sa[:, 0:1], xt, axis=AX.X))(),
                  waits=[(s_ldx, 16 * (t + 1))] + ([(s_lp, t - 1)] if t >= 2 else []), post_drain=True)
        plan.emit("vector", (lambda sa=sa: lambda e: e.tensor_scalar(sa[:, 1:2], sa[:, 0:1], -1.0 / D, None, ALU.mult))(),
                  incs=[(s_mu, 1)])
        plan.emit("scalar", (lambda xt=xt, sa=sa: lambda e: e.activation(xc[:], xt, AF.Identity, bias=sa[:, 1:2]))(),
                  waits=[(s_mu, t + 1)], incs=[(s_xc, 1)])
        plan.emit("scalar", (lambda sa=sa: lambda e: e.activation(sq[:], xc[:], AF.Square, accum_out=sa[:, 2:3]))(), post_drain=True)
        plan.emit("scalar", (lambda sa=sa, sc=ccol(1.0 / D), bi=ccol(EPS): lambda e: e.activation(sa[:, 3:4], sa[:, 2:3], AF.Ln, scale=sc, bias=bi))(),
                  waits=[(s_ms, s_ms.n)], incs=[(s_ln, 1)])
        plan.emit("vector", (lambda sa=sa: lambda e: e.tensor_scalar(sa[:, 4:5], sa[:, 3:4], -0.5, None, ALU.mult))(),
                  waits=[(s_ln, t + 1)], incs=[(s_lh, 1)])
        plan.emit("scalar", (lambda sa=sa: lambda e: e.activation(sa[:, 5:6], sa[:, 4:5], AF.Exp))(),
                  waits=[(s_lh, t + 1)])
        plan.emit("scalar", (lambda t=t, sa=sa: lambda e: e.activation(xn_nat[:, t, :], xc[:], AF.Identity, scale=sa[:, 5:6]))(),
                  incs=[(s_xn, 1)])
        plan.emit("vector", (lambda t=t, sa=sa: lambda e: e.reduce_max(sa[:, 6:7], xn_nat[:, t, :], axis=AX.X))(),
                  waits=[(s_xn, t + 1)], post_drain=True)
        plan.emit("vector", (lambda sa=sa: lambda e: e.tensor_scalar(sa[:, 7:8], sa[:, 6:7], -1.0, None, ALU.mult))(),
                  incs=[(s_m, 1)])
        plan.emit("scalar", (lambda t=t, sa=sa: lambda e: e.activation(sq[:], xn_nat[:, t, :], AF.Exp, bias=sa[:, 7:8], accum_out=sa[:, 0:1]))(),
                  waits=[(s_m, t + 1)], post_drain=True)
        plan.emit("scalar", (lambda sa=sa: lambda e: e.activation(sa[:, 1:2], sa[:, 0:1], AF.Ln))(),
                  incs=[(s_l1, 1)])
        plan.emit("vector", (lambda sa=sa: lambda e: e.scalar_tensor_tensor(sa[:, 2:3], sa[:, 6:7], -1.0, sa[:, 1:2], ALU.mult, ALU.subtract))(),
                  waits=[(s_l1, t + 1)], incs=[(s_nl, 1)])
        plan.emit("scalar", (lambda t=t, sa=sa: lambda e: e.activation(logp_nat[:], xn_nat[:, t, :], AF.Identity, bias=sa[:, 2:3]))(),
                  waits=[(s_nl, t + 1)], incs=[(s_lp, 1)])
        def mk_tr(t=t):
            def f(e):
                ins = None
                for d in range(DB):
                    ins = e.transpose(psT[:, d * 128:(d + 1) * 128], logp_nat[:, d * 128:(d + 1) * 128], id_sb[:])
                return ins
            return f
        plan.emit("tensor", mk_tr(),
                  waits=[(s_lp, t + 1), (s_tev, t), (s_const, 16)], incs=[(s_tr, 1)])
        plan.emit("vector", (lambda t=t: lambda e: e.tensor_copy(
            logpT[:, :, t * 128:(t + 1) * 128],
            psT[:].rearrange("p (a b) -> p a b", a=DB)))(),
            waits=[(s_tr, t + 1)], incs=[(s_tev, 1)])
        def mk_tr2(t=t):
            def f(e):
                ins = None
                for d in range(DB):
                    ins = e.transpose(psT2[:, d * 128:(d + 1) * 128], xn_nat[:, t, d * 128:(d + 1) * 128], id_sb[:])
                return ins
            return f
        plan.emit("tensor", mk_tr2(), waits=[(s_xn, t + 1), (s_xnt, t)], incs=[(s_tr2, 1)])
        plan.emit("vector", (lambda t=t: lambda e: e.tensor_copy(
            xnT_stage[:, :, (t % 4) * 128:(t % 4 + 1) * 128],
            psT2[:].rearrange("p (a b) -> p a b", a=DB)))(),
            waits=[(s_tr2, t + 1)] + ([(s_xntd, 16 * (t // 4))] if t >= 4 else []),
            incs=[(s_xnt, 1)])
        if t % 4 == 3:
            plan.emit("sync", (lambda t=t: lambda e: e.dma_start(
                out=xnT_hbm[:, :, (t - 3) * 128:(t + 1) * 128].rearrange("a p c -> p a c"),
                in_=xnT_stage[:]))(),
                waits=[(s_xnt, t + 1)], incs=[(s_xntd, 16)])

    if debug:
        s_dbg = plan.sem()
        plan.emit("gpsimd", lambda e: e.dma_start(out=dbg_outs["xn_nat_d"], in_=xn_nat[:]),
                  waits=[(s_xn, NT)], incs=[(s_dbg, 16)])
        plan.emit("gpsimd", lambda e: e.dma_start(out=dbg_outs["logpT_d"], in_=logpT[:]),
                  waits=[(s_tev, NT)], incs=[(s_dbg, 16)])

    if "pa" in phases:
        p0.close()
        pa = ExitStack()
        pT = pa.enter_context(nc.sbuf_tensor("pT", [128, DB, T], BF16))
        expst = pa.enter_context(nc.sbuf_tensor("expst", [128, 2, 128], BF16))
        attn_sb = pa.enter_context(nc.sbuf_tensor("attn_sb", [128, 2, D], F32))
        rs_sb = pa.enter_context(nc.sbuf_tensor("rs_sb", [128, 2], F32))
        psS = [pa.enter_context(nc.psum_tensor(f"psS{k}", [128, 512], F32)) for k in range(2)]
        psO = [pa.enter_context(nc.psum_tensor(f"psO{k}", [128, 1024], F32)) for k in range(2)]
        psR = [pa.enter_context(nc.psum_tensor(f"psR{k}", [128, 512], F32)) for k in range(2)]

        for d in range(DB):
            plan.emit("scalar", (lambda d=d: lambda e: e.activation(pT[:, d, :], logpT[:, d, :], AF.Exp))(),
                      waits=[(s_tev, NT), (s_xnt, NT), (s_xntd, 16 * (NT // 4))], incs=[(s_pt, 1)])

        s_sc = plan.sem()
        s_ex = plan.sem()
        s_og = plan.sem()
        s_ev = plan.sem()
        s_st = plan.sem()

        for i in range(NT):
            par = i % 2
            for j in range(NT):
                q = i * NT + j
                sp = q % 2
                def mk_sc(i=i, j=j, sp=sp):
                    def f(e):
                        ins = None
                        for d in range(DB):
                            ins = e.matmul(psS[sp][:, 0:128],
                                           logpT[:, d, j * 128:(j + 1) * 128],
                                           pT[:, d, i * 128:(i + 1) * 128],
                                           start=(d == 0), stop=(d == DB - 1))
                        return ins
                    return f
                plan.emit("tensor", mk_sc(),
                          waits=[(s_pt, DB)] + ([(s_ex, q - 1)] if q >= 2 else []),
                          incs=[(s_sc, 1)])
                plan.emit("scalar", (lambda q=q, sp=sp: lambda e: e.activation(
                    expst[:, sp, :], psS[sp][:, 0:128], AF.Exp))(),
                    waits=[(s_sc, q + 1)] + ([(s_og, q - 1)] if q >= 2 else []),
                    incs=[(s_ex, 1)], post_drain=True)
                def mk_o(i=i, j=j, sp=sp, par=par):
                    def f(e):
                        e.matmul(psO[par][:, 0:512], expst[:, sp, :], xn_nat[:, j, 0:512],
                                 start=(j == 0), stop=(j == NT - 1))
                        e.matmul(psO[par][:, 512:1024], expst[:, sp, :], xn_nat[:, j, 512:1024],
                                 start=(j == 0), stop=(j == NT - 1))
                        return e.matmul(psR[par][:, 0:1], expst[:, sp, :], onecol[:],
                                        start=(j == 0), stop=(j == NT - 1))
                    return f
                plan.emit("tensor", mk_o(),
                          waits=[(s_ex, q + 1), (s_ms, 2)] + ([(s_ev, i - 1)] if (j == 0 and i >= 2) else []),
                          incs=[(s_og, 1)])
            plan.emit("vector", (lambda par=par: lambda e: e.reciprocal(rs_sb[:, par:par + 1], psR[par][:, 0:1]))(),
                      waits=[(s_og, NT * (i + 1))] + ([(s_st, 16 * (i - 1))] if i >= 2 else []), post_drain=True)
            plan.emit("vector", (lambda par=par: lambda e: e.tensor_scalar(
                attn_sb[:, par, :], psO[par][:], rs_sb[:, par:par + 1], None, ALU.mult))(),
                incs=[(s_ev, 1)])
            plan.emit("sync", (lambda i=i, par=par: lambda e: e.dma_start(
                out=attn0_hbm[i * 128:(i + 1) * 128, :], in_=attn_sb[:, par, :]))(),
                waits=[(s_ev, i + 1)], incs=[(s_st, 16)])

        pa_done = [(s_st, 16 * NT)]
        pa.close()
    else:
        p0.close()
    l0res.close()

    SLAB = 2048
    NTB = SLAB // 128
    NCHK = SLAB // 512
    NU = 16
    if "pb" in phases:
        pb = ExitStack()
        xnT_sl = pb.enter_context(nc.sbuf_tensor("xnT_sl", [128, DB, SLAB], BF16))
        acc = pb.enter_context(nc.sbuf_tensor("acc", [128, NTB, D], F32))
        wbuf = pb.enter_context(nc.sbuf_tensor("wbuf", [128, 2, 2, DB, D], BF16))
        hbuf = pb.enter_context(nc.sbuf_tensor("hbuf", [128, DB, 3, 512], BF16))
        eb1col = pb.enter_context(nc.sbuf_tensor("eb1col", [128, 2, DB], F32))
        eb2row = pb.enter_context(nc.sbuf_tensor("eb2row", [1, 2, D], F32))
        cw_sb = pb.enter_context(nc.sbuf_tensor("cw_sb", [128, NTB, 16], F32))
        rWf = pb.enter_context(nc.sbuf_tensor("rWf", [128, DB, 15], F32))
        rWsb = pb.enter_context(nc.sbuf_tensor("rWsb", [128, DB, 15], BF16))
        rbrow = pb.enter_context(nc.sbuf_tensor("rbrow", [1, 15], F32))
        pe_sb = pb.enter_context(nc.sbuf_tensor("pe_sb", [128, 15], F32))
        pe2_sb = pb.enter_context(nc.sbuf_tensor("pe2_sb", [128, 15], F32))
        z_dbg = pb.enter_context(nc.sbuf_tensor("z_dbg", [128, 15], F32))
        z2_dbg = pb.enter_context(nc.sbuf_tensor("z2_dbg", [128, 53], F32))
        rst = pb.enter_context(nc.sbuf_tensor("rst", [128, 8], F32))
        ai_t = pb.enter_context(nc.sbuf_tensor("ai_t", [128, 1, 2, D], F32))
        psH = [pb.enter_context(nc.psum_tensor(f"psH{k}", [128, 512], F32)) for k in range(2)]
        psY = [pb.enter_context(nc.psum_tensor(f"psY{k}", [128, 1024], F32)) for k in range(2)]
        psZ = pb.enter_context(nc.psum_tensor("psZ", [128, 512], F32))

        s_xsl = plan.sem()
        s_ai = plan.sem()
        s_aid = plan.sem()
        s_cw = plan.sem()
        s_rt = plan.sem()
        s_zmm = plan.sem()
        s_zrd = plan.sem()
        s_wW = plan.sem()
        s_wB = plan.sem()
        s_wR = plan.sem()
        s_m1 = plan.sem()
        s_hev = plan.sem()
        s_m2 = plan.sem()
        s_yev = plan.sem()
        s_x1 = plan.sem()
        s_u = plan.sem()
        s_hdump = plan.sem()
        s_accd = plan.sem()

        plan.gate([(s_st, 16 * NT)] if "pa" in phases else [])

        plan.emit("scalar", lambda e: e.dma_start(out=rWf[:], in_=di["rW"].rearrange("(d p) e -> p d e", p=128),
                                                  allow_slow_non_contiguous=True), incs=[(s_wR, 16)])
        plan.emit("scalar", lambda e: e.dma_start(out=rbrow[:], in_=di["rb"][None, :]), incs=[(s_wR, 16)])
        plan.emit("vector", lambda e: e.tensor_copy(rWsb[:], rWf[:]), waits=[(s_wR, 32)], incs=[(s_wR, 16)])

        BIGN = -1e30
        uglob = 0
        for s in range(2):
            t0 = s * NTB
            plan.emit("sync", (lambda s=s: lambda e: e.dma_start(
                out=xnT_sl[:], in_=xnT_hbm[:, :, s * SLAB:(s + 1) * SLAB].rearrange("a p c -> p a c")))(),
                waits=[(s_xntd, 16 * (NT // 4))] + ([(s_u16, 16)] if s == 1 else []),
                incs=[(s_xsl, 16)])
            for tb in range(NTB):
                gt = t0 + tb
                par = 0
                w_ai = [(s_ai, (s * NTB + tb))] if (s * NTB + tb) >= 1 else []
                plan.emit("sync", (lambda gt=gt, par=par: lambda e: e.dma_start(
                    out=ai_t[:, par, 0, :], in_=di["x"][gt * 128:(gt + 1) * 128, :]))(),
                    waits=w_ai, incs=[(s_aid, 16)])
                plan.emit("sync", (lambda gt=gt, par=par: lambda e: e.dma_start(
                    out=ai_t[:, par, 1, :], in_=attn0_hbm[gt * 128:(gt + 1) * 128, :]))(),
                    waits=w_ai, incs=[(s_aid, 16)])
                plan.emit("vector", (lambda tb=tb, par=par: lambda e: e.tensor_tensor(
                    acc[:, tb, :], ai_t[:, par, 0, :], ai_t[:, par, 1, :], ALU.add))(),
                    waits=[(s_aid, 32 * (s * NTB + tb + 1)), (s_x1, 16 * NTB * s)], incs=[(s_ai, 1)])
            for tb in range(NTB):
                gtb = s * NTB + tb
                def mk_z(tb=tb, s=s):
                    def f(e):
                        for d in range(DB):
                            e.matmul(psZ[:, 0:15], xnT_sl[:, d, tb * 128:(tb + 1) * 128],
                                     rWsb[:, d, :], start=(d == 0), stop=False)
                        return e.matmul(psZ[:, 0:15], ones1_f[:], rbrow[:], start=False, stop=True)
                    return f
                plan.emit("tensor", mk_z(),
                          waits=[(s_wR, 48), (s_xsl, 16 * (s + 1)), (s_ms, 2)] + ([(s_zrd, gtb)] if gtb >= 1 else []),
                          incs=[(s_zmm, 1)])
                cw = cw_sb[:, tb, 0:15]
                if debug == "deep" and gtb == 0:
                    plan.emit("vector", (lambda: lambda e: e.tensor_copy(z_dbg[:], psZ[:, 0:15]))(),
                              waits=[(s_zmm, 1)], post_drain=True)
                plan.emit("vector", (lambda: lambda e: e.reduce_max(rst[:, 0:1], psZ[:, 0:15], axis=AX.X))(),
                          waits=[(s_zmm, gtb + 1)], post_drain=True)
                plan.emit("vector", (lambda: lambda e: e.tensor_scalar(rst[:, 1:2], rst[:, 0:1], -1.0, None, ALU.mult))(),
                          incs=[(s_rt, 1)], post_drain=True)
                plan.emit("scalar", (lambda: lambda e: e.activation(pe_sb[:], psZ[:, 0:15], AF.Exp, bias=rst[:, 1:2], accum_out=rst[:, 2:3]))(),
                          waits=[(s_rt, gtb + 1), (s_cw, gtb)], post_drain=True, incs=[(s_zrd, 1)])
                plan.emit("vector", (lambda: lambda e: e.reciprocal(rst[:, 3:4], rst[:, 2:3]))(),
                          waits=[(s_zrd, gtb + 1)], post_drain=True)
                plan.emit("vector", (lambda: lambda e: e.tensor_scalar(pe_sb[:], pe_sb[:], rst[:, 3:4], None, ALU.mult))(), post_drain=True)
                if debug == "deep" and gtb == 0:
                    plan.emit("vector", (lambda: lambda e: e.tensor_copy(z2_dbg[:, 0:15], pe_sb[:]))(), post_drain=True)
                    plan.emit("vector", (lambda: lambda e: e.tensor_copy(z2_dbg[:, 15:23], rst[:]))(), post_drain=True)
                plan.emit("vector", (lambda: lambda e: e.reduce_max(rst[:, 4:5], pe_sb[:], axis=AX.X))(), post_drain=True)
                plan.emit("vector", (lambda cw=cw: lambda e: e.tensor_scalar(cw, pe_sb[:], rst[:, 4:5], None, ALU.is_ge))(), post_drain=True)
                plan.emit("vector", (lambda cw=cw: lambda e: e.scalar_tensor_tensor(cw, cw, BIGN, pe_sb[:], ALU.mult, ALU.add))(), post_drain=True)
                plan.emit("vector", (lambda cw=cw: lambda e: e.reduce_max(rst[:, 5:6], cw, axis=AX.X))(), post_drain=True)
                plan.emit("vector", (lambda cw=cw: lambda e: e.tensor_scalar(pe2_sb[:], cw, rst[:, 5:6], None, ALU.is_ge))(), post_drain=True)
                plan.emit("vector", (lambda cw=cw: lambda e: e.scalar_tensor_tensor(cw, pe2_sb[:], BIGN, cw, ALU.mult, ALU.add))(), post_drain=True)
                plan.emit("vector", (lambda cw=cw: lambda e: e.reduce_max(rst[:, 6:7], cw, axis=AX.X))(), post_drain=True)
                plan.emit("vector", (lambda cw=cw: lambda e: e.tensor_scalar(cw, pe_sb[:], rst[:, 6:7], None, ALU.is_ge))(), post_drain=True)
                plan.emit("vector", (lambda cw=cw: lambda e: e.tensor_tensor(cw, cw, pe_sb[:], ALU.mult))(),
                          incs=[(s_cw, 1)], post_drain=True)
                if debug == "deep" and gtb == 0:
                    plan.emit("vector", (lambda: lambda e: e.tensor_copy(z2_dbg[:, 23:38], pe_sb[:]))(), post_drain=True)
                    plan.emit("vector", (lambda cw=cw: lambda e: e.tensor_copy(z2_dbg[:, 38:53], cw))(), post_drain=True)
            if debug == "deep" and s == 0:
                plan.emit("sync", lambda e: e.dma_start(out=dbg_outs["acc_d"], in_=acc[:]),
                          waits=[(s_yev, 16)], incs=[(s_x1, 16), (s_accd, 16)])
            for u in range(NU):
                ug = uglob; uglob += 1
                par = ug % 2
                if u == 0:
                    w1src, w2src = di["sW1"], di["sW2"]
                    b1src, b2src = di["sb1"], di["sb2"]
                else:
                    w1src, w2src = di["eW1"][u - 1], di["eW2"][u - 1]
                    b1src, b2src = di["eb1"][u - 1], di["eb2"][u - 1]
                wwait = [(s_u, ug - 1)] if ug >= 2 else []
                plan.emit("gpsimd", (lambda w=w1src, par=par: lambda e: e.dma_start(
                    out=wbuf[:, par, 0, :, :], in_=w.rearrange("(d p) h -> p d h", p=128)))(),
                    waits=wwait, incs=[(s_wW, 16)])
                plan.emit("gpsimd", (lambda w=w2src, par=par: lambda e: e.dma_start(
                    out=wbuf[:, par, 1, :, :], in_=w.rearrange("(d p) h -> p d h", p=128)))(),
                    incs=[(s_wW, 16)])
                plan.emit("scalar", (lambda b=b1src, par=par: lambda e: e.dma_start(
                    out=eb1col[:, par, :], in_=b.rearrange("(d p) -> p d", p=128),
                    allow_slow_non_contiguous=True))(),
                    waits=wwait, incs=[(s_wB, 16)])
                plan.emit("scalar", (lambda b=b2src, par=par: lambda e: e.dma_start(
                    out=eb2row[:, par, :], in_=b[None, :]))(),
                    incs=[(s_wB, 16)])
                wready = 32 * (ug + 1)
                for ch in range(NCHK + 1):
                    cidx = ug * NCHK + min(ch, NCHK - 1)
                    if False:
                        pass
                    elif ch < NCHK:
                        cidx = ug * NCHK + ch
                        slot = cidx % 3
                        for hb in range(DB):
                            hidx = cidx * DB + hb
                            def mk_m1b(ch=ch, hb=hb, par=par, bank=hidx % 2, s=s):
                                def f(e):
                                    ins = None
                                    for d in range(DB):
                                        ins = e.matmul(psH[bank][:], wbuf[:, par, 0, d, hb * 128:(hb + 1) * 128],
                                                       xnT_sl[:, d, ch * 512:(ch + 1) * 512],
                                                       start=(d == 0), stop=(d == DB - 1))
                                    return ins
                                return f
                            plan.emit("tensor", mk_m1b(),
                                      waits=[(s_wW, wready), (s_xsl, 16 * (s + 1))] + ([(s_hev, hidx - 1)] if hidx >= 2 else []),
                                      incs=[(s_m1, 1)])
                            hdw = []
                            plan.emit("scalar", (lambda hb=hb, par=par, slot=slot, bank=hidx % 2: lambda e: e.activation(
                                hbuf[:, hb, slot, :], psH[bank][:], AF.Relu, bias=eb1col[:, par, hb:hb + 1]))(),
                                waits=[(s_m1, hidx + 1), (s_wB, wready)] + ([(s_m2, 4 * (cidx - 2))] if cidx >= 3 else []) + hdw,
                                incs=[(s_hev, 1)], post_drain=True)
                    if ch < NCHK:
                        chm = ch
                        cidx2 = ug * NCHK + chm
                        slot2 = cidx2 % 3
                        for tb4 in range(4):
                            tb = chm * 4 + tb4
                            yidx = cidx2 * 4 + tb4
                            bank = yidx % 2
                            def mk_m2(tb=tb, tb4=tb4, par=par, slot2=slot2, bank=bank):
                                def f(e):
                                    e.matmul(psY[bank][:, 0:512], ones1_f[:], eb2row[0:1, par, 0:512], start=True, stop=False)
                                    e.matmul(psY[bank][:, 512:1024], ones1_f[:], eb2row[0:1, par, 512:1024], start=True, stop=False)
                                    ins = None
                                    for hb in range(DB):
                                        lt = hbuf[:, hb, slot2, tb4 * 128:(tb4 + 1) * 128]
                                        e.matmul(psY[bank][:, 0:512], lt, wbuf[:, par, 1, hb, 0:512],
                                                 start=False, stop=(hb == DB - 1))
                                        ins = e.matmul(psY[bank][:, 512:1024], lt, wbuf[:, par, 1, hb, 512:1024],
                                                       start=False, stop=(hb == DB - 1))
                                    return ins
                                return f
                            plan.emit("tensor", mk_m2(),
                                      waits=[(s_hev, DB * (cidx2 + 1)), (s_ms, 2), (s_wB, wready)] + ([(s_yev, yidx - 1)] if yidx >= 2 else []),
                                      incs=[(s_m2, 1)])
                            accdw = [(s_accd, 16)] if (debug == "deep" and s == 0 and u == 1) else []
                            if u == 0:
                                plan.emit("vector", (lambda tb=tb, bank=bank: lambda e: e.tensor_tensor(
                                    acc[:, tb, :], acc[:, tb, :], psY[bank][:], ALU.add))(),
                                    waits=[(s_m2, yidx + 1), (s_ai, NTB * (s + 1))] + accdw,
                                    incs=[(s_yev, 1)])
                            else:
                                plan.emit("vector", (lambda tb=tb, bank=bank, u=u: lambda e: e.scalar_tensor_tensor(
                                    acc[:, tb, :], psY[bank][:], cw_sb[:, tb, u - 1:u], acc[:, tb, :], ALU.mult, ALU.add))(),
                                    waits=[(s_m2, yidx + 1), (s_ai, NTB * (s + 1)), (s_cw, NTB * (s + 1))] + accdw,
                                    incs=[(s_yev, 1)])
                plan.emit("vector", (lambda: lambda e: e.tensor_copy(rst[:, 0:1], rst[:, 0:1]))(),
                          waits=[(s_yev, 16 * (ug + 1))], incs=[(s_u, 1)])
            if s == 0:
                s_u16 = plan.sem()
                plan.emit("vector", (lambda: lambda e: e.tensor_copy(rst[:, 0:1], rst[:, 0:1]))(),
                          waits=[(s_yev, 16 * NU)], incs=[(s_u16, 16)])
            for tb in range(NTB):
                gt = t0 + tb
                plan.emit("sync", (lambda gt=gt, tb=tb: lambda e: e.dma_start(
                    out=x1_hbm[gt * 128:(gt + 1) * 128, :], in_=acc[:, tb, :]))(),
                    waits=[(s_yev, 16 * NU * (s + 1))], incs=[(s_x1, 16)])
        if debug == "deep":
            dbg_outs["z_d"] = nc.dram_tensor("z_d", [128, 15], F32, kind="ExternalOutput").ap()
            plan.emit("gpsimd", lambda e: e.dma_start(out=dbg_outs["z_d"], in_=z_dbg[:]),
                      waits=[(s_cw, NTB)], incs=[(s_x1, 16)])
            dbg_outs["z2_d"] = nc.dram_tensor("z2_d", [128, 53], F32, kind="ExternalOutput").ap()
            plan.emit("gpsimd", lambda e: e.dma_start(out=dbg_outs["z2_d"], in_=z2_dbg[:]),
                      waits=[(s_cw, NTB)], incs=[(s_x1, 16)])
            dbg_outs["cw_d"] = nc.dram_tensor("cw_d", [128, NTB, 16], F32, kind="ExternalOutput").ap()
            dbg_outs["acc_d"] = nc.dram_tensor("acc_d", [128, NTB, D], F32, kind="ExternalOutput").ap()
            plan.emit("gpsimd", lambda e: e.dma_start(out=dbg_outs["cw_d"], in_=cw_sb[:]),
                      waits=[(s_cw, NTB)], incs=[(s_x1, 16)])
        pb_done = [(s_x1, 16 * NT + (64 if debug == "deep" else 0))]
        pb.close()



    if "pc" in phases:
        KD = 64
        NH = 16
        C = 128
        CPS = SLAB // C

        wT_hbm = dscr("wT_hbm", (6, 128, DB, D), BF16)
        retT_hbm = dscr("retT_hbm", (128, DB, T), BF16, dbg=True)
        routersT_hbm = dscr("routersT_hbm", (128, DB, 256), BF16)
        oT_hbm = dscr("oT_hbm", (128, DB, T), BF16, dbg=True)
        rowscrF = dscr("rowscrF", (3, 16, SLAB), F32, dbg=True)

        maskS_h = nc.inline_tensor(np.tril(np.ones((128, 128), np.float32), -1), name="maskS")
        bd = np.zeros((128, 128), np.float32)
        for p_ in range(4):
            bd[p_*32:(p_+1)*32, p_*32:(p_+1)*32] = np.tril(np.ones((32, 32)), -1)
        maskBD_h = nc.inline_tensor(bd, name="maskBD")
        maskBDU_h = nc.inline_tensor(bd.T.copy(), name="maskBDU")
        maskI_h = nc.inline_tensor(np.tril(np.ones((128, 128), np.float32), 0), name="maskI")
        if32_h = nc.inline_tensor(np.eye(128, dtype=np.float32), name="if32")
        ones2_np = np.zeros((8, 128, 16), np.float32)
        for pt in range(8):
            for r in range(2):
                ones2_np[pt, r * 64:(r + 1) * 64, 2 * pt + r] = 1.0
        onesB_h = nc.inline_tensor(ones2_np.astype(ml_dtypes.bfloat16), name="onesB")
        mselnp = np.zeros((4, 128, 256), np.float32)
        for m in range(256):
            for level in range(8):
                node = 2 ** level - 1 + (m >> (8 - level))
                bit = (m >> (7 - level)) & 1
                row = node if bit else 256 + node
                mselnp[row // 128, row % 128, m] = 1.0
        msel_h = nc.inline_tensor(mselnp, name="msel")

        pcc = ExitStack()
        S_sb = pcc.enter_context(nc.sbuf_tensor("S_sb", [128, DB, KD], BF16))
        maskS = pcc.enter_context(nc.sbuf_tensor("maskS_sb", [128, 128], F32))
        maskI = pcc.enter_context(nc.sbuf_tensor("maskI_sb", [128, 128], F32))
        maskBD = pcc.enter_context(nc.sbuf_tensor("maskBD_sb", [128, 128], F32))
        maskBDU = pcc.enter_context(nc.sbuf_tensor("maskBDU_sb", [128, 128], F32))
        if32 = pcc.enter_context(nc.sbuf_tensor("if32_sb", [128, 128], F32))
        onesB = pcc.enter_context(nc.sbuf_tensor("onesB_sb", [128, 8, 16], BF16))
        WgT = pcc.enter_context(nc.sbuf_tensor("WgT", [128, DB, 16], BF16))
        WbT = pcc.enter_context(nc.sbuf_tensor("WbT", [128, DB, 16], BF16))
        ctap = pcc.enter_context(nc.sbuf_tensor("ctap", [128, 3, DB, 4], F32))
        cbia = pcc.enter_context(nc.sbuf_tensor("cbia", [128, 3, DB], F32))
        fwoutb = pcc.enter_context(nc.sbuf_tensor("fwoutb", [1, D], F32))
        halo = pcc.enter_context(nc.sbuf_tensor("halo", [128, 3, DB, 3], BF16))
        ones1b = pcc.enter_context(nc.sbuf_tensor("ones1b", [1, 128], BF16))

        s_pcld = plan.sem()
        s_rowE = plan.sem()
        s_xc = plan.sem()
        s_wn = plan.sem()
        s_wst = plan.sem()
        s_wstd = plan.sem()
        s_out = plan.sem()

        plan.gate([(s_x1, 16 * NT)])

        plan.emit("sync", lambda e: e.dma_start(out=maskS[:], in_=maskS_h.ap()), incs=[(s_pcld, 16)])
        plan.emit("sync", lambda e: e.dma_start(out=maskI[:], in_=maskI_h.ap()), incs=[(s_pcld, 16)])
        plan.emit("sync", lambda e: e.dma_start(out=maskBD[:], in_=maskBD_h.ap()), incs=[(s_pcld, 16)])
        plan.emit("sync", lambda e: e.dma_start(out=maskBDU[:], in_=maskBDU_h.ap()), incs=[(s_pcld, 16)])
        plan.emit("sync", lambda e: e.dma_start(out=if32[:], in_=if32_h.ap()), incs=[(s_pcld, 16)])
        plan.emit("sync", lambda e: e.dma_start(out=onesB[:], in_=onesB_h.ap().rearrange("a p c -> p a c")), incs=[(s_pcld, 16)])
        for i, (wk, bk) in enumerate([("cqw", "cqb"), ("ckw", "ckb"), ("cvw", "cvb")]):
            plan.emit("sync", (lambda i=i, wk=wk: lambda e: e.dma_start(
                out=ctap[:, i, :, :], in_=di[wk].rearrange("(a p) one k -> p a (one k)", p=128),
                allow_slow_non_contiguous=True))(), incs=[(s_pcld, 16)])
            plan.emit("sync", (lambda i=i, bk=bk: lambda e: e.dma_start(
                out=cbia[:, i, :], in_=di[bk].rearrange("(a p) -> p a", p=128),
                allow_slow_non_contiguous=True))(), incs=[(s_pcld, 16)])
        plan.emit("sync", lambda e: e.dma_start(out=fwoutb[:], in_=di["fWoutb"][None, :]), incs=[(s_pcld, 16)])
        plan.emit("vector", lambda e: e.memset(S_sb[:], 0.0), incs=[(s_pcld, 1)])
        plan.emit("vector", lambda e: e.memset(halo[:], 0.0), incs=[(s_pcld, 1)])
        plan.emit("vector", lambda e: e.memset(ones1b[:], 1.0), incs=[(s_pcld, 1)])
        PCLD = s_pcld.n

        wstg = ExitStack()
        wnat = wstg.enter_context(nc.sbuf_tensor("wnat", [128, DB, D], BF16))
        wtv = wstg.enter_context(nc.sbuf_tensor("wtv", [128, DB, D], BF16))
        wgnat = wstg.enter_context(nc.sbuf_tensor("wgnat", [16, D], BF16))
        psW = wstg.enter_context(nc.psum_tensor("psW", [128, 1024], BF16))
        wsrcs = [di["aWq"], di["aWk"], di["aWv"], di["aWo"], di["fWout"], di["fWq"]]
        for wi, wsrc in enumerate(wsrcs):
            plan.emit("gpsimd", (lambda w=wsrc: lambda e: e.dma_start(
                out=wnat[:], in_=w.rearrange("(a p) d -> p a d", p=128)))(),
                waits=([(s_wst, wi * DB)] if wi >= 1 else []), incs=[(s_wn, 16)])
            for db in range(DB):
                def mk_wt(wi=wi, db=db):
                    def f(e):
                        ins = None
                        for a in range(DB):
                            ins = e.transpose(psW[:, a * 128:(a + 1) * 128], wnat[:, a, db * 128:(db + 1) * 128], id_sb[:])
                        return ins
                    return f
                plan.emit("tensor", mk_wt(),
                          waits=[(s_wn, 16 * (wi + 1)), (s_const, 16)] + ([(s_wst, s_wst.n)] if (wi, db) != (0, 0) else []),
                          incs=[(s_tr, 1)])
                plan.emit("vector", (lambda db=db: lambda e: e.tensor_copy(wtv[:, db, :], psW[:]))(),
                          waits=[(s_tr, s_tr.n)], incs=[(s_wst, 1)])
            plan.emit("sync", (lambda wi=wi: lambda e: e.dma_start(out=wT_hbm[wi], in_=wtv[:]))(),
                      waits=[(s_wst, s_wst.n)], incs=[(s_wstd, 16)])
        for gi, (gsrc, gdst) in enumerate([(di["aWg"], WgT), (di["aWb"], WbT)]):
            plan.emit("gpsimd", (lambda g=gsrc: lambda e: e.dma_start(out=wgnat[:], in_=g))(),
                      waits=[(s_wst, s_wst.n)], incs=[(s_wn, 16)])
            def mk_gt(gi=gi):
                def f(e):
                    ins = None
                    for a in range(DB):
                        ins = e.transpose(psW[:, a * 16:(a + 1) * 16], wgnat[:, a * 128:(a + 1) * 128], id_sb[0:16, 0:16])
                    return ins
                return f
            plan.emit("tensor", mk_gt(), waits=[(s_wn, 16 * (7 + gi)), (s_wst, s_wst.n)], incs=[(s_tr, 1)])
            plan.emit("vector", (lambda gdst=gdst: lambda e: e.tensor_copy(
                gdst[:], psW[:, 0:128].rearrange("p (a b) -> p a b", a=DB)))(),
                waits=[(s_tr, s_tr.n)], incs=[(s_wst, 1)])
        plan.emit("vector", lambda e: e.memset(wnat[:, 1, :], 0.0), waits=[(s_wst, s_wst.n)], incs=[(s_wst, 1)])
        plan.emit("gpsimd", lambda e: e.dma_start(out=wnat[:, 0, :], in_=di["frouters"][0:128, :]),
                  waits=[(s_wst, s_wst.n)], incs=[(s_wn, 16)])
        plan.emit("gpsimd", lambda e: e.dma_start(out=wnat[0:127, 1, :], in_=di["frouters"][128:255, :]),
                  incs=[(s_wn, 16)])
        for db in range(DB):
            def mk_rt(db=db):
                def f(e):
                    ins = None
                    for a in range(2):
                        ins = e.transpose(psW[:, a * 128:(a + 1) * 128], wnat[:, a, db * 128:(db + 1) * 128], id_sb[:])
                    return ins
                return f
            plan.emit("tensor", mk_rt(), waits=[(s_wn, 16 * 10), (s_wst, s_wst.n)], incs=[(s_tr, 1)])
            plan.emit("vector", (lambda db=db: lambda e: e.tensor_copy(wtv[:, db, 0:256], psW[:, 0:256]))(),
                      waits=[(s_tr, s_tr.n)], incs=[(s_wst, 1)])
        plan.emit("sync", lambda e: e.dma_start(out=routersT_hbm, in_=wtv[:, :, 0:256]),
                  waits=[(s_wst, s_wst.n)], incs=[(s_wstd, 16)])
        WSTG_E = s_wst.n
        wstg.close()

        s_ffz = plan.sem()
        s_ffe = plan.sem()
        s_ffr = plan.sem()
        s_ffrd = plan.sem()
        s_q1 = plan.sem()
        s_wld2 = plan.sem()
        s_pj = plan.sem()
        s_pje = plan.sem()
        s_cv = plan.sem()
        s_gz = plan.sem()
        s_n2 = plan.sem()
        s_stt = plan.sem()
        s_row = plan.sem()
        s_dl = plan.sem()
        s_dle = plan.sem()
        s_ot = plan.sem()
        s_cmb = plan.sem()
        s_cl = plan.sem()
        s_dbgx = plan.sem()

        for s in range(2):
            t0g = s * NTB
            pcs = ExitStack()
            statsA = pcs.enter_context(nc.sbuf_tensor(f"statsA{s}", [128, SLAB], F32))
            statsB = pcs.enter_context(nc.sbuf_tensor(f"statsB{s}", [128, SLAB], F32))
            statsC = pcs.enter_context(nc.sbuf_tensor(f"statsC{s}", [128, SLAB], F32))
            SL = {"g": (statsA, 0), "e1": (statsA, 32), "lnrho": (statsA, 64), "zb": (statsA, 96),
                  "u": (statsB, 0), "cB": (statsB, 32), "cprev": (statsB, 64), "rn": (statsB, 96),
                  "zg": (statsC, 0), "eps": (statsC, 32), "c": (statsC, 64), "l": (statsC, 96),
                  "lnb": (statsB, 64), "ones": (statsB, 0), "edr": (statsB, 0), "ecr": (statsC, 0)}
            def sl(name, c0=0, c1=SLAB):
                t, b = SL[name]
                return t[b:b + 16, c0:c1]

            sub1 = ExitStack()
            xn1T = sub1.enter_context(nc.sbuf_tensor(f"xn1T{s}", [128, DB, SLAB], BF16, side="right"))

            if PC_STAGE < 2:
                sub1.close()
                pcs.close()
                continue
            ln1 = ExitStack()
            x1t = ln1.enter_context(nc.sbuf_tensor(f"x1t{s}", [128, 2, D], F32))
            xc1 = ln1.enter_context(nc.sbuf_tensor(f"xc1{s}", [128, D], F32))
            sq1 = ln1.enter_context(nc.sbuf_tensor(f"sq1{s}", [128, D], F32))
            xn1n = ln1.enter_context(nc.sbuf_tensor(f"xn1n{s}", [128, D], BF16))
            st1 = ln1.enter_context(nc.sbuf_tensor(f"st1{s}", [128, 2, 8], F32))
            psT1 = ln1.enter_context(nc.psum_tensor(f"psT1{s}", [128, 1024], BF16))
            s_l1s = plan.sem()
            s_l1x = plan.sem()
            s_l1n = plan.sem()
            s_l1t = plan.sem()
            s_l1e = plan.sem()
            for tb in range(NTB):
                gt = t0g + tb
                par = tb % 2
                xt = x1t[:, par, :]
                sa = st1[:, par, :]
                plan.emit("sync", (lambda xt=xt, gt=gt: lambda e: e.dma_start(out=xt, in_=x1_hbm[gt * 128:(gt + 1) * 128, :]))(),
                          waits=[(s_x1, 16 * NT)] + ([(s_l1n, tb - 1)] if tb >= 2 else []), incs=[(s_l1x, 16)])
                plan.emit("vector", (lambda xt=xt, sa=sa: lambda e: e.reduce_sum(sa[:, 0:1], xt, axis=AX.X))(),
                          waits=[(s_l1x, s_l1x.n)] + ([(s_l1n, tb - 1)] if tb >= 2 else []), post_drain=True)
                plan.emit("vector", (lambda sa=sa: lambda e: e.tensor_scalar(sa[:, 1:2], sa[:, 0:1], -1.0 / D, None, ALU.mult))(),
                          incs=[(s_l1s, 1)])
                plan.emit("scalar", (lambda xt=xt, sa=sa: lambda e: e.activation(xc1[:], xt, AF.Identity, bias=sa[:, 1:2]))(),
                          waits=[(s_l1s, s_l1s.n)])
                plan.emit("scalar", (lambda sa=sa: lambda e: e.activation(sq1[:], xc1[:], AF.Square, accum_out=sa[:, 2:3]))(), post_drain=True)
                plan.emit("scalar", (lambda sa=sa, sc=ccol(1.0 / D), bi=ccol(EPS): lambda e: e.activation(sa[:, 3:4], sa[:, 2:3], AF.Ln, scale=sc, bias=bi))())
                plan.emit("scalar", (lambda sa=sa, sc=ccol(-0.5): lambda e: e.activation(sa[:, 4:5], sa[:, 3:4], AF.Exp, scale=sc))(), post_drain=True)
                plan.emit("scalar", (lambda sa=sa: lambda e: e.activation(xn1n[:], xc1[:], AF.Identity, scale=sa[:, 4:5]))(),
                          waits=([(s_l1t, s_l1t.n)] if s_l1t.n >= 1 else []), incs=[(s_l1n, 1)])
                def mk_t1(tb=tb):
                    def f(e):
                        ins = None
                        for d in range(DB):
                            ins = e.transpose(psT1[:, d * 128:(d + 1) * 128], xn1n[:, d * 128:(d + 1) * 128], id_sb[:])
                        return ins
                    return f
                plan.emit("tensor", mk_t1(), waits=[(s_l1n, s_l1n.n), (s_l1e, s_l1e.n)], incs=[(s_l1t, 1)])
                plan.emit("vector", (lambda tb=tb: lambda e: e.tensor_copy(
                    xn1T[:, :, tb * 128:(tb + 1) * 128], psT1[:].rearrange("p (a b) -> p a b", a=DB)))(),
                    waits=[(s_l1t, s_l1t.n)], incs=[(s_l1e, 1)])
            LN1_E = s_l1e.n
            ln1.close()

            if PC_STAGE < 3:
                sub1.close()
                pcs.close()
                continue
            ffn = ExitStack()
            zT = ffn.enter_context(nc.sbuf_tensor(f"zT{s}", [128, 2, SLAB], F32))
            LG = ffn.enter_context(nc.sbuf_tensor(f"LG{s}", [128, 4, SLAB], F32))
            lpT = ffn.enter_context(nc.sbuf_tensor(f"lpT{s}", [128, 2, SLAB], BF16))
            q1c = ffn.enter_context(nc.sbuf_tensor(f"q1c{s}", [128, DB, 512], BF16))
            rstg = ffn.enter_context(nc.sbuf_tensor(f"rstg{s}", [128, DB, 512], BF16))
            wb1 = ffn.enter_context(nc.sbuf_tensor(f"wb1{s}", [128, DB, D], BF16))
            msel = ffn.enter_context(nc.sbuf_tensor(f"msel{s}", [128, 4, 256], F32))
            leafv = ffn.enter_context(nc.sbuf_tensor(f"leafv{s}", [128, 2, D], BF16))
            routersT = ffn.enter_context(nc.sbuf_tensor(f"routersT{s}", [128, DB, 256], BF16))
            psq = ffn.enter_context(nc.psum_tensor(f"psq{s}", [128, 512], F32))
            psz = [ffn.enter_context(nc.psum_tensor(f"psz{s}{k}", [128, 512], F32)) for k in range(2)]

            plan.emit("sync", lambda e: e.dma_start(out=msel[:], in_=msel_h.ap().rearrange("a p c -> p a c")),
                      incs=[(s_pcld, 16)])
            plan.emit("gpsimd", lambda e: e.dma_start(out=leafv[:], in_=di["fleafv"].rearrange("(a p) d -> p a d", p=128)),
                      incs=[(s_pcld, 16)])
            plan.emit("sync", lambda e: e.dma_start(out=routersT[:], in_=routersT_hbm),
                      waits=[(s_wstd, 16 * 7)], incs=[(s_pcld, 16)])
            PCLD2 = s_pcld.n
            plan.emit("sync", lambda e: e.dma_start(out=wb1[:], in_=wT_hbm[5]),
                      waits=[(s_wstd, 16 * 6)], incs=[(s_wld2, 16)])
            W1L = s_wld2.n
            q1_marks = []
            zev_marks = []
            for ch in range(4):
                for eb in range(DB):
                    def mk_q1e(ch=ch, eb=eb):
                        def f(e):
                            ins = None
                            for d in range(DB):
                                ins = e.matmul(psq[:], wb1[:, d, eb * 128:(eb + 1) * 128],
                                               xn1T[:, d, ch * 512:(ch + 1) * 512],
                                               start=(d == 0), stop=(d == DB - 1))
                            return ins
                        return f
                    plan.emit("tensor", mk_q1e(),
                              waits=[(s_wld2, W1L), (s_l1e, LN1_E)] + ([(s_q1, q1_marks[-1])] if q1_marks else []),
                              incs=[(s_ffz, 1)])
                    plan.emit("vector", (lambda eb=eb: lambda e: e.tensor_copy(q1c[:, eb, :], psq[:]))(),
                              waits=[(s_ffz, s_ffz.n)], incs=[(s_q1, 1)])
                    q1_marks.append(s_q1.n)
                for nb in range(2):
                    def mk_z1(ch=ch, nb=nb):
                        def f(e):
                            ins = None
                            for eb in range(DB):
                                ins = e.matmul(psz[nb][:], routersT[:, eb, nb * 128:(nb + 1) * 128],
                                               q1c[:, eb, :], start=(eb == 0), stop=(eb == DB - 1))
                            return ins
                        return f
                    plan.emit("tensor", mk_z1(),
                              waits=[(s_q1, s_q1.n), (s_pcld, PCLD2)] + ([(s_ffe, zev_marks[-2])] if len(zev_marks) >= 2 else []),
                              incs=[(s_ffz, 1)])
                    plan.emit("vector", (lambda ch=ch, nb=nb: lambda e: e.tensor_copy(
                        zT[:, nb, ch * 512:(ch + 1) * 512], psz[nb][:]))(),
                        waits=[(s_ffz, s_ffz.n)], incs=[(s_ffe, 1)])
                    zev_marks.append(s_ffe.n)
            ZT_E = s_ffe.n
            for nb in range(2):
                plan.emit("scalar", (lambda nb=nb: lambda e: e.activation(LG[:, 2 + nb, :], zT[:, nb, :], AF.Abs))(),
                          waits=[(s_ffe, ZT_E)])
                plan.emit("scalar", (lambda nb=nb, sc=ccol(-1.0): lambda e: e.activation(LG[:, 2 + nb, :], LG[:, 2 + nb, :], AF.Exp, scale=sc))())
                plan.emit("scalar", (lambda nb=nb, bi=ccol(1.0): lambda e: e.activation(LG[:, 2 + nb, :], LG[:, 2 + nb, :], AF.Ln, bias=bi))())
                plan.emit("scalar", (lambda nb=nb, sc=ccol(-1.0): lambda e: e.activation(LG[:, nb, :], zT[:, nb, :], AF.Relu, scale=sc))(),
                          incs=[(s_ffe, 1)])
                plan.emit("vector", (lambda nb=nb: lambda e: e.scalar_tensor_tensor(
                    LG[:, nb, :], LG[:, nb, :], -1.0, LG[:, 2 + nb, :], ALU.mult, ALU.subtract))(),
                    waits=[(s_ffe, s_ffe.n)], incs=[(s_ffe, 1)])
                plan.emit("scalar", (lambda nb=nb: lambda e: e.activation(zT[:, nb, :], zT[:, nb, :], AF.Relu))(),
                          waits=[(s_ffe, s_ffe.n)], incs=[(s_ffe, 1)])
                plan.emit("vector", (lambda nb=nb: lambda e: e.scalar_tensor_tensor(
                    LG[:, 2 + nb, :], zT[:, nb, :], -1.0, LG[:, 2 + nb, :], ALU.mult, ALU.subtract))(),
                    waits=[(s_ffe, s_ffe.n)], incs=[(s_ffe, 1)])
            LG_E = s_ffe.n
            lp_marks = []
            for mb in range(2):
                for ch in range(4):
                    def mk_lp(mb=mb, ch=ch):
                        def f(e):
                            ins = None
                            for k in range(4):
                                ins = e.matmul(psz[mb][:], msel[:, k, mb * 128:(mb + 1) * 128],
                                               LG[:, k, ch * 512:(ch + 1) * 512],
                                               start=(k == 0), stop=(k == 3))
                            return ins
                        return f
                    plan.emit("tensor", mk_lp(),
                              waits=[(s_ffe, LG_E), (s_pcld, PCLD2)] + ([(s_ffe, lp_marks[-1])] if lp_marks else []),
                              incs=[(s_ffz, 1)])
                    plan.emit("scalar", (lambda mb=mb, ch=ch: lambda e: e.activation(
                        lpT[:, mb, ch * 512:(ch + 1) * 512], psz[mb][:], AF.Exp))(),
                        waits=[(s_ffz, s_ffz.n)], incs=[(s_ffe, 1)], post_drain=True)
                    lp_marks.append(s_ffe.n)
            LP_E = s_ffe.n
            rt_marks = []
            rtd_marks = []
            for ch in range(4):
                for db in range(DB):
                    def mk_rT(ch=ch, db=db):
                        def f(e):
                            ins = None
                            for m in range(2):
                                ins = e.matmul(psq[:], leafv[:, m, db * 128:(db + 1) * 128],
                                               lpT[:, m, ch * 512:(ch + 1) * 512],
                                               start=(m == 0), stop=(m == 1))
                            return ins
                        return f
                    plan.emit("tensor", mk_rT(),
                              waits=[(s_ffe, LP_E), (s_pcld, PCLD2)] + ([(s_ffr, rt_marks[-1])] if rt_marks else []),
                              incs=[(s_ffz, 1)])
                    plan.emit("vector", (lambda db=db: lambda e: e.tensor_copy(rstg[:, db, :], psq[:]))(),
                              waits=[(s_ffz, s_ffz.n)] + ([(s_ffrd, rtd_marks[-1])] if rtd_marks and db == 0 else []),
                              incs=[(s_ffr, 1)])
                    rt_marks.append(s_ffr.n)
                plan.emit("sync", (lambda s2=s, ch=ch: lambda e: e.dma_start(
                    out=retT_hbm[:, :, (s2 * 4 + ch) * 512:(s2 * 4 + ch + 1) * 512], in_=rstg[:]))(),
                    waits=[(s_ffr, s_ffr.n)], incs=[(s_ffrd, 16)])
                rtd_marks.append(s_ffrd.n)
            RETD_E = s_ffrd.n
            ffn.close()

            if PC_STAGE < 4:
                sub1.close()
                pcs.close()
                continue
            qkvs = ExitStack()
            qc = qkvs.enter_context(nc.sbuf_tensor(f"qc{s}", [128, DB, SLAB], BF16))
            kc = qkvs.enter_context(nc.sbuf_tensor(f"kc{s}", [128, DB, SLAB], BF16))
            vc = qkvs.enter_context(nc.sbuf_tensor(f"vc{s}", [128, DB, SLAB], BF16))
            prj = ExitStack()
            rawwin = prj.enter_context(nc.sbuf_tensor(f"rawwin{s}", [128, 3, DB, 515], BF16, side="right"))
            wb2 = prj.enter_context(nc.sbuf_tensor(f"wb2{s}", [128, DB, D], BF16, side="right"))
            ksq = prj.enter_context(nc.sbuf_tensor(f"ksq{s}", [128, DB, 128], BF16, side="right"))
            psp = [prj.enter_context(nc.psum_tensor(f"psp{s}{k}", [128, 512], F32)) for k in range(2)]
            psg = prj.enter_context(nc.psum_tensor(f"psg{s}", [128, 512], F32))
            convd = [qc, kc, vc]

            gz_marks = []
            for gi, (wt, slot) in enumerate([(WgT, "zg"), (WbT, "zb")]):
                for ch in range(4):
                    gb = 0 if slot == "zg" else 96
                    def mk_gz(wt=wt, ch=ch, gb=gb):
                        def f(e):
                            ins = None
                            for d in range(DB):
                                ins = e.matmul(psg[gb:gb + 16, :], wt[:, d, :], xn1T[:, d, ch * 512:(ch + 1) * 512],
                                               start=(d == 0), stop=(d == DB - 1),
                                               tile_position=(0, gb) if gb else None)
                            return ins
                        return f
                    plan.emit("tensor", mk_gz(),
                              waits=[(s_l1e, LN1_E), (s_wst, WSTG_E)] + ([(s_gz, gz_marks[-1])] if gz_marks else []),
                              incs=[(s_gz, 1)])
                    plan.emit("vector", (lambda slot=slot, ch=ch, gb=gb: lambda e: e.tensor_copy(
                        sl(slot, ch * 512, (ch + 1) * 512), psg[gb:gb + 16, :]))(),
                        waits=[(s_gz, s_gz.n)], incs=[(s_gz, 1)])
                    gz_marks.append(s_gz.n)
            GZ_E = s_gz.n

            pj_marks = []
            cva_marks = {}
            for ch in range(4):
                for ti in range(3):
                    plan.emit("sync", (lambda ti=ti: lambda e: e.dma_start(out=wb2[:], in_=wT_hbm[ti]))(),
                              waits=[(s_wstd, 16 * 6), (s_pj, s_pj.n)], incs=[(s_wld2, 16)])
                    WL2 = s_wld2.n
                    for cb in range(DB):
                        def mk_pj(ti=ti, ch=ch, cb=cb):
                            def f(e):
                                ins = None
                                for d in range(DB):
                                    ins = e.matmul(psp[cb % 2][:], wb2[:, d, cb * 128:(cb + 1) * 128],
                                                   xn1T[:, d, ch * 512:(ch + 1) * 512],
                                                   start=(d == 0), stop=(d == DB - 1))
                                return ins
                            return f
                        plan.emit("tensor", mk_pj(),
                                  waits=[(s_wld2, WL2), (s_l1e, LN1_E)] + ([(s_pje, pj_marks[-2])] if len(pj_marks) >= 2 else []),
                                  incs=[(s_pj, 1)])
                        w_extra = []
                        if ch >= 1 and cb == 0:
                            w_extra = [(s_cv, cva_marks[(ch - 1, ti)])]
                        plan.emit("vector", (lambda ti=ti, cb=cb: lambda e: e.tensor_copy(
                            rawwin[:, ti, cb, 3:515], psp[cb % 2][:]))(),
                            waits=[(s_pj, s_pj.n)] + w_extra, incs=[(s_pje, 1)])
                        pj_marks.append(s_pje.n)
                    plan.emit("vector", (lambda ti=ti: lambda e: e.tensor_copy(rawwin[:, ti, :, 0:3], halo[:, ti, :, :]))(),
                              waits=[(s_pje, s_pje.n), (s_pcld, PCLD)], incs=[(s_cv, 1)])
                    plan.emit("vector", (lambda ti=ti: lambda e: e.tensor_copy(halo[:, ti, :, :], rawwin[:, ti, :, 512:515]))(),
                              incs=[(s_cv, 1)])
                    for cb in range(DB):
                        plan.emit("scalar", (lambda ti=ti, cb=cb, ch=ch: lambda e: e.activation(
                            convd[ti][:, cb, ch * 512:(ch + 1) * 512], rawwin[:, ti, cb, 3:515],
                            AF.Identity, scale=ctap[:, ti, cb, 3:4], bias=cbia[:, ti, cb:cb + 1]))(),
                            waits=[(s_cv, s_cv.n), (s_pcld, PCLD)], incs=[(s_cv, 1)])
                        for tap in range(3):
                            plan.emit("vector", (lambda ti=ti, cb=cb, ch=ch, tap=tap: lambda e: e.scalar_tensor_tensor(
                                convd[ti][:, cb, ch * 512:(ch + 1) * 512], rawwin[:, ti, cb, tap:512 + tap],
                                ctap[:, ti, cb, tap:tap + 1], convd[ti][:, cb, ch * 512:(ch + 1) * 512],
                                ALU.mult, ALU.add))(),
                                waits=[(s_cv, s_cv.n)], incs=[(s_cv, 1)])
                    cva_marks[(ch, ti)] = s_cv.n
                for cc4 in range(4):
                    n = ch * 4 + cc4
                    plan.emit("scalar", (lambda ch=ch, cc4=cc4: lambda e: e.activation(
                        ksq[:],
                        kc[:, :, ch * 512 + cc4 * 128: ch * 512 + (cc4 + 1) * 128],
                        AF.Square))(),
                        waits=[(s_cv, cva_marks[(ch, 1)]), (s_n2, s_n2.n)], incs=[(s_n2, 1)], post_drain=True)
                    def mk_n2(n=n):
                        def f(e):
                            ins = None
                            for pt in range(DB):
                                ins = e.matmul(psg[64:80, 0:128], onesB[:, pt, :], ksq[:, pt, :],
                                               start=(pt == 0), stop=(pt == DB - 1),
                                               tile_position=(0, 64))
                            return ins
                        return f
                    plan.emit("tensor", mk_n2(),
                              waits=[(s_n2, s_n2.n), (s_gz, GZ_E), (s_pcld, PCLD)],
                              incs=[(s_n2, 1)])
                    plan.emit("vector", (lambda n=n: lambda e: e.tensor_copy(
                        sl("lnrho", n * 128, (n + 1) * 128), psg[64:80, 0:128]))(),
                        waits=[(s_n2, s_n2.n)], incs=[(s_n2, 1)])
            PRJ_CV = s_cv.n
            N2_E = s_n2.n

            def sp_compose(zslot, oslot, w0):
                plan.emit("scalar", (lambda zslot=zslot: lambda e: e.activation(sl("u"), sl(zslot), AF.Abs))(), waits=w0)
                plan.emit("scalar", (lambda sc=ccol16(-1.0, SL["l"][1]): lambda e: e.activation(sl("l"), sl("u"), AF.Exp, scale=sc))())
                plan.emit("scalar", (lambda bi=ccol16(1.0, SL["l"][1]): lambda e: e.activation(sl("l"), sl("l"), AF.Ln, bias=bi))())
                plan.emit("scalar", (lambda zslot=zslot, sc=ccol16(-1.0, SL["rn"][1]): lambda e: e.activation(sl("rn"), sl(zslot), AF.Relu, scale=sc))(),
                          incs=[(s_stt, 1)])
                plan.emit("vector", (lambda oslot=oslot: lambda e: e.scalar_tensor_tensor(
                    sl(oslot), sl("rn"), -1.0, sl("l"), ALU.mult, ALU.subtract))(),
                    waits=[(s_stt, s_stt.n)], incs=[(s_stt, 1)])
            sp_compose("zg", "g", [(s_gz, GZ_E)])
            sp_compose("zb", "lnb", [(s_stt, s_stt.n)])
            plan.emit("vector", (lambda: lambda e: e.tensor_scalar(sl("lnrho"), sl("lnrho"), 1e-24, None, ALU.max))(),
                      waits=[(s_n2, N2_E), (s_stt, s_stt.n)], incs=[(s_stt, 1)], post_drain=True)
            plan.emit("scalar", (lambda: lambda e: e.activation(sl("lnrho"), sl("lnrho"), AF.Ln))(),
                      waits=[(s_stt, s_stt.n)], incs=[(s_stt, 1)], post_drain=True)
            plan.emit("vector", (lambda: lambda e: e.tensor_scalar(sl("lnrho"), sl("lnrho"), -0.5, None, ALU.mult))(),
                      waits=[(s_stt, s_stt.n)], incs=[(s_stt, 1)], post_drain=True)
            plan.emit("vector", (lambda: lambda e: e.tensor_tensor(sl("e1"), sl("lnb"), sl("lnrho"), ALU.add))(),
                      waits=[(s_stt, s_stt.n)], incs=[(s_stt, 1)], post_drain=True)
            plan.emit("vector", (lambda: lambda e: e.memset(sl("ones"), 1.0))(), waits=[(s_stt, s_stt.n)], incs=[(s_stt, 1)])
            for n in range(CPS):
                c0, c1 = n * 128, (n + 1) * 128
                plan.emit("vector", (lambda c0=c0, c1=c1: lambda e: e.tensor_tensor_scan(
                    sl("c", c0, c1), sl("ones", c0, c1), sl("g", c0, c1), 0.0, ALU.mult, ALU.add))(),
                    waits=[(s_stt, s_stt.n)], incs=[(s_stt, 1)], post_drain=True)
                plan.emit("vector", (lambda c0=c0, c1=c1: lambda e: e.tensor_tensor_scan(
                    sl("cB", c0, c1), sl("ones", c0, c1), sl("g", c0, c1), 0.0, ALU.mult, ALU.add))(),
                    incs=[(s_stt, 1)], post_drain=True)
                plan.emit("vector", (lambda c0=c0: lambda e: e.memset(sl("cprev", c0, c0 + 1), 0.0))(),
                          incs=[(s_stt, 1)])
                plan.emit("vector", (lambda c0=c0, c1=c1: lambda e: e.tensor_copy(
                    sl("cprev", c0 + 1, c1), sl("c", c0, c1 - 1)))(),
                    incs=[(s_stt, 1)], post_drain=True)
                plan.emit("vector", (lambda c0=c0, c1=c1: lambda e: e.tensor_tensor(
                    sl("e1", c0, c1), sl("e1", c0, c1), sl("cB", c0, c1), ALU.subtract))(),
                    incs=[(s_stt, 1)], post_drain=True)
                plan.emit("vector", (lambda c0=c0, c1=c1: lambda e: e.tensor_scalar(
                    sl("eps", c0, c1), sl("e1", c0, c1), sl("cB", c1 - 1, c1), None, ALU.add))(),
                    incs=[(s_stt, 1)], post_drain=True)
            plan.emit("vector", (lambda: lambda e: e.tensor_tensor(sl("edr"), sl("cprev"), sl("lnrho"), ALU.add))(),
                      waits=[(s_stt, s_stt.n)], incs=[(s_stt, 1)], post_drain=True)
            plan.emit("sync", (lambda: lambda e: e.dma_start(out=rowscrF[0], in_=sl("edr")))(),
                      waits=[(s_stt, s_stt.n)], incs=[(s_row, 16)])
            plan.emit("sync", (lambda: lambda e: e.dma_start(out=rowscrF[1], in_=sl("e1")))(),
                      incs=[(s_row, 16)])
            plan.emit("sync", (lambda: lambda e: e.dma_start(out=rowscrF[2], in_=sl("c")))(),
                      incs=[(s_row, 16)])
            STT_E = s_stt.n
            ROW_E = s_row.n
            prj.close()
            sub1.close()

            if PC_STAGE < 5:
                prj.close() if False else None
                qkvs.close()
                pcs.close()
                continue
            dl = ExitStack()
            knat = dl.enter_context(nc.sbuf_tensor(f"knat{s}", [128, DB, 128], BF16))
            rhsb = dl.enter_context(nc.sbuf_tensor(f"rhsb{s}", [128, NH, 128], BF16))
            kepb = dl.enter_context(nc.sbuf_tensor(f"kepb{s}", [128, NH, 64], BF16))
            rowb = dl.enter_context(nc.sbuf_tensor(f"rowb{s}", [1, 3, SLAB], F32))
            rowbE = dl.enter_context(nc.sbuf_tensor(f"rowbE{s}", [1, SLAB], BF16))
            Earg = dl.enter_context(nc.sbuf_tensor(f"Earg{s}", [128, 2, 128], F32))
            statT = dl.enter_context(nc.sbuf_tensor(f"statT{s}", [128, 64], F32))
            Edm = dl.enter_context(nc.sbuf_tensor(f"Edm{s}", [128, 2, 128], F32))
            DREP = dl.enter_context(nc.sbuf_tensor(f"DREP{s}", [128, 128], F32))
            Abuf = dl.enter_context(nc.sbuf_tensor(f"Abuf{s}", [128, 128], BF16))
            Mbuf = dl.enter_context(nc.sbuf_tensor(f"Mbuf{s}", [128, 128], BF16))
            Tbuf = dl.enter_context(nc.sbuf_tensor(f"Tbuf{s}", [128, 2, 128], BF16))
            Dbuf = dl.enter_context(nc.sbuf_tensor(f"Dbuf{s}", [128, 128], BF16))
            DTbuf = dl.enter_context(nc.sbuf_tensor(f"DTbuf{s}", [128, 128], BF16))
            negAT = dl.enter_context(nc.sbuf_tensor(f"negAT{s}", [128, 128], BF16))
            Bsb = dl.enter_context(nc.sbuf_tensor(f"Bsb{s}", [128, 128], BF16))
            Pbuf = dl.enter_context(nc.sbuf_tensor(f"Pbuf{s}", [128, 2, 256], BF16))
            MTb = dl.enter_context(nc.sbuf_tensor(f"MTb{s}", [128, 128], BF16))
            TRH = dl.enter_context(nc.sbuf_tensor(f"TRH{s}", [128, 128], BF16))
            nMWT = dl.enter_context(nc.sbuf_tensor(f"nMWT{s}", [128, 128], BF16))
            PPT = dl.enter_context(nc.sbuf_tensor(f"PPT{s}", [128, 64], BF16))
            SQb = dl.enter_context(nc.sbuf_tensor(f"SQb{s}", [128, 64], BF16))
            otmp = dl.enter_context(nc.sbuf_tensor(f"otmp{s}", [128, 128], F32))
            ostg = dl.enter_context(nc.sbuf_tensor(f"ostg{s}", [128, DB, 128], BF16))
            psA = dl.enter_context(nc.psum_tensor(f"psA{s}", [128, 512], F32))
            psB = dl.enter_context(nc.psum_tensor(f"psB{s}", [128, 1024], BF16))
            psC = dl.enter_context(nc.psum_tensor(f"psC{s}", [128, 512], F32))
            psD = dl.enter_context(nc.psum_tensor(f"psD{s}", [128, 512], F32))
            psE = dl.enter_context(nc.psum_tensor(f"psE{s}", [128, 512], F32))

            if debug and s == 0:
                dbg_outs.setdefault("qc_d", nc.dram_tensor("qc_d", [128, DB, SLAB], BF16, kind="ExternalOutput").ap())
                dbg_outs.setdefault("kc_d", nc.dram_tensor("kc_d", [128, DB, SLAB], BF16, kind="ExternalOutput").ap())
                dbg_outs.setdefault("vc_d", nc.dram_tensor("vc_d", [128, DB, SLAB], BF16, kind="ExternalOutput").ap())
                plan.emit("gpsimd", lambda e: e.dma_start(out=dbg_outs["qc_d"], in_=qc[:]), waits=[(s_cv, PRJ_CV)], incs=[(s_dbgx, 16)])
                plan.emit("gpsimd", lambda e: e.dma_start(out=dbg_outs["kc_d"], in_=kc[:]), waits=[(s_cv, PRJ_CV)], incs=[(s_dbgx, 16)])
                plan.emit("gpsimd", lambda e: e.dma_start(out=dbg_outs["vc_d"], in_=vc[:]), waits=[(s_cv, PRJ_CV)], incs=[(s_dbgx, 16)])
            if debug and s == 0:
                dbgA_sb = dl.enter_context(nc.sbuf_tensor("dbgA_sb", [128, NH, 128], BF16))
                dbgT_sb = dl.enter_context(nc.sbuf_tensor("dbgT_sb", [128, NH, 128], BF16))
                dbg_outs.setdefault("A_d", nc.dram_tensor("A_d", [128, NH, 128], BF16, kind="ExternalOutput").ap())
                dbg_outs.setdefault("T_d", nc.dram_tensor("T_d", [128, NH, 128], BF16, kind="ExternalOutput").ap())
            ot_mark = 0
            for n in range(CPS):
                gn = s * CPS + n
                c0 = n * 128
                plan.emit("scalar", (lambda n=n: lambda e: e.dma_start(
                    out=rowb[0:1, :, :].rearrange("o q (h t) -> o q h t", h=16),
                    in_=rowscrF[None, :, :, n * 128:(n + 1) * 128]))(),
                    waits=[(s_row, ROW_E), (s_dle, s_dle.n)], incs=[(s_row, 16)])
                plan.emit("scalar", (lambda: lambda e: e.activation(rowbE[0:1, :], rowb[0:1, 2, :], AF.Exp))(),
                          waits=[(s_row, s_row.n)], incs=[(s_row, 1)], post_drain=True)
                ROWN = s_row.n
                if debug and s == 0 and n == DBGN:
                    dbg_outs.setdefault("rowb_d", nc.dram_tensor("rowb_d", [1, 3, SLAB], F32, kind="ExternalOutput").ap())
                    dbg_outs.setdefault("rowbE_d", nc.dram_tensor("rowbE_d", [1, SLAB], BF16, kind="ExternalOutput").ap())
                    plan.emit("gpsimd", lambda e: e.dma_start(out=dbg_outs["rowb_d"], in_=rowb[:]),
                              waits=[(s_row, ROWN)], incs=[(s_dbgx, 16)])
                    plan.emit("gpsimd", lambda e: e.dma_start(out=dbg_outs["rowbE_d"], in_=rowbE[:]),
                              incs=[(s_dbgx, 16)])
                def mk_st2(c0=c0):
                    def f(e):
                        ins = None
                        for qi, qn in enumerate(["cprev", "lnrho", "c", "eps"]):
                            b = SL[qn][1]
                            ins = e.transpose(psA[:, qi * 16:(qi + 1) * 16], sl(qn, c0, c0 + 128),
                                              if32[b:b + 16, b:b + 16])
                        return ins
                    return f
                plan.emit("tensor", mk_st2(),
                          waits=[(s_stt, STT_E), (s_pcld, PCLD), (s_dle, s_dle.n)], incs=[(s_dl, 1)])
                plan.emit("vector", (lambda: lambda e: e.tensor_copy(statT[:, 0:64], psA[:, 0:64]))(),
                          waits=[(s_dl, s_dl.n)], incs=[(s_dl, 1)], post_drain=True)
                plan.emit("vector", (lambda: lambda e: e.tensor_tensor(statT[:, 0:16], statT[:, 0:16], statT[:, 16:32], ALU.add))(),
                          incs=[(s_dl, 1)], post_drain=True)
                plan.emit("scalar", (lambda: lambda e: e.activation(statT[:, 0:64], statT[:, 0:64], AF.Exp))(),
                          waits=[(s_dl, s_dl.n)], incs=[(s_dl, 1)], post_drain=True)
                plan.emit("vector", (lambda: lambda e: e.tensor_scalar(statT[:, 16:32], statT[:, 32:48], 0.125, None, ALU.mult))(),
                          waits=[(s_dl, s_dl.n)], incs=[(s_dl, 1)], post_drain=True)
                STCH = s_dl.n
                def mk_kn1(c0=c0):
                    def f(e):
                        ins = None
                        for pt in range(DB):
                            ins = e.transpose(psB[:, pt * 128:(pt + 1) * 128], kc[:, pt, c0:c0 + 128], id_sb[:])
                        return ins
                    return f
                plan.emit("tensor", mk_kn1(), waits=[(s_cv, PRJ_CV), (s_dle, s_dle.n)], incs=[(s_dl, 1)])
                plan.emit("vector", (lambda: lambda e: e.tensor_copy(knat[:], psB[:].rearrange("p (a b) -> p a b", a=DB)))(),
                          waits=[(s_dl, s_dl.n)], incs=[(s_dle, 1)])
                def mk_kn2(c0=c0):
                    def f(e):
                        ins = None
                        for pt in range(DB):
                            ins = e.transpose(psB[:, pt * 128:(pt + 1) * 128], vc[:, pt, c0:c0 + 128], id_sb[:])
                    
                        return ins
                    return f
                plan.emit("tensor", mk_kn2(), waits=[(s_dle, s_dle.n)], incs=[(s_dl, 1)])
                plan.emit("vector", (lambda: lambda e: e.tensor_copy(
                    rhsb[:, :, 64:128], psB[:].rearrange("p (h x) -> p h x", h=NH)))(),
                    waits=[(s_dl, s_dl.n)], incs=[(s_dle, 1)])
                for h in range(NH):
                    pt, b64 = h // 2, (h % 2) * 64
                    plan.emit("vector", (lambda h=h, pt=pt, b64=b64: lambda e: e.tensor_scalar(
                        rhsb[:, h, 0:64], knat[:, pt, b64:b64 + 64], statT[:, h:h + 1], None, ALU.mult))(),
                        waits=[(s_dle, s_dle.n), (s_dl, STCH)], incs=[(s_dle, 1)])
                    plan.emit("vector", (lambda h=h, pt=pt, b64=b64: lambda e: e.tensor_scalar(
                        kepb[:, h, :], knat[:, pt, b64:b64 + 64], statT[:, 48 + h:49 + h], None, ALU.mult))(),
                        incs=[(s_dle, 1)])
                RH_E = s_dle.n

                for h in range(NH):
                    pt, q64 = h // 2, (h % 2) * 64
                    tpK = (q64, q64)
                    tpM = (0, q64)
                    kTc = kc[q64:q64 + 64, pt, c0:c0 + 128]
                    qTc = qc[q64:q64 + 64, pt, c0:c0 + 128]
                    def mk_s1(h=h, kTc=kTc, qTc=qTc, q64=q64):
                        def f(e):
                            e.matmul(psA[:, 0:128], kTc, kTc, start=True, stop=True)
                            e.matmul(psA[:, 128:256], qTc, kTc, start=True, stop=True)
                            e.matmul(psA[:, 256:384], ones1_f[:], rowb[0:1, 1, h * 128:(h + 1) * 128], start=True, stop=False)
                            e.matmul(psA[:, 256:384], rowb[0:1, 0, h * 128:(h + 1) * 128], ones1_f[:], start=False, stop=True)
                            e.matmul(psA[:, 384:512], ones1_f[:], rowb[0:1, 1, h * 128:(h + 1) * 128], start=True, stop=False)
                            e.matmul(psA[:, 384:512], rowb[0:1, 2, h * 128:(h + 1) * 128], ones1_f[:], start=False, stop=True)
                            return e.matmul(psE[q64:q64 + 64, 384:512], ones1b[0:1, 0:64], rowbE[0:1, h * 128:(h + 1) * 128],
                                            start=True, stop=True, tile_position=(0, q64))
                        return f
                    plan.emit("tensor", mk_s1(),
                              waits=[(s_dle, RH_E), (s_row, ROWN), (s_dle, s_dle.n)], incs=[(s_dl, 1)])
                    S1 = s_dl.n
                    plan.emit("vector", (lambda: lambda e: e.tensor_tensor(Edm[:, 0, :], psA[:, 0:128], maskS[:], ALU.mult))(),
                              waits=[(s_dl, S1)], incs=[(s_dle, 1)])
                    plan.emit("vector", (lambda: lambda e: e.tensor_tensor(Edm[:, 1, :], psA[:, 128:256], maskI[:], ALU.mult))(),
                              incs=[(s_dle, 1)])
                    plan.emit("vector", (lambda q64=q64: lambda e: e.tensor_copy(DREP[q64:q64 + 64, :], psE[q64:q64 + 64, 384:512]))(),
                              incs=[(s_dle, 1)])
                    plan.emit("vector", (lambda: lambda e: e.tensor_scalar(Earg[:, 0, :], psA[:, 256:384], 60.0, None, ALU.min))(),
                              incs=[(s_dle, 1)])
                    plan.emit("vector", (lambda: lambda e: e.tensor_scalar(Earg[:, 1, :], psA[:, 384:512], 60.0, None, ALU.min))(),
                              incs=[(s_dle, 1)], post_drain=True)
                    plan.emit("scalar", (lambda: lambda e: e.activation(Earg[:, 0, :], Earg[:, 0, :], AF.Exp))(),
                              waits=[(s_dle, s_dle.n)], incs=[(s_dle, 1)], post_drain=True)
                    plan.emit("scalar", (lambda bi=ccol(float(np.log(0.125))): lambda e: e.activation(
                        Earg[:, 1, :], Earg[:, 1, :], AF.Exp, bias=bi))(),
                        incs=[(s_dle, 1)], post_drain=True)
                    plan.emit("vector", (lambda: lambda e: e.tensor_tensor(Abuf[:], Edm[:, 0, :], Earg[:, 0, :], ALU.mult))(),
                              waits=[(s_dle, s_dle.n)], incs=[(s_dle, 1)])
                    plan.emit("vector", (lambda: lambda e: e.tensor_tensor(Mbuf[:], Edm[:, 1, :], Earg[:, 1, :], ALU.mult))(),
                              incs=[(s_dle, 1)])
                    EV1 = s_dle.n
                    def mk_tr(h=h):
                        def f(e):
                            e.transpose(psB[:, 0:128], Abuf[:], id_sb[:])
                            return e.transpose(psB[:, 128:256], Mbuf[:], id_sb[:])
                        return f
                    if debug and s == 0 and n == DBGN:
                        plan.emit("vector", (lambda h=h: lambda e: e.tensor_copy(dbgA_sb[:, h, :], Abuf[:]))(),
                                  waits=[(s_dle, EV1)], incs=[(s_dle, 1)])
                        EV1 = s_dle.n
                    plan.emit("tensor", mk_tr(), waits=[(s_dle, EV1)], incs=[(s_dl, 1)])
                    plan.emit("vector", (lambda: lambda e: e.tensor_copy(MTb[:], psB[:, 128:256]))(),
                              waits=[(s_dl, s_dl.n)], incs=[(s_dle, 1)])
                    plan.emit("vector", (lambda: lambda e: e.tensor_scalar(negAT[:], psB[:, 0:128], -1.0, None, ALU.mult))(),
                              incs=[(s_dle, 1)])
                    plan.emit("vector", (lambda: lambda e: e.tensor_tensor(DTbuf[:], psB[:, 0:128], maskBDU[:], ALU.mult))(),
                              incs=[(s_dle, 1)])
                    plan.emit("vector", (lambda: lambda e: e.tensor_tensor(Dbuf[:], Abuf[:], maskBD[:], ALU.mult))(),
                              incs=[(s_dle, 1)])
                    for lv in range(4):
                        cur, nxt = lv % 2, (lv + 1) % 2
                        def mk_pw(cur=cur, lv=lv):
                            def f(e):
                                pa = Dbuf[:] if lv == 0 else Pbuf[:, cur, 0:128]
                                na = DTbuf[:] if lv == 0 else Pbuf[:, cur, 128:256]
                                e.matmul(psC[:, 0:128], na, pa, start=True, stop=True)
                                return e.matmul(psC[:, 128:256], pa, na, start=True, stop=True)
                            return f
                        plan.emit("tensor", mk_pw(), waits=[(s_dle, s_dle.n)], incs=[(s_dl, 1)])
                        plan.emit("vector", (lambda nxt=nxt: lambda e: e.tensor_copy(Pbuf[:, nxt, :], psC[:, 0:256]))(),
                                  waits=[(s_dl, s_dl.n)], incs=[(s_dle, 1)])
                        if lv == 0:
                            plan.emit("vector", (lambda: lambda e: e.scalar_tensor_tensor(
                                Tbuf[:, 0, :], DTbuf[:], -1.0, id_sb[:], ALU.mult, ALU.add))(),
                                incs=[(s_dle, 1)])
                        def mk_ch(cur=cur, nxt=nxt):
                            def f(e):
                                e.matmul(psD[:, 0:128], id_sb[:], Tbuf[:, cur, :], start=True, stop=False)
                                return e.matmul(psD[:, 0:128], Pbuf[:, nxt, 0:128], Tbuf[:, cur, :], start=False, stop=True)
                            return f
                        plan.emit("tensor", mk_ch(), waits=[(s_dle, s_dle.n)], incs=[(s_dl, 1)])
                        plan.emit("vector", (lambda nxt=nxt: lambda e: e.tensor_copy(Tbuf[:, nxt, :], psD[:, 0:128]))(),
                                  waits=[(s_dl, s_dl.n)], incs=[(s_dle, 1)])
                    fin = 4 % 2
                    def mk_binit(h=h):
                        def f(e):
                            return e.matmul(psE[:, 0:128], id_sb[:], rhsb[:, h, :], start=True, stop=False)
                        return f
                    plan.emit("tensor", mk_binit(), waits=[(s_dle, s_dle.n)], incs=[(s_dl, 1)])
                    for p in range(4):
                        r0 = p * 32
                        if p == 0:
                            plan.emit("vector", (lambda h=h: lambda e: e.tensor_copy(Bsb[0:32, :], rhsb[0:32, h, :]))(),
                                      waits=[(s_dle, s_dle.n), (s_dl, s_dl.n)], incs=[(s_dle, 1)])
                        else:
                            plan.emit("vector", (lambda r0=r0: lambda e: e.tensor_copy(Bsb[r0:r0 + 32, :], psE[r0:r0 + 32, 0:128]))(),
                                      waits=[(s_dl, s_dl.n)], incs=[(s_dle, 1)])
                        def mk_u(p=p, r0=r0, fin=fin):
                            def f(e):
                                return e.matmul(psD[r0:r0 + 32, 128:256], Tbuf[r0:r0 + 32, fin, r0:r0 + 32],
                                                Bsb[r0:r0 + 32, :], start=True, stop=True,
                                                tile_position=(r0, r0))
                            return f
                        plan.emit("tensor", mk_u(), waits=[(s_dle, s_dle.n)], incs=[(s_dl, 1)])
                        plan.emit("vector", (lambda r0=r0: lambda e: e.tensor_copy(TRH[r0:r0 + 32, :], psD[r0:r0 + 32, 128:256]))(),
                                  waits=[(s_dl, s_dl.n)], incs=[(s_dle, 1)])
                        if p < 3:
                            def mk_corr(p=p, r0=r0):
                                def f(e):
                                    r = None
                                    for pd in range(p + 1, 4):
                                        c0 = pd * 32
                                        r = e.matmul(psE[c0:c0 + 32, 0:128], negAT[r0:r0 + 32, c0:c0 + 32],
                                                     TRH[r0:r0 + 32, :], start=False, stop=(pd == p + 1),
                                                     tile_position=(r0, c0))
                                    return r
                                return f
                            plan.emit("tensor", mk_corr(), waits=[(s_dle, s_dle.n)], incs=[(s_dl, 1)])
                    if debug and s == 0 and n == DBGN:
                        plan.emit("vector", (lambda h=h: lambda e: e.tensor_copy(dbgT_sb[:, h, :], TRH[:]))(),
                                  incs=[(s_dle, 1)])
                    def mk_sp(h=h, q64=q64):
                        def f(e):
                            e.matmul(psE[q64:q64 + 64, 128:192], TRH[:, 0:64], kepb[:, h, :], start=True, stop=True, tile_position=(0, q64))
                            e.matmul(psE[q64:q64 + 64, 192:256], kepb[:, h, :], TRH[:, 64:128], start=True, stop=True, tile_position=(0, q64))
                            return e.matmul(psE[q64:q64 + 64, 256:384], TRH[:, 0:64], MTb[:], start=True, stop=True, tile_position=(0, q64))
                        return f
                    plan.emit("tensor", mk_sp(), waits=[(s_dle, s_dle.n)], incs=[(s_dl, 1)])
                    SP1 = s_dl.n
                    plan.emit("vector", (lambda q64=q64: lambda e: e.tensor_scalar(
                        nMWT[q64:q64 + 64, :], psE[q64:q64 + 64, 256:384], -1.0, None, ALU.mult))(),
                        waits=[(s_dl, SP1)], incs=[(s_dle, 1)])
                    plan.emit("vector", (lambda q64=q64: lambda e: e.scalar_tensor_tensor(
                        PPT[q64:q64 + 64, :], if32[q64:q64 + 64, q64:q64 + 64], DREP[q64:q64 + 64, 127:128],
                        psE[q64:q64 + 64, 128:192], ALU.mult, ALU.subtract))(),
                        incs=[(s_dle, 1)])
                    plan.emit("vector", (lambda q64=q64: lambda e: e.tensor_copy(SQb[q64:q64 + 64, :], psE[q64:q64 + 64, 192:256]))(),
                              incs=[(s_dle, 1)])
                    EV2 = s_dle.n
                    def mk_o(h=h, pt=pt, q64=q64, qTc=qTc):
                        def f(e):
                            e.matmul(psD[q64:q64 + 64, 256:384], S_sb[q64:q64 + 64, pt, :], qTc, start=True, stop=True, tile_position=(q64, q64))
                            e.matmul(psD[q64:q64 + 64, 384:512], TRH[:, 64:128], MTb[:], start=True, stop=False, tile_position=(0, q64))
                            return e.matmul(psD[q64:q64 + 64, 384:512], S_sb[q64:q64 + 64, pt, :], nMWT[q64:q64 + 64, :],
                                            start=False, stop=True, tile_position=(q64, q64))
                        return f
                    plan.emit("tensor", mk_o(), waits=[(s_dle, EV2)], incs=[(s_dl, 1)])
                    O1 = s_dl.n
                    def mk_su(h=h, pt=pt, q64=q64):
                        def f(e):
                            e.matmul(psC[q64:q64 + 64, 256:320], PPT[q64:q64 + 64, :], S_sb[q64:q64 + 64, pt, :],
                                     start=True, stop=False, tile_position=(q64, q64))
                            return e.matmul(psC[q64:q64 + 64, 256:320], id_sb[q64:q64 + 64, q64:q64 + 64], SQb[q64:q64 + 64, :],
                                            start=False, stop=True, tile_position=(q64, q64))
                        return f
                    plan.emit("tensor", mk_su(), waits=[(s_dl, O1)], incs=[(s_dl, 1)])
                    SU1 = s_dl.n
                    plan.emit("vector", (lambda q64=q64: lambda e: e.tensor_tensor(
                        otmp[q64:q64 + 64, :], psD[q64:q64 + 64, 256:384], DREP[q64:q64 + 64, :], ALU.mult))(),
                        waits=[(s_dl, SU1)] + ([(s_ot, ot_mark)] if h == 0 else []), incs=[(s_dle, 1)])
                    plan.emit("vector", (lambda pt=pt, q64=q64: lambda e: e.scalar_tensor_tensor(
                        ostg[q64:q64 + 64, pt, :], otmp[q64:q64 + 64, :], 0.125, psD[q64:q64 + 64, 384:512],
                        ALU.mult, ALU.add))(),
                        incs=[(s_dle, 1)])
                    plan.emit("vector", (lambda pt=pt, q64=q64: lambda e: e.tensor_copy(
                        S_sb[q64:q64 + 64, pt, :], psC[q64:q64 + 64, 256:320]))(),
                        incs=[(s_dle, 1)])
                plan.emit("sync", (lambda gn=gn: lambda e: e.dma_start(
                    out=oT_hbm[:, :, gn * 128:(gn + 1) * 128], in_=ostg[:]))(),
                    waits=[(s_dle, s_dle.n)], incs=[(s_ot, 16)])

                ot_mark = s_ot.n
                if debug and s == 0 and n == DBGN:
                    plan.emit("gpsimd", lambda e: e.dma_start(out=dbg_outs["A_d"], in_=dbgA_sb[:]),
                              waits=[(s_dle, s_dle.n)], incs=[(s_dbgx, 16)])
                    plan.emit("gpsimd", lambda e: e.dma_start(out=dbg_outs["T_d"], in_=dbgT_sb[:]),
                              incs=[(s_dbgx, 16)])

            pin_scr = dscr(f"pin{s}_scr", (1, 16), F32)
            plan.emit("gpsimd", (lambda pin_scr=pin_scr: lambda e: e.dma_start(out=pin_scr, in_=rowb[0:1, 0, 0:16]))(),
                      waits=[(s_dle, s_dle.n)], incs=[(s_ot, 16)])
            dl.close()
            qkvs.close()

            if PC_STAGE < 6:
                pcs.close()
                continue
            cmb = ExitStack()
            WoT = cmb.enter_context(nc.sbuf_tensor(f"WoT{s}", [128, DB, D], BF16))
            WoutT = cmb.enter_context(nc.sbuf_tensor(f"WoutT{s}", [128, DB, D], BF16))
            otile = cmb.enter_context(nc.sbuf_tensor(f"otile{s}", [128, 2, DB, 512], BF16))
            rtile = cmb.enter_context(nc.sbuf_tensor(f"rtile{s}", [128, 2, DB, 512], BF16))
            x2t = cmb.enter_context(nc.sbuf_tensor(f"x2t{s}", [128, 2, D], F32))
            x1c = cmb.enter_context(nc.sbuf_tensor(f"x1c{s}", [128, 2, D], F32))
            psX = [cmb.enter_context(nc.psum_tensor(f"psX{s}{k}", [128, 1024], F32)) for k in range(2)]
            plan.emit("scalar", lambda e: e.dma_start(out=WoT[:], in_=wT_hbm[3]), incs=[(s_cl, 16)])
            plan.emit("scalar", lambda e: e.dma_start(out=WoutT[:], in_=wT_hbm[4]), incs=[(s_cl, 16)])
            cm_marks = []
            for tb in range(NTB):
                gt = t0g + tb
                par = tb % 2
                qpar = (tb // 4) % 2
                if tb % 4 == 0:
                    tile_gate = [(s_cmb, cm_marks[tb - 5])] if tb >= 8 else []
                    plan.emit("scalar", (lambda s2=s, tb=tb, qpar=qpar: lambda e: e.dma_start(
                        out=otile[:, qpar, :, :], in_=oT_hbm[:, :, (s2 * NTB + tb) * 128:(s2 * NTB + tb + 4) * 128]))(),
                        waits=[(s_ot, ot_mark)] + tile_gate, incs=[(s_cl, 16)])
                    plan.emit("scalar", (lambda s2=s, tb=tb, qpar=qpar: lambda e: e.dma_start(
                        out=rtile[:, qpar, :, :], in_=retT_hbm[:, :, (s2 * NTB + tb) * 128:(s2 * NTB + tb + 4) * 128]))(),
                        waits=[(s_ffrd, RETD_E)], incs=[(s_cl, 16)])
                CL_T = s_cl.n
                plan.emit("sync", (lambda gt=gt, par=par: lambda e: e.dma_start(
                    out=x1c[:, par, :], in_=x1_hbm[gt * 128:(gt + 1) * 128, :]))(),
                    waits=([(s_cmb, cm_marks[-2])] if len(cm_marks) >= 2 else []), incs=[(s_xc, 16)])
                XL_T = s_xc.n
                def mk_cmb(tb=tb, par=par, qpar=qpar):
                    def f(e):
                        t4 = (tb % 4) * 128
                        e.matmul(psX[par][:, 0:512], ones1_f[:], fwoutb[:, 0:512], start=True, stop=False)
                        e.matmul(psX[par][:, 512:1024], ones1_f[:], fwoutb[:, 512:1024], start=True, stop=False)
                        ins = None
                        for cb in range(DB):
                            e.matmul(psX[par][:, 0:512], otile[:, qpar, cb, t4:t4 + 128], WoT[:, cb, 0:512], start=False, stop=False)
                            e.matmul(psX[par][:, 512:1024], otile[:, qpar, cb, t4:t4 + 128], WoT[:, cb, 512:1024], start=False, stop=False)
                            e.matmul(psX[par][:, 0:512], rtile[:, qpar, cb, t4:t4 + 128], WoutT[:, cb, 0:512], start=False, stop=(cb == DB - 1))
                            ins = e.matmul(psX[par][:, 512:1024], rtile[:, qpar, cb, t4:t4 + 128], WoutT[:, cb, 512:1024], start=False, stop=(cb == DB - 1))
                        return ins
                    return f
                ev_gate = [(s_cmb, cm_marks[-2])] if len(cm_marks) >= 2 else []
                plan.emit("tensor", mk_cmb(),
                          waits=[(s_cl, CL_T), (s_pcld, PCLD), (s_ms, 2)] + ev_gate,
                          incs=[(s_cmb, 1)])
                plan.emit("vector", (lambda tb=tb, par=par: lambda e: e.tensor_tensor(
                    x2t[:, par, :], psX[par][:], x1c[:, par, :], ALU.add))(),
                    waits=[(s_cmb, s_cmb.n), (s_xc, XL_T)] + ([(s_out, s_out.n - 16)] if s_out.n >= 16 else []),
                    incs=[(s_cmb, 1)])
                plan.emit("sync", (lambda gt=gt, par=par: lambda e: e.dma_start(
                    out=out[gt * 128:(gt + 1) * 128, :], in_=x2t[:, par, :]))(),
                    waits=[(s_cmb, s_cmb.n)], incs=[(s_out, 16)])
                cm_marks.append(s_cmb.n)
            cmb.close()
            pcs.close()
        pcc.close()
        fin_waits = [(s_wstd, s_wstd.n), (s_l1e, s_l1e.n) if PC_STAGE >= 2 else (s_wstd, s_wstd.n)]
        for sm in [s_ffrd, s_row, s_ot, s_out, s_pje, s_cv, s_n2, s_stt, s_dle, s_gz, s_cl, s_xc, s_cmb]:
            fin_waits.append((sm, sm.n))
        plan.emit("sync", lambda e: e.nop(), waits=fin_waits)
        pin_sb = st.enter_context(nc.sbuf_tensor("pin_sb", [1, 128], F32))
        plan.emit("gpsimd", lambda e: e.memset(pin_sb[:], 0.0), waits=list(fin_waits))

    if "pc" not in phases:
        zeros = st.enter_context(nc.sbuf_tensor("zeros_ph", [128, D], F32))
        s_z = plan.sem()
        s_fin = plan.sem()
        plan.emit("vector", lambda e: e.memset(zeros[:], 0.0), incs=[(s_z, 1)])
        for t in range(NT):
            plan.emit("sync", (lambda t=t: lambda e: e.dma_start(out=out[t * 128:(t + 1) * 128, :], in_=zeros[:]))(),
                      waits=[(s_z, 1)], incs=[(s_fin, 16)])
        plan.emit("sync", lambda e: e.nop(),
                  waits=[(s_fin, 16 * NT), (s_xntd, 16 * (NT // 4))]
                        + (pa_done if "pa" in phases else [])
                        + (pb_done if "pb" in phases else []))

    with nc.Block() as block:
        block.tensor(lambda e: plan.replay(e, "tensor"))
        block.vector(lambda e: plan.replay(e, "vector"))
        block.scalar(lambda e: plan.replay(e, "scalar"))
        block.sync(lambda e: plan.replay(e, "sync"))
        block.gpsimd(lambda e: plan.replay(e, "gpsimd"))

    st.close()
    return nc, di, dbg_outs


_IN_KEYMAP = {
    "rW": "L0_ffn_router_W", "rb": "L0_ffn_router_b",
    "sW1": "L0_ffn_sW1", "sb1": "L0_ffn_sb1", "sW2": "L0_ffn_sW2", "sb2": "L0_ffn_sb2",
    "eW1": "L0_ffn_eW1", "eb1": "L0_ffn_eb1", "eW2": "L0_ffn_eW2", "eb2": "L0_ffn_eb2",
    "aWq": "L1_attn_Wq", "aWk": "L1_attn_Wk", "aWv": "L1_attn_Wv", "aWo": "L1_attn_Wo",
    "aWg": "L1_attn_Wg", "aWb": "L1_attn_Wb",
    "cqw": "L1_attn_cq_w", "cqb": "L1_attn_cq_b", "ckw": "L1_attn_ck_w", "ckb": "L1_attn_ck_b",
    "cvw": "L1_attn_cv_w", "cvb": "L1_attn_cv_b",
    "fWq": "L1_ffn_Wq", "frouters": "L1_ffn_routers", "fleafv": "L1_ffn_leaf_values",
    "fWout": "L1_ffn_Wout_w", "fWoutb": "L1_ffn_Wout_b",
}


def kernel(**inputs):
    x_full = np.ascontiguousarray(np.asarray(inputs["L0_attn_x"], dtype=np.float32))
    B = x_full.shape[0]
    nc, di, dbg = build_nc()
    base = {k: np.ascontiguousarray(np.asarray(inputs[v], dtype=np.float32))
            for k, v in _IN_KEYMAP.items()}
    in_maps = []
    for b in range(B):
        m = dict(base)
        m["x"] = np.ascontiguousarray(x_full[b])
        in_maps.append(m)
    res = run_bass_kernel_spmd(nc, in_maps, core_ids=list(range(B)))
    return np.stack([r["out"] for r in res.results], axis=0)
